# version 06
# baseline (speedup 1.0000x reference)
import os
import numpy as np
import ml_dtypes
import concourse.bass as bass
import concourse.mybir as mybir
from concourse.bass_utils import run_bass_kernel_spmd

F32 = mybir.dt.float32
BF16 = mybir.dt.bfloat16
U8 = mybir.dt.uint8
AF = mybir.ActivationFunctionType
ALU = mybir.AluOpType
AX = mybir.AxisListType

T = 2048
D = 1024
NT = 16
NC = 8
H = 16
HD = 64
DFF = 4096
NE = 4

ARENA_KB = 128
R0, R1, R2, R3 = 0, 32 * 1024, 64 * 1024, 96 * 1024
STAGES = 99
DEBUG = ()


class Sem:
    def __init__(self, nc, name):
        self.h = nc.alloc_semaphore(name)
        self.n = 0

    def take(self, k=1):
        self.n += k
        return self.n


class KB:

    def __init__(self, nc):
        self.nc = nc
        self.sems = {}
        self.last_wait = {}
        self.dbg = {}
        self.dram = {}

    def sem(self, name):
        if name not in self.sems:
            self.sems[name] = Sem(self.nc, name)
        return self.sems[name]

    def wait(self, ename, eng, sem, v):
        key = (ename, sem.h.name)
        if self.last_wait.get(key, -1) < v:
            eng.wait_ge(sem.h, v)
            self.last_wait[key] = v

    def block(self, name, ops):
        with self.nc.Block(name) as blk:
            for ename in ("sync", "gpsimd", "tensor", "scalar", "vector"):
                lst = ops.get(ename)
                if lst:
                    getattr(blk, ename)(lambda e, lst=lst: [f(e) for f in lst])


def bf(x):
    return np.ascontiguousarray(x.astype(ml_dtypes.bfloat16))


def build_nc(stages=STAGES, debug=(), scan_chunks=NT, scan_sbs=4):
    SS = int(os.environ.get('SSECT', 99))
    nc = bass.Bass("TRN2", target_bir_lowering=False, detect_race_conditions=False)
    kb = KB(nc)
    kb.debug_names = set(debug)

    def din(name, shape):
        kb.dram[name] = nc.dram_tensor(name, list(shape), F32,
                                       kind="ExternalInput").ap()
        return kb.dram[name]

    x_in = din("x_in", (T, D))
    W_qkv = din("W_qkv", (D, 3 * D)); b_qkv = din("b_qkv", (3 * D,))
    A_qkv = din("A_qkv", (D, 8)); B_qkv = din("B_qkv", (8, 3 * D))
    W_proj = din("W_proj", (D, D)); b_proj = din("b_proj", (D,))
    A_proj = din("A_proj", (D, 8)); B_proj = din("B_proj", (8, D))
    ln1_w = din("ln1_w", (D,)); ln1_b = din("ln1_b", (D,))
    ln2_w = din("ln2_w", (D,)); ln2_b = din("ln2_b", (D,))
    conv_w = din("conv_w", (D, 1, 3))
    attn_w = din("attn_w", (D, 3 * D)); proj_w = din("proj_w", (D, D))
    router_w = din("router_w", (D, NE))
    expert_wv = din("expert_wv", (NE, D, 2 * DFF))
    expert_proj = din("expert_proj", (NE, DFF, D))
    Wq1 = din("Wq1", (D, D)); Wk1 = din("Wk1", (D, D)); Wv1 = din("Wv1", (D, D))
    Wgamma = din("Wgamma", (D, H)); Wf = din("Wf", (D, H))
    Wg1 = din("Wg1", (D, HD)); Wg2 = din("Wg2", (HD, D))
    Wo1 = din("Wo1", (D, D)); norm_w = din("norm_w", (D,))
    Win = din("Win", (D, 64)); bin_ = din("bin_", (64,))
    W1f = din("W1f", (64, 16)); b1f = din("b1f", (64, 16))
    W2f = din("W2f", (64, 16)); b2f = din("b2f", (64,))
    Wout = din("Wout", (64, D)); bout = din("bout", (D,))

    out = nc.dram_tensor("out", [T, D], F32, kind="ExternalOutput").ap()

    ident_f_c = nc.inline_tensor(np.eye(128, dtype=np.float32), name="c_idf")
    ident_b_c = nc.inline_tensor(np.eye(128, dtype=ml_dtypes.bfloat16), name="c_idb")
    bandm = (np.arange(896)[None, :] - np.arange(128)[:, None] >= 384)
    band_c = nc.inline_tensor(bf(bandm.astype(np.float32)), name="c_band")
    j = np.arange(128)[None, :]; p = np.arange(128)[:, None]
    smask = np.concatenate([-(j < p).astype(np.float32),
                            -(j > p).astype(np.float32),
                            (j >= p).astype(np.float32)], axis=1)
    smask_c = nc.inline_tensor(bf(smask), name="c_smask")
    invf = 1.0 / (10000.0 ** (np.arange(0, HD, 2) / HD))
    tt = np.arange(T)
    ang = tt[None, :] * invf[:, None]
    cs = np.cos(ang); sn = np.sin(ang)
    cos128 = np.concatenate([cs, cs, cs, cs], axis=0)
    sin128 = np.concatenate([sn, sn, sn, sn], axis=0)
    cos_c = nc.inline_tensor(bf(cos128), name="c_cos")
    sin_c = nc.inline_tensor(bf(sin128), name="c_sin")
    Pm = np.zeros((128, 128), dtype=np.float32)
    for blk0 in (0, 64):
        for d in range(32):
            Pm[blk0 + d + 32, blk0 + d] = -1.0
            Pm[blk0 + d, blk0 + d + 32] = 1.0
    rotp_c = nc.inline_tensor(bf(Pm), name="c_rotp")
    sel = np.zeros((16, 16, 128), dtype=np.float32)
    for h in range(16):
        sel[h, h, :] = 1.0
    sel_c = nc.inline_tensor(sel.transpose(1, 0, 2).reshape(16, 16 * 128),
                             name="c_sel")
    e127 = np.zeros((128, 128), dtype=np.float32); e127[127, :] = 1.0
    e127_c = nc.inline_tensor(e127, name="c_e127")
    onesf_c = nc.inline_tensor(np.ones((128, 128), dtype=np.float32), name="c_onef")
    onesb_c = nc.inline_tensor(np.ones((128, 128), dtype=ml_dtypes.bfloat16), name="c_oneb")

    acc = nc.alloc_sbuf_tensor("acc", [128, NT, D], F32)
    ident_f = nc.alloc_sbuf_tensor("ident_f", [128, 128], F32)
    ident_b = nc.alloc_sbuf_tensor("ident_b", [128, 128], BF16)
    smaskT = nc.alloc_sbuf_tensor("smaskT", [128, 384], BF16)
    e127s = nc.alloc_sbuf_tensor("e127s", [128, 128], F32)
    onesf = nc.alloc_sbuf_tensor("onesf", [128, 128], F32)
    onesb = nc.alloc_sbuf_tensor("onesb", [128, 128], BF16)
    cst = nc.alloc_sbuf_tensor("cst", [128, 8], F32)
    lnst = nc.alloc_sbuf_tensor("lnst", [128, NT, 8], F32)
    wslab = nc.alloc_sbuf_tensor("wslab", [128, 8, 256], BF16)
    t1s = nc.alloc_sbuf_tensor("t1s", [64, T], BF16)
    logftok = nc.alloc_sbuf_tensor("logftok", [128, NT, H], F32)
    gammatok = nc.alloc_sbuf_tensor("gammatok", [128, NT, H], BF16)
    sffn = nc.alloc_sbuf_tensor("sffn", [128, NT, 64], BF16)
    arena = nc.alloc_sbuf_tensor("arena", [128, ARENA_KB * 1024], U8)
    psA = nc.alloc_psum_tensor("psA", [128, 8, 512], F32)

    kb.acc = acc
    wsl = lambda hh: wslab[:, :, hh * 128:(hh + 1) * 128]

    def carve(off_b, shape, dtype):
        parts = shape[0]
        nbytes = int(np.prod(shape[1:])) * mybir.dt.size(dtype)
        assert off_b % 64 == 0 and off_b + nbytes <= ARENA_KB * 1024, (off_b, shape)
        v = arena[0:parts, off_b:off_b + nbytes].bitcast(dtype)
        if len(shape) > 2:
            pat = "p (" + " ".join(f"d{i}" for i in range(1, len(shape))) + ") -> p " + \
                  " ".join(f"d{i}" for i in range(1, len(shape)))
            v = v.rearrange(pat, **{f"d{i}": s for i, s in enumerate(shape) if i})
        return v

    def pbank(b, rows=128, cols=512, base=0):
        return psA[0:rows, b, base:base + cols]

    def dbg_dump(name, ap, shape, dtype=F32):
        if name not in kb.debug_names:
            return
        t = nc.dram_tensor("dbg_" + name, list(shape), dtype, kind="ExternalOutput").ap()
        s = kb.sem("dbgsem")
        v = s.take(16)
        ops = {"sync": [lambda e, t=t, ap=ap, v=v: (e.dma_start(out=t, in_=ap)
                                                    .then_inc(s.h, 16),
                                                    e.wait_ge(s.h, v))]}
        kb.block("dbg_" + name, ops)

    sW = kb.sem("sW")
    sG = kb.sem("sG")
    sT = kb.sem("sT")
    sS = kb.sem("sS")
    sV = kb.sem("sV")

    def ph0():
        ops = {"sync": [], "gpsimd": []}
        o = ops["sync"]
        sW.take(16)
        o.append(lambda e: e.dma_start(
            out=acc[:], in_=x_in.rearrange("(t p) d -> p t d", p=128))
            .then_inc(sW.h, 16))
        for src, dst in ((ident_f_c, ident_f), (ident_b_c, ident_b),
                         (smask_c, smaskT), (e127_c, e127s),
                         (onesf_c, onesf), (onesb_c, onesb)):
            sW.take(16)
            o.append(lambda e, s=src, d=dst: e.dma_start(
                out=d[:], in_=s.ap()).then_inc(sW.h, 16))
        g = ops["gpsimd"]
        g.append(lambda e: e.memset(cst[:, 0:1], 1e-5))
        g.append(lambda e: e.memset(cst[:, 1:2], 1e-12))
        g.append(lambda e: e.memset(cst[:, 2:3], 0.0))
        g.append(lambda e: e.memset(cst[:, 3:4], 1.0))
        wv = sW.n
        o.append(lambda e, v=wv: kb.wait("sync", e, sW, v))
        kb.block("ph0", ops)

    def ln_pass(name, src_tile, dst_tile, gb=None, n_tiles=NT):
        st = lnst
        ops = {"vector": [], "scalar": []}
        v_, s_ = ops["vector"], ops["scalar"]
        pdump = psA[:, 6:8, :].rearrange("p a b -> p (a b)")
        C = lambda t, i: st[:, t, i:i + 1]
        vpre = sV.n
        s_.append(lambda e, v=vpre: kb.wait("scalar", e, sS if False else sV, v))
        for t in range(n_tiles):
            if t < n_tiles - 1:
                s_.append(lambda e, t=t: e.activation(
                    out=pdump, in_=src_tile(t), func=AF.Square, accum_out=C(t, 1)))
        v1 = sS.take()
        s_.append(lambda e, t=n_tiles - 1: e.activation(
            out=pdump, in_=src_tile(t), func=AF.Square,
            accum_out=C(t, 1)).then_inc(sS.h, 1))
        for t in range(n_tiles):
            v_.append(lambda e, t=t: e.reduce_sum(
                out=C(t, 0), in_=src_tile(t), axis=AX.X))
        v_.append(lambda e, v=v1: kb.wait("vector", e, sS, v))
        for t in range(n_tiles):
            v_.append(lambda e, t=t: e.tensor_scalar(
                out=C(t, 3), in0=C(t, 1), scalar1=1.0 / D, scalar2=None,
                op0=ALU.mult))
        for t in range(n_tiles):
            v_.append(lambda e, t=t: e.tensor_tensor(
                out=C(t, 4), in0=C(t, 0), in1=C(t, 0), op=ALU.mult))
        for t in range(n_tiles):
            v_.append(lambda e, t=t: e.tensor_scalar(
                out=C(t, 4), in0=C(t, 4), scalar1=1.0 / (D * D), scalar2=None,
                op0=ALU.mult))
        for t in range(n_tiles):
            v_.append(lambda e, t=t: e.tensor_tensor(
                out=C(t, 4), in0=C(t, 3), in1=C(t, 4), op=ALU.subtract))
        v2 = sV.take()
        for t in range(n_tiles):
            v_.append(lambda e, t=t, last=(t == n_tiles - 1): (
                e.tensor_scalar(out=C(t, 2), in0=C(t, 0), scalar1=1.0 / D,
                                scalar2=None, op0=ALU.mult)
                .then_inc(sV.h, 1) if last else
                e.tensor_scalar(out=C(t, 2), in0=C(t, 0), scalar1=1.0 / D,
                                scalar2=None, op0=ALU.mult)))
        s_.append(lambda e, v=v2: kb.wait("scalar", e, sV, v))
        v3 = sS.take()
        for t in range(n_tiles):
            v_dummy = None
            s_.append(lambda e, t=t, last=(t == n_tiles - 1): (
                e.activation(out=C(t, 5), in_=C(t, 4), func=AF.Sqrt,
                             scale=1.0, bias=cst[:, 0:1])
                .then_inc(sS.h, 1) if last else
                e.activation(out=C(t, 5), in_=C(t, 4), func=AF.Sqrt,
                             scale=1.0, bias=cst[:, 0:1])))
        v_.append(lambda e, v=v3: kb.wait("vector", e, sS, v))
        for t in range(n_tiles):
            v_.append(lambda e, t=t: e.reciprocal(out=C(t, 6), in_=C(t, 5)))
        for t in range(n_tiles):
            v_.append(lambda e, t=t: e.tensor_scalar(
                out=C(t, 7), in0=C(t, 2), scalar1=C(t, 6), scalar2=None,
                op0=ALU.mult))
        v4 = sV.take()
        for t in range(n_tiles):
            v_.append(lambda e, t=t, last=(t == n_tiles - 1): (
                e.tensor_scalar(out=C(t, 7), in0=C(t, 7), scalar1=-1.0,
                                scalar2=None, op0=ALU.mult)
                .then_inc(sV.h, 1) if last else
                e.tensor_scalar(out=C(t, 7), in0=C(t, 7), scalar1=-1.0,
                                scalar2=None, op0=ALU.mult)))
        s_.append(lambda e, v=v4: kb.wait("scalar", e, sV, v))
        v5 = sS.take()
        for t in range(n_tiles):
            s_.append(lambda e, t=t, last=(t == n_tiles - 1): (
                e.activation(out=dst_tile(t), in_=src_tile(t),
                             func=AF.Identity, scale=C(t, 6), bias=C(t, 7))
                .then_inc(sS.h, 1) if last else
                e.activation(out=dst_tile(t), in_=src_tile(t),
                             func=AF.Identity, scale=C(t, 6), bias=C(t, 7))))
        if gb is not None:
            gba, bba = gb
            v_.append(lambda e, v=v5: kb.wait("vector", e, sS, v))
            if bba is None:
                sV.take()
                for t in range(n_tiles):
                    v_.append(lambda e, t=t, last=(t == n_tiles - 1): (
                        e.tensor_tensor(out=dst_tile(t), in0=dst_tile(t),
                                        in1=gba, op=ALU.mult)
                        .then_inc(sV.h, 1) if last else
                        e.tensor_tensor(out=dst_tile(t), in0=dst_tile(t),
                                        in1=gba, op=ALU.mult)))
            else:
                for t in range(n_tiles):
                    v_.append(lambda e, t=t: e.tensor_tensor(
                        out=dst_tile(t), in0=dst_tile(t), in1=gba, op=ALU.mult))
                sV.take()
                for t in range(n_tiles):
                    v_.append(lambda e, t=t, last=(t == n_tiles - 1): (
                        e.tensor_tensor(out=dst_tile(t), in0=dst_tile(t), in1=bba,
                                        op=ALU.add).then_inc(sV.h, 1) if last else
                        e.tensor_tensor(out=dst_tile(t), in0=dst_tile(t), in1=bba,
                                        op=ALU.add)))
        kb.block(name, ops)

    def transpose_tok_to_feat(name, src, dst, n_tiles=NT, nchunks=NC):
        ops = {"tensor": [], "vector": []}
        t_, v_ = ops["tensor"], ops["vector"]
        ngrp = n_tiles // 8
        for c in range(nchunks):
            for tg in range(ngrp):
                bank = (c * ngrp + tg) % 2
                tn = sT.take()
                for k8 in range(8):
                    t = tg * 8 + k8
                    pv = psA[:, bank, k8 * 64:(k8 + 1) * 64].bitcast(BF16)
                    t_.append(lambda e, pv=pv, src=src, t=t, c=c,
                              last=(k8 == 7): (
                        e.transpose(pv, src[:, t, c * 128:(c + 1) * 128],
                                    ident_b[:]).then_inc(sT.h, 1) if last else
                        e.transpose(pv, src[:, t, c * 128:(c + 1) * 128],
                                    ident_b[:])))
                v_.append(lambda e, v=tn: kb.wait("vector", e, sT, v))
                pvfull = psA[:, bank, :].bitcast(BF16)
                dsl = dst[:, c, tg * 1024:(tg + 1) * 1024]
                vn = sV.take()
                v_.append(lambda e, pvfull=pvfull, dsl=dsl: e.tensor_copy(
                    out=dsl, in_=pvfull).then_inc(sV.h, 1))
                t_.append(lambda e, v=vn: kb.wait("tensor", e, sV, v))
        kb.block(name, ops)

    def attention_phase(name, inT, qkv_w, proj_w, out_tiles,
                        qreg, oreg, causal, use_rope, lora,
                        rowbound=False):
        QT = carve(qreg, (128, 2, T), BF16)
        KT = carve(qreg + 8 * 1024, (128, 2, T), BF16)
        Vt = carve(qreg + 16 * 1024, (128, NT, 4 * 66), BF16)
        if rowbound:
            QT65 = carve(qreg, (65, 2, T), BF16)
            KT65 = carve(qreg + 8 * 1024, (65, 2, T), BF16)
            qabs_s = carve(qreg + 25 * 1024, (64, 512), BF16)
            aKbn = carve(qreg + 26 * 1024, (64, 8), BF16)
            aKf = carve(qreg + 26 * 1024 + 64, (64, 8), F32)
            scrb_l = carve(qreg + 27 * 1024, (128, 512), BF16)
            negbrow = carve(qreg + 28 * 1024, (1, 512), BF16)
            assert not use_rope and not causal
        OT = carve(oreg, (128, 2, T), BF16)
        Pbuf = carve(oreg + 8 * 1024, (128, 4, 512), BF16)
        scr = carve(oreg + 12 * 1024, (128, 512), F32)
        cols = carve(oreg + 14 * 1024, (128, 16), F32)
        bcol = carve(oreg + 14 * 1024 + 64, (128, 24), F32)
        Aq_l = carve(oreg + 14 * 1024 + 192, (128, 8, 8), BF16)
        Ap_l = carve(oreg + 14 * 1024 + 320, (128, 8, 8), BF16)
        fstage = carve(oreg + 27 * 1024, (128, 8, 128), F32)
        frow = carve(oreg + 27 * 1024, (1, 1024), F32)
        fstage_p = carve(oreg + 27 * 1024, (128, 2, 512), F32)
        wproj = wslab[:].rearrange("p a b -> p (a b)").rearrange(
            "p (c h x) -> p c h x", c=2, h=2)
        wsl = lambda hh: wslab[:, :, hh * 128:(hh + 1) * 128]
        wsl_v = wslab[:]
        if lora is not None:
            tmpq = carve(oreg + 14 * 1024 + 512, (8, T), BF16)
            Bproj_s = carve(oreg + 14 * 1024 + 512, (8, D), BF16)
            bprow = carve(oreg + 16 * 1024 + 512, (1, D), BF16)
            tmpP = carve(oreg + 18 * 1024 + 512, (8, T), BF16)
            Bqs = carve(oreg + 22 * 1024 + 512, (8, 3, 256), BF16)
            bvrow = carve(oreg + 24 * 1024 + 128, (1, D), BF16)
        if use_rope:
            scrb = carve(oreg + 14 * 1024 + 512, (128, 2, 512), BF16)
            cosb = carve(oreg + 16 * 1024 + 512, (128, T), BF16)
            sinb = carve(oreg + 20 * 1024 + 512, (128, T), BF16)
            bandb = carve(oreg + 25 * 1024, (128, 896), BF16)
            rotpb = carve(oreg + 24 * 1024 + 512, (128, 128), BF16)
        ops = {"sync": [], "gpsimd": [], "tensor": [], "scalar": [], "vector": []}
        sy_, g_, t_, s_, v_ = (ops["sync"], ops["gpsimd"], ops["tensor"],
                               ops["scalar"], ops["vector"])
        bank_free = {}

        def tw(ename, lst, *gates):
            for sem, val in gates:
                if sem is not None:
                    lst.append(lambda e, s=sem, v=val: kb.wait(ename, e, s, v))

        def bfree(b):
            return bank_free.get(b, (None, 0))

        if rowbound:
            v_.append(lambda e: e.memset(KT65[64:65, :, :], 1.0))
        if use_rope:
            for src, dst in ((cos_c, cosb), (sin_c, sinb), (band_c, bandb),
                             (rotp_c, rotpb)):
                sW.take(16)
                sy_.append(lambda e, s=src, d=dst: e.dma_start(
                    out=d[:], in_=s.ap()).then_inc(sW.h, 16))
        fstage_free = (None, 0)
        if lora is not None:
            for src, dst, pat in (
                    (lora["A_qkv"], Aq_l, "(c p) r -> p c r"),
                    (lora["A_proj"], Ap_l, "(c p) r -> p c r")):
                sG.take(16)
                g_.append(lambda e, s=src, d=dst, pat=pat: e.dma_start(
                    out=d[:], in_=s.rearrange(pat, p=128)).then_inc(sG.h, 16))
            sG.take(16)
            g_.append(lambda e: e.dma_start(
                out=bvrow[0:1, :],
                in_=lora["b_qkv"][None, 2 * D:3 * D]).then_inc(sG.h, 16))
            setup_g = sG.n
            for part in range(2):
                wv0 = sW.take(16)
                fs, fv = fstage_free
                tw("sync", sy_, (fs, fv))
                sy_.append(lambda e, part=part: e.dma_start(
                    out=frow[0:1, :],
                    in_=lora["b_qkv"][None, part * D:(part + 1) * D])
                    .then_inc(sW.h, 16))
                tw("tensor", t_, (sW, wv0), bfree(5))
                tn = sT.take()
                for c in range(8):
                    t_.append(lambda e, part=part, c=c, last=(c == 7): (
                        e.transpose(psA[:, 5, part * 8 + c:part * 8 + c + 1],
                                    frow[0:1, c * 128:(c + 1) * 128],
                                    ident_f[0:1, 0:1]).then_inc(sT.h, 1)
                        if last else
                        e.transpose(psA[:, 5, part * 8 + c:part * 8 + c + 1],
                                    frow[0:1, c * 128:(c + 1) * 128],
                                    ident_f[0:1, 0:1])))
                fstage_free = (sT, tn)
            tw("vector", v_, (sT, sT.n))
            vb = sV.take()
            v_.append(lambda e: e.tensor_copy(
                out=bcol[:, 0:16], in_=psA[:, 5, 0:16]).then_inc(sV.h, 1))
            bank_free[5] = (sV, vb)
            tw("tensor", t_, (sG, setup_g))
            for tq in range(4):
                tw("tensor", t_, bfree(6))
                tn0 = sT.take()
                for c in range(8):
                    t_.append(lambda e, tq=tq, c=c, last=(c == 7): (
                        e.matmul(psA[0:8, 6, :], Aq_l[:, c, :],
                                 inT[:, c, tq * 512:(tq + 1) * 512],
                                 start=(c == 0), stop=last).then_inc(sT.h, 1)
                        if last else
                        e.matmul(psA[0:8, 6, :], Aq_l[:, c, :],
                                 inT[:, c, tq * 512:(tq + 1) * 512],
                                 start=(c == 0), stop=False)))
                tw("scalar", s_, (sT, tn0))
                ssn = sS.take()
                s_.append(lambda e, tq=tq: e.activation(
                    out=tmpq[:, tq * 512:(tq + 1) * 512],
                    in_=psA[0:8, 6, :], func=AF.Identity,
                    scale=2.0).then_inc(sS.h, 1))
                bank_free[6] = (sS, ssn)
            tmpq_ready = (sS, sS.n)

        half_ready = {0: (None, 0), 1: (None, 0)}
        half_busy = {0: (None, 0), 1: (None, 0)}

        def wload(coff, half):
            nonlocal fstage_free
            wv0 = sW.take(16)
            tw("sync", sy_, fstage_free)
            sy_.append(lambda e, coff=coff: e.dma_start(
                out=fstage[:],
                in_=qkv_w.rearrange("(c p) n -> p c n", p=128)
                [:, :, coff:coff + 128]).then_inc(sW.h, 16))
            tw("vector", v_, (sW, wv0), half_busy[half])
            vn = sV.take()
            v_.append(lambda e, half=half: e.tensor_copy(
                out=wsl(half), in_=fstage[:]).then_inc(sV.h, 1))
            fstage_free = (sV, vn)
            half_ready[half] = (sV, vn)

        score_gate_prev = (None, 0)
        scrb_gate = (None, 0)
        bqs_waru = (None, 0)
        for hg in range(4):
            qcol = hg * 256
            if lora is not None:
                tw("gpsimd", g_, bqs_waru)
                bq_gate = sG.take(16)
                g_.append(lambda e, hg=hg: e.dma_start(
                    out=Bqs[:],
                    in_=bass.AP(lora["B_qkv"].tensor, hg * 256,
                                [[3 * D, 8], [D, 3], [1, 256]]))
                    .then_inc(sG.h, 16))
            else:
                bq_gate = 0
            if rowbound:
                wload(2 * D + qcol, 0)
                wload(2 * D + qcol + 128, 1)
                vn_ones = sV.take()
                v_.append(lambda e: e.memset(
                    Vt[:].rearrange("p t (h w) -> p t h w", h=4)
                    [:, :, :, 64:66], 1.0).then_inc(sV.h, 1))
                for tt in range(NT):
                    bsel = tt % 2
                    tw("tensor", t_, half_ready[0], half_ready[1],
                       bfree(bsel))
                    tn0 = sT.take()
                    for c in range(8):
                        first = (c == 0)
                        last = (c == 7 and lora is None)
                        t_.append(lambda e, c=c, tt=tt, bsel=bsel,
                                  first=first, last=last: (
                            e.matmul(psA[:, bsel, 0:256],
                                     inT[:, c, tt * 128:(tt + 1) * 128],
                                     wsl_v[:, c, :], start=first, stop=last)
                            .then_inc(sT.h, 1) if last else
                            e.matmul(psA[:, bsel, 0:256],
                                     inT[:, c, tt * 128:(tt + 1) * 128],
                                     wsl_v[:, c, :], start=first,
                                     stop=False)))
                    if lora is not None:
                        tw("tensor", t_, (sG, bq_gate), tmpq_ready)
                        t_.append(lambda e, bsel=bsel, qcol=qcol: e.matmul(
                            psA[:, bsel, 0:256], onesb[0:1, :],
                            bvrow[0:1, qcol:qcol + 256], start=False,
                            stop=False))
                        t_.append(lambda e, tt=tt, bsel=bsel: e.matmul(
                            psA[:, bsel, 0:256],
                            tmpq[:, tt * 128:(tt + 1) * 128],
                            Bqs[:, 2, :], start=False, stop=True)
                            .then_inc(sT.h, 1))
                    tw("vector", v_, (sT, tn0))
                    vn = sV.take()
                    v_.append(lambda e, tt=tt, bsel=bsel: e.tensor_copy(
                        out=Vt[:, tt, :].rearrange("p (h w) -> p h w", h=4)
                        [:, :, 0:64],
                        in_=psA[:, bsel, 0:256]
                        .rearrange("p (h w) -> p h w", h=4))
                        .then_inc(sV.h, 1))
                    bank_free[bsel] = (sV, vn)
                half_busy[0] = (sT, sT.n)
                half_busy[1] = (sT, sT.n)
                v_all = sV.n
                for pair in range(2):
                    negb_gate = {}
                    for dst_i, buf65 in ((0, QT65), (1, KT65)):
                        half = dst_i
                        wload(dst_i * D + qcol + pair * 128, half)
                        for tq in range(4):
                            bsel = tq % 2
                            tw("tensor", t_, half_ready[half], bfree(bsel))
                            tn0 = sT.take()
                            for c in range(8):
                                first = (c == 0)
                                last = (c == 7 and lora is None)
                                t_.append(lambda e, c=c, half=half, tq=tq,
                                          bsel=bsel, first=first,
                                          last=last: (
                                    e.matmul(psA[:, bsel, :],
                                             wsl(half)[:, c, :],
                                             inT[:, c,
                                                 tq * 512:(tq + 1) * 512],
                                             start=first, stop=last)
                                    .then_inc(sT.h, 1) if last else
                                    e.matmul(psA[:, bsel, :],
                                             wsl(half)[:, c, :],
                                             inT[:, c,
                                                 tq * 512:(tq + 1) * 512],
                                             start=first, stop=False)))
                            if lora is not None:
                                tw("tensor", t_, (sG, bq_gate))
                                t_.append(lambda e, dst_i=dst_i, pair=pair,
                                          tq=tq, bsel=bsel: e.matmul(
                                    psA[:, bsel, :],
                                    Bqs[:, dst_i,
                                        pair * 128:pair * 128 + 128],
                                    tmpq[:, tq * 512:(tq + 1) * 512],
                                    start=False, stop=True)
                                    .then_inc(sT.h, 1))
                            tw("scalar", s_, (sT, tn0), scrb_gate)
                            ssn = sS.take()
                            bidx = dst_i * 8 + (qcol + pair * 128) // 128
                            if lora is not None:
                                s_.append(lambda e, bsel=bsel, bidx=bidx:
                                          e.activation(
                                    out=scrb_l[:], in_=psA[:, bsel, :],
                                    func=AF.Identity,
                                    bias=bcol[:, bidx:bidx + 1])
                                    .then_inc(sS.h, 1))
                            else:
                                s_.append(lambda e, bsel=bsel: e.activation(
                                    out=scrb_l[:], in_=psA[:, bsel, :],
                                    func=AF.Identity).then_inc(sS.h, 1))
                            bank_free[bsel] = (sS, ssn)
                            tw("vector", v_, (sS, ssn), score_gate_prev)
                            vcp = sV.take()
                            v_.append(lambda e, buf65=buf65, tq=tq: (
                                e.tensor_copy(
                                    out=buf65[0:64, 0,
                                              tq * 512:(tq + 1) * 512],
                                    in_=scrb_l[0:64, :]),
                                e.tensor_copy(
                                    out=buf65[0:64, 1,
                                              tq * 512:(tq + 1) * 512],
                                    in_=scrb_l[64:128, :])
                                .then_inc(sV.h, 1)))
                            scrb_gate = (sV, vcp)
                        half_busy[half] = (sT, sT.n)
                    vaK0 = sV.take()
                    v_.append(lambda e: (
                        e.reduce_max(out=aKf[:, 0:1], in_=KT65[0:64, 0, :],
                                     axis=AX.X, apply_absolute_value=True),
                        e.reduce_max(out=aKf[:, 1:2], in_=KT65[0:64, 1, :],
                                     axis=AX.X, apply_absolute_value=True)
                        .then_inc(sV.h, 1)))
                    tw("vector", v_, (sV, vaK0))
                    vaK = sV.take()
                    v_.append(lambda e: e.tensor_scalar(
                        out=aKbn[:, 0:2], in0=aKf[:, 0:2], scalar1=-1.0,
                        scalar2=None, op0=ALU.mult).then_inc(sV.h, 1))
                    qa_gate = scrb_gate
                    B_prev = (None, 0)
                    for j in range(2):
                        for tq in range(4):
                            tw("scalar", s_, qa_gate, B_prev)
                            sqa = sS.take()
                            s_.append(lambda e, j=j, tq=tq: e.activation(
                                out=qabs_s[:],
                                in_=QT65[0:64, j, tq * 512:(tq + 1) * 512],
                                func=AF.Abs).then_inc(sS.h, 1))
                            tw("tensor", t_, (sS, sqa), bfree(5),
                               (sV, vaK))
                            tB = sT.take()
                            t_.append(lambda e, j=j: e.matmul(
                                psA[0:1, 5, 0:512], aKbn[:, j:j + 1],
                                qabs_s[:], start=True, stop=True)
                                .then_inc(sT.h, 1))
                            tw("vector", v_, (sT, tB))
                            vnb0 = sV.take()
                            v_.append(lambda e: e.tensor_copy(
                                out=negbrow[0:1, :],
                                in_=psA[0:1, 5, 0:512]).then_inc(sV.h, 1))
                            tw("vector", v_, (sV, vnb0))
                            vnb = sV.take()
                            v_.append(lambda e, j=j, tq=tq: e.tensor_copy(
                                out=QT65[64:65, j,
                                         tq * 512:(tq + 1) * 512],
                                in_=negbrow[0:1, :]).then_inc(sV.h, 1))
                            bank_free[5] = (sV, vnb)
                            negb_gate[(j, tq)] = (sV, vnb)
                            B_prev = (sT, tB)
                    tw("tensor", t_, (sV, v_all))
                    for j in range(2):
                        hh = pair * 2 + j
                        for tq in range(4):
                            tw("tensor", t_, negb_gate[(j, tq)])
                            groups = [list(range(g4, g4 + 4))
                                      for g4 in range(0, 16, 4)]
                            first_pv = True
                            for gi, grp in enumerate(groups):
                                for kk, kc in enumerate(grp):
                                    tw("tensor", t_, bfree(kk))
                                tn0 = sT.take()
                                for kk, kc in enumerate(grp):
                                    last = (kk == len(grp) - 1)
                                    t_.append(lambda e, kk=kk, kc=kc, j=j,
                                              tq=tq, last=last: (
                                        e.matmul(
                                            psA[:, kk, :],
                                            KT65[0:65, j,
                                                 kc * 128:(kc + 1) * 128],
                                            QT65[0:65, j,
                                                 tq * 512:(tq + 1) * 512],
                                            start=True, stop=True)
                                        .then_inc(sT.h, 1) if last else
                                        e.matmul(
                                            psA[:, kk, :],
                                            KT65[0:65, j,
                                                 kc * 128:(kc + 1) * 128],
                                            QT65[0:65, j,
                                                 tq * 512:(tq + 1) * 512],
                                            start=True, stop=True)))
                                ng = len(grp)
                                tw("scalar", s_, (sT, tn0))
                                ssn = sS.take()
                                s_.append(lambda e, ng=ng: e.activation(
                                    out=Pbuf[:, 0:ng, :],
                                    in_=psA[:, 0:ng, :],
                                    func=AF.Exp, scale=0.125,
                                    bias=0.0).then_inc(sS.h, 1))
                                pv_gate = (sS, ssn)
                                tw("tensor", t_, pv_gate)
                                if first_pv:
                                    tw("tensor", t_, bfree(4))
                                for kk, kc in enumerate(grp):
                                    islast = (gi == len(groups) - 1) and \
                                        (kk == len(grp) - 1)
                                    tpv = sT.take()
                                    t_.append(lambda e, kk=kk, kc=kc, hh=hh,
                                              fp=(first_pv and kk == 0),
                                              lo=islast:
                                        e.matmul(psA[0:65, 4, :],
                                                 Vt[:, kc,
                                                    hh * 66:hh * 66 + 65],
                                                 Pbuf[:, kk, :],
                                                 start=fp, stop=lo)
                                        .then_inc(sT.h, 1))
                                    bank_free[kk] = (sT, tpv)
                                first_pv = False
                            odone = sT.n
                            tw("vector", v_, (sT, odone))
                            vr = sV.take()
                            v_.append(lambda e: e.reciprocal(
                                out=scr[64:65, 0:512],
                                in_=psA[64:65, 4, :]).then_inc(sV.h, 1))
                            tw("tensor", t_, (sV, vr), bfree(5))
                            tb = sT.take()
                            t_.append(lambda e: e.matmul(
                                psA[0:64, 5, :], onesf[64:65, 0:64],
                                scr[64:65, 0:512], start=True, stop=True)
                                .then_inc(sT.h, 1))
                            tw("vector", v_, (sT, tb))
                            vc1 = sV.take()
                            v_.append(lambda e: e.tensor_copy(
                                out=scr[0:64, :], in_=psA[0:64, 5, :])
                                .then_inc(sV.h, 1))
                            tw("vector", v_, (sV, vc1))
                            vv2 = sV.take()
                            v_.append(lambda e, hh=hh, tq=tq:
                                e.tensor_tensor(
                                    out=OT[(hh % 2) * 64:(hh % 2) * 64 + 64,
                                           hh // 2,
                                           tq * 512:(tq + 1) * 512],
                                    in0=psA[0:64, 4, :], in1=scr[0:64, :],
                                    op=ALU.mult).then_inc(sV.h, 1))
                            bank_free[5] = (sV, vv2)
                            bank_free[4] = (sV, vv2)
                    score_gate_prev = (sT, sT.n)
            else:
                for dst_i, buf in ((0, QT), (1, KT)):
                    for c2 in range(2):
                        wload(dst_i * D + qcol + c2 * 128, c2)
                        for tq in range(4):
                            bsel = (c2 * 4 + tq) % 2
                            tw("tensor", t_, half_ready[c2], bfree(bsel))
                            tn0 = sT.take()
                            for c in range(8):
                                first = (c == 0)
                                last = (c == 7 and lora is None)
                                t_.append(lambda e, c=c, c2=c2, tq=tq, bsel=bsel,
                                          first=first, last=last: (
                                    e.matmul(psA[:, bsel, :], wsl(c2)[:, c, :],
                                             inT[:, c, tq * 512:(tq + 1) * 512],
                                             start=first, stop=last)
                                    .then_inc(sT.h, 1) if last else
                                    e.matmul(psA[:, bsel, :], wsl(c2)[:, c, :],
                                             inT[:, c, tq * 512:(tq + 1) * 512],
                                             start=first, stop=False)))
                            if lora is not None:
                                tw("tensor", t_, (sG, bq_gate))
                                t_.append(lambda e, dst_i=dst_i, c2=c2, tq=tq,
                                          bsel=bsel: e.matmul(
                                    psA[:, bsel, :],
                                    Bqs[:, dst_i, c2 * 128:c2 * 128 + 128],
                                    tmpq[:, tq * 512:(tq + 1) * 512],
                                    start=False, stop=True).then_inc(sT.h, 1))
                            tw("scalar", s_, (sT, tn0))
                            ssn = sS.take()
                            bidx = dst_i * 8 + (qcol + c2 * 128) // 128
                            if lora is not None:
                                s_.append(lambda e, buf=buf, c2=c2, tq=tq,
                                          bsel=bsel, bidx=bidx: e.activation(
                                    out=buf[:, c2, tq * 512:(tq + 1) * 512],
                                    in_=psA[:, bsel, :], func=AF.Identity,
                                    bias=bcol[:, bidx:bidx + 1]).then_inc(sS.h, 1))
                            else:
                                s_.append(lambda e, buf=buf, c2=c2, tq=tq,
                                          bsel=bsel: e.activation(
                                    out=buf[:, c2, tq * 512:(tq + 1) * 512],
                                    in_=psA[:, bsel, :], func=AF.Identity)
                                    .then_inc(sS.h, 1))
                            bank_free[bsel] = (sS, ssn)
                            if use_rope:
                                tw("tensor", t_, (sS, ssn), bfree(5))
                                rn = sT.take()
                                t_.append(lambda e, buf=buf, c2=c2, tq=tq:
                                    e.matmul(psA[:, 5, :], rotpb[:],
                                             buf[:, c2, tq * 512:(tq + 1) * 512],
                                             start=True, stop=True)
                                    .then_inc(sT.h, 1))
                                tw("vector", v_, (sT, rn))
                                vn1 = sV.take(2)
                                v_.append(lambda e, buf=buf, c2=c2, tq=tq: (
                                    e.tensor_tensor(
                                        out=scrb[:, 0, :],
                                        in0=buf[:, c2, tq * 512:(tq + 1) * 512],
                                        in1=cosb[:, tq * 512:(tq + 1) * 512],
                                        op=ALU.mult),
                                    e.tensor_tensor(
                                        out=scrb[:, 1, :], in0=psA[:, 5, :],
                                        in1=sinb[:, tq * 512:(tq + 1) * 512],
                                        op=ALU.mult).then_inc(sV.h, 2)))
                                bank_free[5] = (sV, vn1)
                                vn2 = sV.take()
                                v_.append(lambda e, buf=buf, c2=c2, tq=tq:
                                    e.tensor_tensor(
                                        out=buf[:, c2, tq * 512:(tq + 1) * 512],
                                        in0=scrb[:, 0, :], in1=scrb[:, 1, :],
                                        op=ALU.add).then_inc(sV.h, 1))
                        half_busy[c2] = (sT, sT.n)
                wload(2 * D + qcol, 0)
                wload(2 * D + qcol + 128, 1)
                vn_ones = sV.take()
                v_.append(lambda e: e.memset(
                    Vt[:].rearrange("p t (h w) -> p t h w", h=4)[:, :, :, 64:66],
                    1.0).then_inc(sV.h, 1))
                for tt in range(NT):
                    bsel = tt % 2
                    tw("tensor", t_, half_ready[0], half_ready[1], bfree(bsel))
                    tn0 = sT.take()
                    for c in range(8):
                        first = (c == 0)
                        last = (c == 7 and lora is None)
                        t_.append(lambda e, c=c, tt=tt, bsel=bsel,
                                  first=first, last=last: (
                            e.matmul(psA[:, bsel, 0:256],
                                     inT[:, c, tt * 128:(tt + 1) * 128],
                                     wsl_v[:, c, :], start=first, stop=last)
                            .then_inc(sT.h, 1) if last else
                            e.matmul(psA[:, bsel, 0:256],
                                     inT[:, c, tt * 128:(tt + 1) * 128],
                                     wsl_v[:, c, :], start=first, stop=False)))
                    if lora is not None:
                        t_.append(lambda e, bsel=bsel, qcol=qcol: e.matmul(
                            psA[:, bsel, 0:256], onesb[0:1, :],
                            bvrow[0:1, qcol:qcol + 256], start=False, stop=False))
                        t_.append(lambda e, tt=tt, bsel=bsel: e.matmul(
                            psA[:, bsel, 0:256], tmpq[:, tt * 128:(tt + 1) * 128],
                            Bqs[:, 2, :], start=False, stop=True).then_inc(sT.h, 1))
                    tw("vector", v_, (sT, tn0))
                    vn = sV.take()
                    v_.append(lambda e, tt=tt, bsel=bsel: e.tensor_copy(
                        out=Vt[:, tt, :].rearrange("p (h w) -> p h w", h=4)
                        [:, :, 0:64],
                        in_=psA[:, bsel, 0:256].rearrange("p (h w) -> p h w", h=4))
                        .then_inc(sV.h, 1))
                    bank_free[bsel] = (sV, vn)
                half_busy[0] = (sT, sT.n)
                half_busy[1] = (sT, sT.n)
                v_all = sV.n
                qk_gate = (sV, sV.n) if use_rope else (sS, sS.n)
                tw("vector", v_, qk_gate)
                vr1 = sV.take(2)
                v_.append(lambda e: (
                    e.reduce_max(out=cols[:, 0:1], in_=QT[:], axis=AX.XY,
                                 apply_absolute_value=True),
                    e.reduce_max(out=cols[:, 1:2], in_=KT[:], axis=AX.XY,
                                 apply_absolute_value=True).then_inc(sV.h, 2)))
                tw("tensor", t_, (sV, vr1), bfree(5))
                tn = sT.take()
                t_.append(lambda e: (
                    e.transpose(psA[0:1, 5, 0:128], cols[:, 0:1], ident_f[:]),
                    e.transpose(psA[0:1, 5, 128:256], cols[:, 1:2], ident_f[:])
                    .then_inc(sT.h, 1)))
                tw("vector", v_, (sT, tn))
                vr2 = sV.take(2)
                v_.append(lambda e: (
                    e.reduce_max(out=scr[0:1, 0:1], in_=psA[0:1, 5, 0:128],
                                 axis=AX.X),
                    e.reduce_max(out=scr[0:1, 1:2], in_=psA[0:1, 5, 128:256],
                                 axis=AX.X).then_inc(sV.h, 2)))
                tw("vector", v_, (sV, vr2))
                vr3 = sV.take()
                v_.append(lambda e: e.tensor_tensor(
                    out=scr[0:1, 2:3], in0=scr[0:1, 0:1], in1=scr[0:1, 1:2],
                    op=ALU.mult).then_inc(sV.h, 1))
                tw("tensor", t_, (sV, vr3))
                tn2 = sT.take()
                t_.append(lambda e: e.matmul(
                    psA[:, 5, 0:1], onesf[0:1, :], scr[0:1, 2:3],
                    start=True, stop=True).then_inc(sT.h, 1))
                tw("vector", v_, (sT, tn2))
                vneg = sV.take()
                v_.append(lambda e: e.tensor_scalar(
                    out=cols[:, 2:3], in0=psA[:, 5, 0:1], scalar1=-8.0,
                    scalar2=None, op0=ALU.mult).then_inc(sV.h, 1))
                bank_free[5] = (sV, vneg)
                tw("tensor", t_, (sV, v_all))
                for hh in range(4):
                    sub = (hh % 2) * 64
                    hc = hh // 2
                    for tq in range(4):
                        kcs = list(range(16)) if not causal else \
                            list(range(4 * tq + 4))
                        groups = [kcs[i:i + 4] for i in range(0, len(kcs), 4)]
                        first_pv = True
                        for gi, grp in enumerate(groups):
                            for kk, kc in enumerate(grp):
                                tw("tensor", t_, bfree(kk))
                            tn0 = sT.take()
                            for kk, kc in enumerate(grp):
                                last = (kk == len(grp) - 1)
                                t_.append(lambda e, kk=kk, kc=kc, sub=sub, hc=hc,
                                          tq=tq, last=last: (
                                    e.matmul(psA[:, kk, :],
                                             KT[sub:sub + 64, hc,
                                                kc * 128:(kc + 1) * 128],
                                             QT[sub:sub + 64, hc,
                                                tq * 512:(tq + 1) * 512],
                                             start=True, stop=True)
                                    .then_inc(sT.h, 1) if last else
                                    e.matmul(psA[:, kk, :],
                                             KT[sub:sub + 64, hc,
                                                kc * 128:(kc + 1) * 128],
                                             QT[sub:sub + 64, hc,
                                                tq * 512:(tq + 1) * 512],
                                             start=True, stop=True)))
                            ng = len(grp)
                            tw("scalar", s_, (sT, tn0), (sV, vneg))
                            ssn = sS.take()
                            s_.append(lambda e, ng=ng: e.activation(
                                out=Pbuf[:, 0:ng, :], in_=psA[:, 0:ng, :],
                                func=AF.Exp, scale=0.125,
                                bias=cols[:, 2:3]).then_inc(sS.h, 1))
                            pv_gate = (sS, ssn)
                            if causal and grp[-1] >= 4 * tq:
                                mask_ops = tuple(
                                    (kk, 384 - 128 * (kc - 4 * tq))
                                    for kk, kc in enumerate(grp) if kc >= 4 * tq)
                                tw("vector", v_, (sS, ssn))
                                vm = sV.take()
                                def mk_mask(e, mo=mask_ops):
                                    for i, (kk, sh) in enumerate(mo):
                                        r = e.tensor_tensor(
                                            out=Pbuf[:, kk, :],
                                            in0=Pbuf[:, kk, :],
                                            in1=bandb[:, sh:sh + 512],
                                            op=ALU.mult)
                                        if i == len(mo) - 1:
                                            r.then_inc(sV.h, 1)
                                v_.append(mk_mask)
                                pv_gate = (sV, vm)
                            tw("tensor", t_, pv_gate)
                            if first_pv:
                                tw("tensor", t_, bfree(4))
                            for kk, kc in enumerate(grp):
                                islast = (gi == len(groups) - 1) and \
                                    (kk == len(grp) - 1)
                                tpv = sT.take()
                                t_.append(lambda e, kk=kk, kc=kc, hh=hh,
                                          fp=(first_pv and kk == 0), lo=islast:
                                    e.matmul(psA[0:65, 4, :],
                                             Vt[:, kc, hh * 66:hh * 66 + 65],
                                             Pbuf[:, kk, :], start=fp, stop=lo)
                                    .then_inc(sT.h, 1))
                                bank_free[kk] = (sT, tpv)
                            first_pv = False
                        odone = sT.n
                        tw("vector", v_, (sT, odone))
                        vr = sV.take()
                        v_.append(lambda e: e.reciprocal(
                            out=scr[64:65, 0:512], in_=psA[64:65, 4, :])
                            .then_inc(sV.h, 1))
                        tw("tensor", t_, (sV, vr), bfree(5))
                        tb = sT.take()
                        t_.append(lambda e: e.matmul(
                            psA[0:64, 5, :], onesf[64:65, 0:64],
                            scr[64:65, 0:512], start=True, stop=True)
                            .then_inc(sT.h, 1))
                        tw("vector", v_, (sT, tb))
                        vc1 = sV.take()
                        v_.append(lambda e: e.tensor_copy(
                            out=scr[0:64, :], in_=psA[0:64, 5, :])
                            .then_inc(sV.h, 1))
                        tw("vector", v_, (sV, vc1))
                        vv2 = sV.take()
                        v_.append(lambda e, hh=hh, tq=tq: e.tensor_tensor(
                            out=OT[(hh % 2) * 64:(hh % 2) * 64 + 64, hh // 2,
                                   tq * 512:(tq + 1) * 512],
                            in0=psA[0:64, 4, :], in1=scr[0:64, :],
                            op=ALU.mult).then_inc(sV.h, 1))
                        bank_free[5] = (sV, vv2)
                        bank_free[4] = (sV, vv2)

            pl_done = []
            for half in range(2):
                nonlocal_dummy = 0
                wv0 = sW.take(16)
                tw("sync", sy_, fstage_free)
                sy_.append(lambda e, half=half, hg=hg: e.dma_start(
                    out=fstage_p[:],
                    in_=proj_w.rearrange("(c p) n -> p c n", p=128)
                    [:, 2 * hg:2 * hg + 2, half * 512:(half + 1) * 512])
                    .then_inc(sW.h, 16))
                tw("vector", v_, (sW, wv0), half_busy[0], half_busy[1])
                vn = sV.take()
                v_.append(lambda e, half=half: e.tensor_copy(
                    out=wproj[:, :, half, :], in_=fstage_p[:])
                    .then_inc(sV.h, 1))
                fstage_free = (sV, vn)
                pl_done.append(vn)
            half_ready[0] = (sV, pl_done[1])
            half_ready[1] = (sV, pl_done[1])
            ot_done = sV.n
            for tt in range(NT):
                for half in range(2):
                    bsel = 6 + (tt * 2 + half) % 2
                    tw("tensor", t_, (sV, max(pl_done[half], ot_done)),
                       bfree(bsel))
                    tn0 = sT.take()
                    for c2 in range(2):
                        first, last = (c2 == 0), (c2 == 1)
                        t_.append(lambda e, tt=tt, half=half, c2=c2,
                                  bsel=bsel, first=first, last=last: (
                            e.matmul(psA[:, bsel, :],
                                     OT[:, c2, tt * 128:(tt + 1) * 128],
                                     wproj[:, c2, half, :],
                                     start=first, stop=last)
                            .then_inc(sT.h, 1) if last else
                            e.matmul(psA[:, bsel, :],
                                     OT[:, c2, tt * 128:(tt + 1) * 128],
                                     wproj[:, c2, half, :],
                                     start=first, stop=False)))
                    tw("vector", v_, (sT, tn0))
                    vn = sV.take()
                    v_.append(lambda e, tt=tt, half=half, bsel=bsel:
                        e.tensor_tensor(
                            out=out_tiles(tt)[:, half * 512:(half + 1) * 512],
                            in0=out_tiles(tt)[:, half * 512:(half + 1) * 512],
                            in1=psA[:, bsel, :], op=ALU.add).then_inc(sV.h, 1))
                    bank_free[bsel] = (sV, vn)
            half_busy[0] = (sT, sT.n)
            half_busy[1] = (sT, sT.n)
            if lora is not None:
                for tq in range(4):
                    bsel = 6 + tq % 2
                    tw("tensor", t_, bfree(bsel))
                    tn0 = sT.take()
                    for c2 in range(2):
                        first, last = (c2 == 0), (c2 == 1)
                        t_.append(lambda e, tq=tq, c2=c2, bsel=bsel, hg=hg,
                                  first=first, last=last: (
                            e.matmul(psA[0:8, bsel, :],
                                     Ap_l[:, hg * 2 + c2, :],
                                     OT[:, c2, tq * 512:(tq + 1) * 512],
                                     start=first, stop=last)
                            .then_inc(sT.h, 1) if last else
                            e.matmul(psA[0:8, bsel, :],
                                     Ap_l[:, hg * 2 + c2, :],
                                     OT[:, c2, tq * 512:(tq + 1) * 512],
                                     start=first, stop=False)))
                    tw("vector", v_, (sT, tn0))
                    vn = sV.take()
                    if hg == 0:
                        v_.append(lambda e, tq=tq, bsel=bsel: e.tensor_copy(
                            out=tmpP[:, tq * 512:(tq + 1) * 512],
                            in_=psA[0:8, bsel, :]).then_inc(sV.h, 1))
                    else:
                        v_.append(lambda e, tq=tq, bsel=bsel: e.tensor_tensor(
                            out=tmpP[:, tq * 512:(tq + 1) * 512],
                            in0=tmpP[:, tq * 512:(tq + 1) * 512],
                            in1=psA[0:8, bsel, :], op=ALU.add)
                            .then_inc(sV.h, 1))
                    bank_free[bsel] = (sV, vn)
            bqs_waru = (sT, sT.n)
        if lora is not None:
            tmpq_done = (sT, sT.n)
            tw("gpsimd", g_, tmpq_done)
            gq = sG.take(32)
            g_.append(lambda e: (
                e.dma_start(out=Bproj_s[:], in_=lora["B_proj"])
                .then_inc(sG.h, 16),
                e.dma_start(out=bprow[0:1, :], in_=lora["b_proj"][None, :])
                .then_inc(sG.h, 16)))
            tw("vector", v_, (sG, gq))
            vsc = sV.take()
            v_.append(lambda e: e.tensor_scalar(
                out=Bproj_s[:], in0=Bproj_s[:], scalar1=2.0, scalar2=None,
                op0=ALU.mult).then_inc(sV.h, 1))
            for tt in range(NT):
                for half in range(2):
                    bsel = 6 + (tt * 2 + half) % 2
                    tw("tensor", t_, (sV, vsc), bfree(bsel))
                    tn0 = sT.take()
                    t_.append(lambda e, tt=tt, half=half, bsel=bsel: (
                        e.matmul(psA[:, bsel, :],
                                 tmpP[:, tt * 128:(tt + 1) * 128],
                                 Bproj_s[:, half * 512:(half + 1) * 512],
                                 start=True, stop=False),
                        e.matmul(psA[:, bsel, :], onesb[0:1, :],
                                 bprow[0:1, half * 512:(half + 1) * 512],
                                 start=False, stop=True).then_inc(sT.h, 1)))
                    tw("vector", v_, (sT, tn0))
                    vn = sV.take()
                    v_.append(lambda e, tt=tt, half=half, bsel=bsel:
                        e.tensor_tensor(
                            out=out_tiles(tt)[:, half * 512:(half + 1) * 512],
                            in0=out_tiles(tt)[:, half * 512:(half + 1) * 512],
                            in1=psA[:, bsel, :], op=ALU.add).then_inc(sV.h, 1))
                    bank_free[bsel] = (sV, vn)
        kb.block(name, ops)

    ph0()

    xn_tok = carve(0, (128, NT, D), BF16)
    dbg_dump("acc0", acc[:], (128, NT, D), F32)
    if stages >= 1:
        ln_pass("l0_ln", lambda t: acc[:, t, :], lambda t: xn_tok[:, t, :])
        dbg_dump("xn", xn_tok[:], (128, NT, D), BF16)
        dbg_dump("lnst", lnst, (128, NT, 8), F32)

    xnT = carve(R1, (128, NC, T), BF16)
    if stages >= 2:
        transpose_tok_to_feat("l0_tr", xn_tok, xnT)
        dbg_dump("xnT", xnT[:], (128, NC, T), BF16)

    if stages >= 3:
        attention_phase("l0_attn", xnT, W_qkv, W_proj,
                        lambda t: acc[:, t, :],
                        qreg=R2, oreg=R3, causal=False, use_rope=False,
                        lora=dict(A_qkv=A_qkv, B_qkv=B_qkv, b_qkv=b_qkv,
                                  A_proj=A_proj, B_proj=B_proj,
                                  b_proj=b_proj),
                        rowbound=True)
        dbg_dump("acc_l03", acc[:], (128, NT, D), F32)
        dbg_dump("l0_QT", carve(R2, (65, 2, T), BF16), (65, 2, T), BF16)
        dbg_dump("l0_KT", carve(R2 + 8 * 1024, (65, 2, T), BF16), (65, 2, T), BF16)
        dbg_dump("l0_Vt", carve(R2 + 16 * 1024, (128, NT, 264), BF16), (128, NT, 264), BF16)
        dbg_dump("l0_OT", carve(R3, (128, 2, T), BF16), (128, 2, T), BF16)
        dbg_dump("l0_tmpq", carve(R3 + 14 * 1024 + 512, (8, T), BF16), (8, T), BF16)
        dbg_dump("l0_bcol", carve(R3 + 14 * 1024 + 64, (128, 16), F32), (128, 16), F32)
        dbg_dump("l0_cols", carve(R3 + 14 * 1024, (128, 3), F32), (128, 3), F32)
        dbg_dump("l0_Aql", carve(R3 + 14 * 1024 + 192, (128, 8, 8), BF16), (128, 8, 8), BF16)
        dbg_dump("l0_Bqs", carve(R3 + 22 * 1024 + 512, (8, 3, 256), BF16), (8, 3, 256), BF16)
        dbg_dump("l0_bvrow", carve(R3 + 24 * 1024 + 64, (1, D), BF16), (1, D), BF16)

    ln1out = carve(R2, (128, NT, D), BF16)
    ln1T = carve(R1, (128, NC, T), BF16)
    hconvT = carve(R2, (128, NC, T), BF16)
    if stages >= 4:
        g_row = carve(R3, (1, D), BF16)
        b_row = carve(R3 + 2 * 1024, (1, D), BF16)
        g_til = carve(R3 + 4 * 1024, (128, D), BF16)
        b_til = carve(R3 + 6 * 1024, (128, D), BF16)
        wcols = carve(R3 + 8 * 1024, (128, 8, 3), F32)
        ops = {"sync": [], "gpsimd": [], "tensor": [], "vector": []}
        gv0 = sG.take(32)
        ops["gpsimd"].append(lambda e: (
            e.dma_start(out=g_row[0:1, :], in_=ln1_w[None, :]).then_inc(sG.h, 16),
            e.dma_start(out=b_row[0:1, :], in_=ln1_b[None, :]).then_inc(sG.h, 16)))
        wv0 = sW.take(16)
        ops["sync"].append(lambda e: e.dma_start(
            out=wcols[:], in_=conv_w.rearrange("(c p) o k -> p c (o k)", p=128))
            .then_inc(sW.h, 16))
        ops["tensor"].append(lambda e, v=gv0: kb.wait("tensor", e, sG, v))
        tn = sT.take()
        ops["tensor"].append(lambda e: (
            e.matmul(psA[:, 0, :], onesb[0:1, :], g_row[0:1, 0:512],
                     start=True, stop=True),
            e.matmul(psA[:, 1, :], onesb[0:1, :], g_row[0:1, 512:1024],
                     start=True, stop=True),
            e.matmul(psA[:, 2, :], onesb[0:1, :], b_row[0:1, 0:512],
                     start=True, stop=True),
            e.matmul(psA[:, 3, :], onesb[0:1, :], b_row[0:1, 512:1024],
                     start=True, stop=True).then_inc(sT.h, 1)))
        ops["vector"].append(lambda e, v=tn: kb.wait("vector", e, sT, v))
        vn = sV.take()
        ops["vector"].append(lambda e: (
            e.tensor_copy(out=g_til[:, 0:512], in_=psA[:, 0, :]),
            e.tensor_copy(out=g_til[:, 512:1024], in_=psA[:, 1, :]),
            e.tensor_copy(out=b_til[:, 0:512], in_=psA[:, 2, :]),
            e.tensor_copy(out=b_til[:, 512:1024], in_=psA[:, 3, :])
            .then_inc(sV.h, 1)))
        kb.block("l04_pre", ops)
        ln_pass("l04_ln", lambda t: xn_tok[:, t, :], lambda t: ln1out[:, t, :],
                gb=(g_til[:], b_til[:]))
        dbg_dump("ln1", ln1out[:], (128, NT, D), BF16)
        transpose_tok_to_feat("l04_tr", ln1out, ln1T)
        ops = {"vector": []}
        wc = sW.n
        ops["vector"].append(lambda e, v=wc: kb.wait("vector", e, sW, v))
        for c in range(NC):
            ops["vector"].append(lambda e, c=c: e.tensor_scalar(
                out=hconvT[:, c, :], in0=ln1T[:, c, :],
                scalar1=wcols[:, c, 2:3], scalar2=None, op0=ALU.mult))
        for c in range(NC):
            ops["vector"].append(lambda e, c=c: e.scalar_tensor_tensor(
                out=hconvT[:, c, 1:T], in0=ln1T[:, c, 0:T - 1],
                scalar=wcols[:, c, 1:2], in1=hconvT[:, c, 1:T],
                op0=ALU.mult, op1=ALU.add))
        for c in range(NC):
            last = (c == NC - 1)
            sV.take() if last else None
            ops["vector"].append(lambda e, c=c, last=last: (
                e.scalar_tensor_tensor(
                    out=hconvT[:, c, 2:T], in0=ln1T[:, c, 0:T - 2],
                    scalar=wcols[:, c, 0:1], in1=hconvT[:, c, 2:T],
                    op0=ALU.mult, op1=ALU.add).then_inc(sV.h, 1) if last else
                e.scalar_tensor_tensor(
                    out=hconvT[:, c, 2:T], in0=ln1T[:, c, 0:T - 2],
                    scalar=wcols[:, c, 0:1], in1=hconvT[:, c, 2:T],
                    op0=ALU.mult, op1=ALU.add)))
        kb.block("l04_conv", ops)
        dbg_dump("hconvT", hconvT[:], (128, NC, T), BF16)

    if stages >= 5:
        attention_phase("l0i_attn", hconvT, attn_w, proj_w,
                        lambda t: xn_tok[:, t, :],
                        qreg=R1, oreg=R3, causal=True, use_rope=True,
                        lora=None)
        dbg_dump("x_f1", xn_tok[:], (128, NT, D), BF16)

    h2tok = carve(R2, (128, NT, D), BF16)
    h2T = carve(R0, (128, NC, T), BF16)
    if stages >= 6:
        g_row = carve(R3 + 2 * 1024, (1, D), BF16)
        b_row = carve(R3 + 4 * 1024, (1, D), BF16)
        g_til = carve(R3 + 6 * 1024, (128, D), BF16)
        b_til = carve(R3 + 8 * 1024, (128, D), BF16)
        ops = {"gpsimd": [], "tensor": [], "vector": []}
        gv0 = sG.take(32)
        ops["gpsimd"].append(lambda e: (
            e.dma_start(out=g_row[0:1, :], in_=ln2_w[None, :]).then_inc(sG.h, 16),
            e.dma_start(out=b_row[0:1, :], in_=ln2_b[None, :]).then_inc(sG.h, 16)))
        ops["tensor"].append(lambda e, v=gv0: kb.wait("tensor", e, sG, v))
        tn = sT.take()
        ops["tensor"].append(lambda e: (
            e.matmul(psA[:, 0, :], onesb[0:1, :], g_row[0:1, 0:512],
                     start=True, stop=True),
            e.matmul(psA[:, 1, :], onesb[0:1, :], g_row[0:1, 512:1024],
                     start=True, stop=True),
            e.matmul(psA[:, 2, :], onesb[0:1, :], b_row[0:1, 0:512],
                     start=True, stop=True),
            e.matmul(psA[:, 3, :], onesb[0:1, :], b_row[0:1, 512:1024],
                     start=True, stop=True).then_inc(sT.h, 1)))
        ops["vector"].append(lambda e, v=tn: kb.wait("vector", e, sT, v))
        sV.take()
        ops["vector"].append(lambda e: (
            e.tensor_copy(out=g_til[:, 0:512], in_=psA[:, 0, :]),
            e.tensor_copy(out=g_til[:, 512:1024], in_=psA[:, 1, :]),
            e.tensor_copy(out=b_til[:, 0:512], in_=psA[:, 2, :]),
            e.tensor_copy(out=b_til[:, 512:1024], in_=psA[:, 3, :])
            .then_inc(sV.h, 1)))
        kb.block("l06_pre", ops)
        ln_pass("l06_ln", lambda t: xn_tok[:, t, :], lambda t: h2tok[:, t, :],
                gb=(g_til[:], b_til[:]))
        dbg_dump("h2", h2tok[:], (128, NT, D), BF16)
        ops = {"vector": []}
        for t in range(NT):
            last = (t == NT - 1)
            sV.take() if last else None
            ops["vector"].append(lambda e, t=t, last=last: (
                e.tensor_tensor(out=acc[:, t, :], in0=acc[:, t, :],
                                in1=xn_tok[:, t, :], op=ALU.add)
                .then_inc(sV.h, 1) if last else
                e.tensor_tensor(out=acc[:, t, :], in0=acc[:, t, :],
                                in1=xn_tok[:, t, :], op=ALU.add)))
        kb.block("l06_accadd", ops)
        transpose_tok_to_feat("l06_tr", h2tok, h2T)

        masks = carve(R3, (128, NT, 4), F32)
        lg = carve(R3 + 256, (128, NT, 4), F32)
        mgescr = carve(R3 + 512, (128, NT, 8), F32)
        rslab = carve(R3 + 1536, (128, 8, 4), BF16)
        ops = {"gpsimd": [], "tensor": [], "vector": []}
        g_, t_, v_ = ops["gpsimd"], ops["tensor"], ops["vector"]
        gv0 = sG.take(16)
        g_.append(lambda e: e.dma_start(
            out=rslab[:], in_=router_w.rearrange("(c p) n -> p c n", p=128))
            .then_inc(sG.h, 16))
        t_.append(lambda e, v=gv0: kb.wait("tensor", e, sG, v))
        rcopy = {}
        for tt in range(NT):
            bsel = tt % 2
            if bsel in rcopy:
                t_.append(lambda e, v=rcopy[bsel]: kb.wait("tensor", e, sV, v))
            tn0 = sT.take()
            t_.append(lambda e, tt=tt, bsel=bsel: [
                e.matmul(psA[0:128, bsel, 0:4], h2T[:, c, tt * 128:(tt + 1) * 128],
                         rslab[:, c, :], start=(c == 0), stop=(c == 7))
                .then_inc(sT.h, 1) if c == 7 else
                e.matmul(psA[0:128, bsel, 0:4], h2T[:, c, tt * 128:(tt + 1) * 128],
                         rslab[:, c, :], start=(c == 0), stop=False)
                for c in range(8)][-1])
            v_.append(lambda e, v=tn0: kb.wait("vector", e, sT, v))
            vn = sV.take()
            v_.append(lambda e, tt=tt, bsel=bsel: e.tensor_copy(
                out=lg[:, tt, :], in_=psA[0:128, bsel, 0:4]).then_inc(sV.h, 1))
            rcopy[bsel] = vn
        for tt in range(NT):
            v_.append(lambda e, tt=tt: e.reduce_max(
                out=mgescr[:, tt, 0:1], in_=lg[:, tt, :], axis=AX.X))
        for ei in range(4):
            for tt in range(NT):
                v_.append(lambda e, tt=tt, ei=ei: e.tensor_scalar(
                    out=mgescr[:, tt, 1 + ei:2 + ei], in0=lg[:, tt, ei:ei + 1],
                    scalar1=mgescr[:, tt, 0:1], scalar2=None, op0=ALU.is_ge))
        for tt in range(NT):
            v_.append(lambda e, tt=tt: e.tensor_copy(
                out=masks[:, tt, 0:1], in_=mgescr[:, tt, 1:2]))
        for tt in range(NT):
            v_.append(lambda e, tt=tt: e.scalar_tensor_tensor(
                out=mgescr[:, tt, 5:6], in0=mgescr[:, tt, 1:2], scalar=-1.0,
                in1=cst[:, 3:4], op0=ALU.mult, op1=ALU.add))
        for tt in range(NT):
            v_.append(lambda e, tt=tt: e.tensor_tensor(
                out=masks[:, tt, 1:2], in0=mgescr[:, tt, 2:3],
                in1=mgescr[:, tt, 5:6], op=ALU.mult))
        for tt in range(NT):
            v_.append(lambda e, tt=tt: e.scalar_tensor_tensor(
                out=mgescr[:, tt, 6:7], in0=mgescr[:, tt, 2:3], scalar=-1.0,
                in1=cst[:, 3:4], op0=ALU.mult, op1=ALU.add))
        for tt in range(NT):
            v_.append(lambda e, tt=tt: e.tensor_tensor(
                out=mgescr[:, tt, 5:6], in0=mgescr[:, tt, 5:6],
                in1=mgescr[:, tt, 6:7], op=ALU.mult))
        for tt in range(NT):
            v_.append(lambda e, tt=tt: e.tensor_tensor(
                out=masks[:, tt, 2:3], in0=mgescr[:, tt, 3:4],
                in1=mgescr[:, tt, 5:6], op=ALU.mult))
        for tt in range(NT):
            v_.append(lambda e, tt=tt: e.scalar_tensor_tensor(
                out=mgescr[:, tt, 6:7], in0=mgescr[:, tt, 3:4], scalar=-1.0,
                in1=cst[:, 3:4], op0=ALU.mult, op1=ALU.add))
        for tt in range(NT):
            v_.append(lambda e, tt=tt: e.tensor_tensor(
                out=mgescr[:, tt, 5:6], in0=mgescr[:, tt, 5:6],
                in1=mgescr[:, tt, 6:7], op=ALU.mult))
        for tt in range(NT):
            last = (tt == NT - 1)
            sV.take() if last else None
            v_.append(lambda e, tt=tt, last=last: (
                e.tensor_tensor(out=masks[:, tt, 3:4], in0=mgescr[:, tt, 4:5],
                                in1=mgescr[:, tt, 5:6], op=ALU.mult)
                .then_inc(sV.h, 1) if last else
                e.tensor_tensor(out=masks[:, tt, 3:4], in0=mgescr[:, tt, 4:5],
                                in1=mgescr[:, tt, 5:6], op=ALU.mult)))
        kb.block("l06_router", ops)
        dbg_dump("moemasks", masks, (128, NT, 4), F32)
        dbg_dump("moelg", lg, (128, NT, 4), F32)
        dbg_dump("moerslab", rslab, (128, 8, 4), BF16)

        actA = carve(R1, (128, 16, D), BF16)
        actB = carve(R2, (128, 16, D), BF16)
        sgscr = carve(R3 + 1 * 1024 + 2048, (128, 2, 512), BF16)
        wvst = [carve(R3 + 12 * 1024 + i * 4096, (128, 8, 128), F32)
                for i in range(4)]
        pjst = [carve(R3 + 28 * 1024 + i * 2048, (128, 512), F32)
                for i in range(2)]
        mask_gate = sV.n
        ops = {"sync": [], "gpsimd": [], "tensor": [], "scalar": [], "vector": []}
        sy_, g_, t_, s_, v_ = (ops["sync"], ops["gpsimd"], ops["tensor"],
                               ops["scalar"], ops["vector"])
        bankF = {}

        def bfree2(b):
            return bankF.get(b, (None, 0))

        def tw2(ename, lst, *gates):
            for sem, val in gates:
                if sem is not None:
                    lst.append(lambda e, s=sem, v=val: kb.wait(ename, e, s, v))

        wv_view = expert_wv.rearrange("e (c p) n -> e p c n", p=128)
        pj_view = expert_proj.rearrange("e (f p) n -> e p f n", p=128)
        st_free = {0: (None, 0), 1: (None, 0), 2: (None, 0), 3: (None, 0)}
        half_cast = {0: (None, 0), 1: (None, 0)}
        half_used = {0: (None, 0), 1: (None, 0)}
        for e_i in range(NE):
            for half in range(2):
                tokoff = half * 1024
                for f in range(32):
                    ring_g, ring_v = (f % 2) * 2, (f % 2) * 2 + 1
                    wv0 = sW.take(32)
                    tw2("sync", sy_, st_free[ring_g], st_free[ring_v])
                    sy_.append(lambda e, e_i=e_i, f=f, rg=ring_g, rv=ring_v: (
                        e.dma_start(out=wvst[rg][:],
                                    in_=wv_view[e_i, :, :,
                                                f * 128:(f + 1) * 128])
                        .then_inc(sW.h, 16),
                        e.dma_start(out=wvst[rv][:],
                                    in_=wv_view[e_i, :, :, DFF + f * 128:
                                                DFF + (f + 1) * 128])
                        .then_inc(sW.h, 16)))
                    tw2("vector", v_, (sW, wv0 - 16), half_used[0])
                    vng = sV.take()
                    v_.append(lambda e, rg=ring_g: e.tensor_copy(
                        out=wsl(0), in_=wvst[rg][:]).then_inc(sV.h, 1))
                    tw2("scalar", s_, (sW, wv0), half_used[1])
                    sns = sS.take()
                    s_.append(lambda e, rv=ring_v: e.copy(
                        out=wsl(1), in_=wvst[rv][:]).then_inc(sS.h, 1))
                    half_cast[0] = (sV, vng)
                    half_cast[1] = (sS, sns)
                    for tb in range(2):
                        bg, bv2 = tb, 2 + tb
                        tw2("tensor", t_, half_cast[0], bfree2(bg))
                        tn0 = sT.take()
                        for c in range(8):
                            t_.append(lambda e, c=c, tb=tb, bg=bg,
                                      tokoff=tokoff, last=(c == 7): (
                                e.matmul(psA[:, bg, :], wsl(0)[:, c, :],
                                         h2T[:, c, tokoff + tb * 512:
                                             tokoff + (tb + 1) * 512],
                                         start=(c == 0), stop=last)
                                .then_inc(sT.h, 1) if last else
                                e.matmul(psA[:, bg, :], wsl(0)[:, c, :],
                                         h2T[:, c, tokoff + tb * 512:
                                             tokoff + (tb + 1) * 512],
                                         start=(c == 0), stop=False)))
                        tw2("tensor", t_, half_cast[1], bfree2(bv2))
                        tn1 = sT.take()
                        for c in range(8):
                            t_.append(lambda e, c=c, tb=tb, bv2=bv2,
                                      tokoff=tokoff, last=(c == 7): (
                                e.matmul(psA[:, bv2, :], wsl(1)[:, c, :],
                                         h2T[:, c, tokoff + tb * 512:
                                             tokoff + (tb + 1) * 512],
                                         start=(c == 0), stop=last)
                                .then_inc(sT.h, 1) if last else
                                e.matmul(psA[:, bv2, :], wsl(1)[:, c, :],
                                         h2T[:, c, tokoff + tb * 512:
                                             tokoff + (tb + 1) * 512],
                                         start=(c == 0), stop=False)))
                        tw2("scalar", s_, (sT, tn0))
                        sn2 = sS.take()
                        s_.append(lambda e, tb=tb, bg=bg: e.activation(
                            out=sgscr[:, tb, :], in_=psA[:, bg, :],
                            func=AF.Silu).then_inc(sS.h, 1))
                        bankF[bg] = (sS, sn2)
                        tw2("vector", v_, (sS, sn2), (sT, tn1))
                        vn2 = sV.take()
                        act_sl = (actA if f < 16 else actB)
                        v_.append(lambda e, f=f, tb=tb, bv2=bv2,
                                  act_sl=act_sl: e.tensor_tensor(
                            out=act_sl[:, f % 16, tb * 512:(tb + 1) * 512],
                            in0=sgscr[:, tb, :], in1=psA[:, bv2, :],
                            op=ALU.mult).then_inc(sV.h, 1))
                        bankF[bv2] = (sV, vn2)
                    st_free[ring_g] = (sV, half_cast[0][1])
                    st_free[ring_v] = (sS, half_cast[1][1])
                    half_used[0] = (sT, sT.n)
                    half_used[1] = (sT, sT.n)
                act_done_v = sV.n
                for dh in range(2):
                    pj_free = [(sS, sS.n), (sV, sV.n)]
                    evac_done = (sV, sV.n)
                    for f in range(32):
                        ring = f % 2
                        wv0 = sW.take(16)
                        tw2("sync", sy_, pj_free[ring])
                        sy_.append(lambda e, e_i=e_i, f=f, dh=dh, ring=ring:
                            e.dma_start(out=pjst[ring][:],
                                        in_=pj_view[e_i, :, f,
                                                    dh * 512:(dh + 1) * 512])
                            .then_inc(sW.h, 16))
                        dst = wslab[:].rearrange("p a b -> p (a b)")
                        if ring == 0:
                            tw2("scalar", s_, (sW, wv0), half_used[0])
                            snc = sS.take()
                            s_.append(lambda e, ring=ring, dst=dst:
                                e.copy(out=dst[:, 0:512],
                                       in_=pjst[ring][:])
                                .then_inc(sS.h, 1))
                            pj_free[0] = (sS, snc)
                            cast_gate = (sS, snc)
                        else:
                            tw2("vector", v_, (sW, wv0), half_used[1])
                            vnc = sV.take()
                            v_.append(lambda e, ring=ring, dst=dst:
                                e.tensor_copy(out=dst[:, 512:1024],
                                              in_=pjst[ring][:])
                                .then_inc(sV.h, 1))
                            pj_free[1] = (sV, vnc)
                            cast_gate = (sV, vnc)
                        tw2("tensor", t_, cast_gate, (sV, act_done_v))
                        if f == 0:
                            tw2("tensor", t_, evac_done)
                        tn0 = sT.take()
                        for ts in range(8):
                            act_sl = (actA if f < 16 else actB)
                            t_.append(lambda e, f=f, ts=ts, ring=ring,
                                      act_sl=act_sl, dst=dst,
                                      last=(ts == 7): (
                                e.matmul(psA[:, ts, :],
                                         act_sl[:, f % 16,
                                                ts * 128:(ts + 1) * 128],
                                         dst[:, ring * 512:(ring + 1) * 512],
                                         start=(f == 0), stop=(f == 31))
                                .then_inc(sT.h, 1) if last else
                                e.matmul(psA[:, ts, :],
                                         act_sl[:, f % 16,
                                                ts * 128:(ts + 1) * 128],
                                         dst[:, ring * 512:(ring + 1) * 512],
                                         start=(f == 0), stop=(f == 31))))
                            half_used[ring] = (sT, sT.n)
                        ymm_done = tn0
                    tw2("vector", v_, (sT, sT.n), (sV, mask_gate))
                    ev = 0
                    for ts in range(8):
                        tile = half * 8 + ts
                        last = (ts == 7)
                        ev = sV.take()
                        v_.append(lambda e, ts=ts, tile=tile, dh=dh,
                                  e_i=e_i: e.scalar_tensor_tensor(
                            out=acc[:, tile, dh * 512:(dh + 1) * 512],
                            in0=psA[:, ts, :],
                            scalar=masks[:, tile, e_i:e_i + 1],
                            in1=acc[:, tile, dh * 512:(dh + 1) * 512],
                            op0=ALU.mult, op1=ALU.add).then_inc(sV.h, 1))
                    evac_done = (sV, ev)
                    for b in range(8):
                        bankF[b] = evac_done
        kb.block("l06_moe", ops)
        dbg_dump("acc_l06", acc[:], (128, NT, D), F32)

    xn1tok = carve(R0, (128, NT, D), BF16)
    xn1T = carve(R1, (128, NC, T), BF16)
    qTf = carve(R2, (128, NC, T), BF16)
    ktok = carve(R3, (128, NT, D), BF16)
    vtok = carve(R0, (128, NT, D), BF16)
    knst = nc.alloc_sbuf_tensor("knst", [128, NT, 16], F32)

    def l1_feat_pass(name, wdram, dst_feat, dst_tok, scr_off, act=None,
                     norm=False):
        scrp = carve(scr_off, (128, 512), BF16)
        ops = {"gpsimd": [], "tensor": [], "scalar": [], "vector": []}
        g_, t_, s_, v_ = (ops["gpsimd"], ops["tensor"], ops["scalar"],
                          ops["vector"])
        bankF = {}
        half_ready = {0: (None, 0), 1: (None, 0)}
        half_used = {0: (None, 0), 1: (None, 0)}

        def tw2(ename, lst, *gates):
            for sem, val in gates:
                if sem is not None:
                    lst.append(lambda e, s=sem, v=val: kb.wait(ename, e, s, v))

        for dc in range(NC):
            half = dc % 2
            tw2("gpsimd", g_, half_used[half])
            gv = sG.take(16)
            g_.append(lambda e, dc=dc, half=half: e.dma_start(
                out=wsl(half),
                in_=wdram.rearrange("(c p) n -> p c n", p=128)
                [:, :, dc * 128:(dc + 1) * 128]).then_inc(sG.h, 16))
            half_ready[half] = (sG, gv)
            for tq in range(4):
                bsel = tq % 2
                tw2("tensor", t_, half_ready[half], bankF.get(bsel, (None, 0)))
                tn0 = sT.take()
                for c in range(8):
                    t_.append(lambda e, c=c, dc=dc, tq=tq, bsel=bsel,
                              half=half, last=(c == 7): (
                        e.matmul(psA[:, bsel, :], wsl(half)[:, c, :],
                                 xn1T[:, c, tq * 512:(tq + 1) * 512],
                                 start=(c == 0), stop=last).then_inc(sT.h, 1)
                        if last else
                        e.matmul(psA[:, bsel, :], wsl(half)[:, c, :],
                                 xn1T[:, c, tq * 512:(tq + 1) * 512],
                                 start=(c == 0), stop=False)))
                half_used[half] = (sT, sT.n)
                tw2("scalar", s_, (sT, tn0))
                ssn = sS.take()
                if dst_feat is not None:
                    s_.append(lambda e, dc=dc, tq=tq, bsel=bsel: e.activation(
                        out=dst_feat[:, dc, tq * 512:(tq + 1) * 512],
                        in_=psA[:, bsel, :], func=act).then_inc(sS.h, 1))
                    bankF[bsel] = (sS, ssn)
                else:
                    s_.append(lambda e, bsel=bsel: e.activation(
                        out=scrp[:], in_=psA[:, bsel, :],
                        func=act).then_inc(sS.h, 1))
                    bankF[bsel] = (sS, ssn)
                    tb = 4 + tq % 2
                    tw2("tensor", t_, (sS, ssn), bankF.get(tb, (None, 0)))
                    tn1 = sT.take()
                    for st4 in range(4):
                        pv = psA[:, tb, st4 * 64:(st4 + 1) * 64].bitcast(BF16)
                        t_.append(lambda e, pv=pv, st4=st4, last=(st4 == 3): (
                            e.transpose(pv, scrp[:, st4 * 128:(st4 + 1) * 128],
                                        ident_b[:]).then_inc(sT.h, 1)
                            if last else
                            e.transpose(pv, scrp[:, st4 * 128:(st4 + 1) * 128],
                                        ident_b[:])))
                    tw2("vector", v_, (sT, tn1))
                    vn = sV.take()
                    v_.append(lambda e, dc=dc, tq=tq, tb=tb: e.tensor_copy(
                        out=dst_tok[:, tq * 4:(tq + 1) * 4,
                                    dc * 128:(dc + 1) * 128],
                        in_=psA[:, tb, 0:256].bitcast(BF16)
                        .rearrange("p (a b) -> p a b", a=4)).then_inc(sV.h, 1))
                    bankF[tb] = (sV, vn)
                    tw2("scalar", s_, (sT, tn1))
        kb.block(name, ops)

    if stages >= 7:
        ln_pass("l1_ln", lambda t: acc[:, t, :], lambda t: xn1tok[:, t, :])
        dbg_dump("xn1", xn1tok[:], (128, NT, D), BF16)
        transpose_tok_to_feat("l1_tr", xn1tok, xn1T)

        mslab = carve(R2, (128, 8, 112), BF16)
        ops = {"gpsimd": [], "tensor": [], "scalar": [], "vector": []}
        g_, t_, s_, v_ = (ops["gpsimd"], ops["tensor"], ops["scalar"],
                          ops["vector"])
        gv = sG.take(16 * 4)
        g_.append(lambda e: (
            e.dma_start(out=mslab[:, :, 0:16],
                        in_=Wf.rearrange("(c p) n -> p c n", p=128))
            .then_inc(sG.h, 16),
            e.dma_start(out=mslab[:, :, 16:32],
                        in_=Wgamma.rearrange("(c p) n -> p c n", p=128))
            .then_inc(sG.h, 16),
            e.dma_start(out=mslab[:, :, 32:96],
                        in_=Wg1.rearrange("(c p) n -> p c n", p=128))
            .then_inc(sG.h, 16),
            e.dma_start(out=mslab[:, :, 96:112],
                        in_=Win.rearrange("(c p) n -> p c n", p=128)
                        [:, :, 0:16]).then_inc(sG.h, 16)))
        miX = carve(R2 + 2 * 1024, (128, 8, 64), BF16)
        gv2 = sG.take(16)
        g_.append(lambda e: e.dma_start(
            out=miX[:], in_=Win.rearrange("(c p) n -> p c n", p=128))
            .then_inc(sG.h, 16))
        binrow = carve(R2 + 4 * 1024, (1, 64), BF16)
        gv3 = sG.take(16)
        g_.append(lambda e: e.dma_start(
            out=binrow[0:1, :], in_=bin_[None, :]).then_inc(sG.h, 16))
        t_.append(lambda e, v=sG.n: kb.wait("tensor", e, sG, v))
        mbank = {}
        for tt in range(NT):
            b0 = tt % 2
            if b0 in mbank:
                t_.append(lambda e, v=mbank[b0]: kb.wait("tensor", e, sS, v))
            tn0 = sT.take()
            for c in range(8):
                t_.append(lambda e, c=c, tt=tt, b0=b0, last=(c == 7): (
                    e.matmul(psA[:, b0, 0:32],
                             xn1T[:, c, tt * 128:(tt + 1) * 128],
                             mslab[:, c, 0:32], start=(c == 0), stop=last)
                    .then_inc(sT.h, 1) if last else
                    e.matmul(psA[:, b0, 0:32],
                             xn1T[:, c, tt * 128:(tt + 1) * 128],
                             mslab[:, c, 0:32], start=(c == 0), stop=False)))
            b1 = 2 + tt % 2
            if b1 in mbank:
                t_.append(lambda e, v=mbank[b1]: kb.wait("tensor", e, sS, v))
            tn1 = sT.take()
            for c in range(8):
                t_.append(lambda e, c=c, tt=tt, b1=b1: e.matmul(
                    psA[:, b1, 0:64], xn1T[:, c, tt * 128:(tt + 1) * 128],
                    miX[:, c, :], start=(c == 0), stop=False))
            t_.append(lambda e, tt=tt, b1=b1: e.matmul(
                psA[:, b1, 0:64], onesb[0:1, :], binrow[0:1, :],
                start=False, stop=True).then_inc(sT.h, 1))
            s_.append(lambda e, v=tn0: kb.wait("scalar", e, sT, v))
            sn0 = sS.take(2)
            s_.append(lambda e, tt=tt, b0=b0: (
                e.activation(out=knst[:, tt, 0:16], in_=psA[:, b0, 0:16],
                             func=AF.Sigmoid),
                e.activation(out=gammatok[:, tt, :], in_=psA[:, b0, 16:32],
                             func=AF.Sigmoid).then_inc(sS.h, 2)))
            mbank[b0] = sn0
            s_.append(lambda e, v=tn1: kb.wait("scalar", e, sT, v))
            sn1 = sS.take()
            s_.append(lambda e, tt=tt, b1=b1: e.activation(
                out=sffn[:, tt, :], in_=psA[:, b1, 0:64],
                func=AF.Identity).then_inc(sS.h, 1))
            mbank[b1] = sn1
        for tt in range(NT):
            last = (tt == NT - 1)
            sS.take() if last else None
            s_.append(lambda e, tt=tt, last=last: (
                e.activation(out=logftok[:, tt, :], in_=knst[:, tt, 0:16],
                             func=AF.Ln).then_inc(sS.h, 1) if last else
                e.activation(out=logftok[:, tt, :], in_=knst[:, tt, 0:16],
                             func=AF.Ln)))
        for tq in range(4):
            b2 = 4 + tq % 2
            if b2 in mbank:
                t_.append(lambda e, v=mbank[b2]: kb.wait("tensor", e, sS, v))
            tn2 = sT.take()
            for c in range(8):
                t_.append(lambda e, c=c, tq=tq, b2=b2, last=(c == 7): (
                    e.matmul(psA[0:64, b2, :], mslab[:, c, 32:96],
                             xn1T[:, c, tq * 512:(tq + 1) * 512],
                             start=(c == 0), stop=last).then_inc(sT.h, 1)
                    if last else
                    e.matmul(psA[0:64, b2, :], mslab[:, c, 32:96],
                             xn1T[:, c, tq * 512:(tq + 1) * 512],
                             start=(c == 0), stop=False)))
            s_.append(lambda e, v=tn2: kb.wait("scalar", e, sT, v))
            sn2b = sS.take()
            s_.append(lambda e, tq=tq, b2=b2: e.activation(
                out=t1s[:, tq * 512:(tq + 1) * 512], in_=psA[0:64, b2, :],
                func=AF.Identity).then_inc(sS.h, 1))
            mbank[b2] = sn2b
        kb.block("l1_minis", ops)
        dbg_dump("logftok", logftok[:], (128, NT, 16), F32)
        dbg_dump("sffntok", sffn[:], (128, NT, 64), BF16)

        l1_feat_pass("l1_v", Wv1, None, vtok, R2, act=AF.Silu)
        dbg_dump("v1tok", vtok[:], (128, NT, D), BF16)
        l1_feat_pass("l1_k", Wk1, None, ktok, R2, act=AF.Silu)
        ops = {"scalar": [], "vector": []}
        s_, v_ = ops["scalar"], ops["vector"]
        pdump = psA[:, 6:8, :].rearrange("p a b -> p (a b)")
        vpre = sV.n
        s_.append(lambda e, v=vpre: kb.wait("scalar", e, sV, v))
        for tt in range(NT):
            sn = sS.take()
            s_.append(lambda e, tt=tt: (
                e.activation(out=pdump, in_=ktok[:, tt, :], func=AF.Square)
                .then_inc(sS.h, 1)))
            v_.append(lambda e, v=sn: kb.wait("vector", e, sS, v))
            vn = sV.take()
            v_.append(lambda e, tt=tt: e.reduce_sum(
                out=knst[:, tt, :],
                in_=pdump.rearrange("p (h w) -> p h w", h=16), axis=AX.X)
                .then_inc(sV.h, 1))
            s_.append(lambda e, v=vn: kb.wait("scalar", e, sV, v))
        for tt in range(NT):
            last = (tt == NT - 1)
            sS.take() if last else None
            s_.append(lambda e, tt=tt, last=last: (
                e.activation(out=knst[:, tt, :], in_=knst[:, tt, :],
                             func=AF.Sqrt).then_inc(sS.h, 1) if last else
                e.activation(out=knst[:, tt, :], in_=knst[:, tt, :],
                             func=AF.Sqrt)))
        v_.append(lambda e, v=sS.n: kb.wait("vector", e, sS, v))
        for tt in range(NT):
            v_.append(lambda e, tt=tt: e.tensor_single_scalar(
                out=knst[:, tt, :], in_=knst[:, tt, :], scalar=cst[:, 1:2],
                op=ALU.max))
        for tt in range(NT):
            v_.append(lambda e, tt=tt: e.reciprocal(
                out=knst[:, tt, :], in_=knst[:, tt, :]))
        for tt in range(NT):
            last = (tt == NT - 1)
            sV.take() if last else None
            v_.append(lambda e, tt=tt, last=last: (
                e.tensor_tensor(
                    out=ktok[:, tt, :].rearrange("p (h w) -> p h w", h=16),
                    in0=ktok[:, tt, :].rearrange("p (h w) -> p h w", h=16),
                    in1=knst[:, tt, :].unsqueeze(2).broadcast_to(
                        [128, 16, 64]), op=ALU.mult).then_inc(sV.h, 1)
                if last else
                e.tensor_tensor(
                    out=ktok[:, tt, :].rearrange("p (h w) -> p h w", h=16),
                    in0=ktok[:, tt, :].rearrange("p (h w) -> p h w", h=16),
                    in1=knst[:, tt, :].unsqueeze(2).broadcast_to(
                        [128, 16, 64]), op=ALU.mult)))
        kb.block("l1_knorm", ops)
        dbg_dump("k1tok", ktok[:], (128, NT, D), BF16)
        l1_feat_pass("l1_q", Wq1, qTf, None, R3 + 31 * 1024, act=AF.Silu)
        dbg_dump("qTf", qTf[:], (128, NC, T), BF16)
        dbg_dump("gamtok", gammatok[:], (128, NT, 16), BF16)

    if stages >= 8:
        selrb = carve(R1, (16, 16 * 128), F32)
        diffb = carve(R1 + 8 * 1024, (128, 4, 128), F32)
        dmaxb = carve(R1 + 10 * 1024, (128, 4, 128), F32)
        Db = carve(R1 + 12 * 1024, (128, 4, 128), BF16)
        Dtb = carve(R1 + 13 * 1024, (128, 4, 128), BF16)
        m1b = carve(R1 + 14 * 1024, (128, 4, 128), BF16)
        AqTb = carve(R1 + 15 * 1024, (128, 4, 128), BF16)
        Tb = carve(R1 + 16 * 1024, (128, 4, 128), BF16)
        Ttb = carve(R1 + 17 * 1024, (128, 4, 128), BF16)
        m2db = carve(R1 + 18 * 1024, (128, 4, 128), BF16)
        Ub = carve(R1 + 19 * 1024, (128, 4, 64), BF16)
        kpb = carve(R1 + 19 * 1024 + 512, (128, 4, 64), BF16)
        t5b = carve(R1 + 20 * 1024, (128, 4, 64), F32)
        t6b = carve(R1 + 20 * 1024, (128, 4, 64), F32)
        Uf = carve(R1 + 21 * 1024, (128, 4, 64), F32)
        Sf = carve(R1 + 22 * 1024, (64, 16, 64), F32)
        Sb_ = carve(R1 + 26 * 1024, (128, 8, 64), BF16)
        kfeat = carve(R1 + 27 * 1024, (128, 8, 128), BF16)
        lfrow = carve(R1 + 29 * 1024, (16, 128), F32)
        Lgrow = carve(R1 + 29 * 1024 + 512, (16, 128), F32)
        brow = carve(R1 + 30 * 1024, (16, 128), F32)
        carry = carve(R1 + 30 * 1024 + 512, (16, 1), F32)
        Lgtok_ = carve(R1 + 30 * 1024 + 576, (128, 16), F32)
        gae = carve(R1 + 30 * 1024 + 640, (128, 16), F32)
        bpos = carve(R1 + 30 * 1024 + 704, (128, 16), F32)
        kfac = carve(R1 + 30 * 1024 + 768, (128, 16), F32)
        eGa = carve(R1 + 30 * 1024 + 832, (128, 16), F32)
        Lam = carve(R1 + 30 * 1024 + 896, (128, 16), F32)
        cpos = carve(R1 + 30 * 1024 + 960, (128, 16), F32)
        cscr = carve(R1 + 31 * 1024, (128, 16), F32)
        lamd = carve(R1 + 31 * 1024 + 64, (128, 16), F32)
        egad = carve(R1 + 31 * 1024 + 128, (128, 16), F32)
        prevGb = carve(R1 + 31 * 1024 + 192, (128, 16), F32)

        ops = {"sync": [], "gpsimd": [], "tensor": [], "scalar": [], "vector": []}
        sy_, g_, t_, s_, v_ = (ops["sync"], ops["gpsimd"], ops["tensor"],
                               ops["scalar"], ops["vector"])
        bankF = {}

        def bf2(b):
            return bankF.get(b, (None, 0))

        def tw2(ename, lst, *gates):
            for sem, val in gates:
                if sem is not None:
                    lst.append(lambda e, s=sem, v=val: kb.wait(ename, e, s, v))

        wv0 = sW.take(16)
        sy_.append(lambda e: e.dma_start(out=selrb[:], in_=sel_c.ap())
                   .then_inc(sW.h, 16))
        vz = sV.take()
        v_.append(lambda e: (e.memset(carry[:], 0.0),
                             e.memset(Sf[:], 0.0),
                             e.memset(prevGb[:], 0.0),
                             e.memset(Sb_[:], 0.0).then_inc(sV.h, 1)))
        t_.append(lambda e, v=wv0: kb.wait("tensor", e, sW, v))
        t_.append(lambda e, v=vz: kb.wait("tensor", e, sV, v))

        smneg = smaskT

        for ch in range(scan_chunks):
            tw2("tensor", t_, bf2(6))
            tp1 = sT.take()
            t_.append(lambda e, ch=ch: e.transpose(
                psA[0:16, 6, 0:128], logftok[:, ch, :], ident_f[:])
                .then_inc(sT.h, 1))
            tw2("vector", v_, (sT, sT.n), (sT, tp1))
            vc1 = sV.take()
            v_.append(lambda e: e.tensor_copy(out=lfrow[:], in_=psA[0:16, 6, 0:128])
                      .then_inc(sV.h, 1))
            tw2("vector", v_, (sV, vc1))
            vc2 = sV.take()
            v_.append(lambda e: e.tensor_tensor_scan(
                out=Lgrow[:], data0=lfrow[:], data1=cst[0:16, 2:3]
                .broadcast_to([16, 128]), initial=carry[:, 0:1],
                op0=ALU.add, op1=ALU.add).then_inc(sV.h, 1))
            tw2("vector", v_, (sV, vc2))
            vc3 = sV.take()
            v_.append(lambda e: e.tensor_copy(out=carry[:, 0:1],
                                              in_=Lgrow[:, 127:128])
                      .then_inc(sV.h, 1))
            tw2("tensor", t_, (sV, vc2), bf2(7))
            tp2 = sT.take()
            t_.append(lambda e: e.transpose(
                psA[0:128, 7, 0:16], Lgrow[:], ident_f[0:16, 0:16])
                .then_inc(sT.h, 1))
            tw2("vector", v_, (sT, tp2))
            vc4 = sV.take()
            v_.append(lambda e: e.tensor_copy(out=Lgtok_[:],
                                              in_=psA[0:128, 7, 0:16])
                      .then_inc(sV.h, 1))
            tw2("scalar", s_, (sV, vc4))
            sp1 = sS.take()
            s_.append(lambda e, ch=ch: e.activation(
                out=gae[:], in_=logftok[:, ch, :], func=AF.Exp)
                .then_inc(sS.h, 1))
            tw2("vector", v_, (sS, sp1))
            vc5 = sV.take()
            v_.append(lambda e, ch=ch: e.tensor_tensor(
                out=bpos[:], in0=gammatok[:, ch, :], in1=gae[:],
                op=ALU.mult).then_inc(sV.h, 1))
            tw2("tensor", t_, (sV, vc5), bf2(6))
            tp3 = sT.take()
            t_.append(lambda e: e.transpose(
                psA[0:16, 6, 0:128], bpos[:], ident_f[:]).then_inc(sT.h, 1))
            tw2("vector", v_, (sT, tp3))
            vc6 = sV.take()
            v_.append(lambda e: e.tensor_copy(out=brow[:],
                                              in_=psA[0:16, 6, 0:128])
                      .then_inc(sV.h, 1))
            bankF[6] = (sV, vc6)
            tw2("tensor", t_, (sV, vc4), bf2(7))
            tp4 = sT.take()
            t_.append(lambda e: e.matmul(
                psA[0:128, 7, 0:16], e127s[:], Lgtok_[:],
                start=True, stop=True).then_inc(sT.h, 1))
            tw2("vector", v_, (sT, tp4))
            vc7 = sV.take(5)
            v_.append(lambda e, ch=ch: (
                e.tensor_tensor(out=cscr[:], in0=psA[0:128, 7, 0:16],
                                in1=Lgtok_[:], op=ALU.subtract),
                e.tensor_tensor(out=lamd[:], in0=Lgtok_[:],
                                in1=prevGb[:], op=ALU.subtract),
                e.tensor_tensor(out=egad[:], in0=psA[0:128, 7, 0:16],
                                in1=prevGb[:], op=ALU.subtract),
                e.tensor_copy(out=prevGb[:], in_=psA[0:128, 7, 0:16]),
                e.tensor_tensor(out=cpos[:], in0=logftok[:, ch, :],
                                in1=lamd[:], op=ALU.add).then_inc(sV.h, 5)))
            tw2("scalar", s_, (sV, vc7))
            sp2 = sS.take(4)
            s_.append(lambda e: (
                e.activation(out=kfac[:], in_=cscr[:], func=AF.Exp),
                e.activation(out=eGa[:], in_=egad[:], func=AF.Exp),
                e.activation(out=Lam[:], in_=lamd[:], func=AF.Exp),
                e.activation(out=cscr[:], in_=cpos[:], func=AF.Exp)
                .then_inc(sS.h, 4)))
            bankF[7] = (sS, sp2)
            tw2("vector", v_, (sS, sp2))
            vc8 = sV.take()
            v_.append(lambda e, ch=ch: e.tensor_tensor(
                out=cpos[:], in0=gammatok[:, ch, :], in1=cscr[:],
                op=ALU.mult).then_inc(sV.h, 1))
            tw2("tensor", t_, bf2(6), bf2(7))
            tp5 = sT.take()
            for dc in range(8):
                bb = 6 + dc // 4
                pv = psA[:, bb, (dc % 4) * 64:(dc % 4) * 64 + 64].bitcast(BF16)
                t_.append(lambda e, pv=pv, dc=dc, ch=ch, last=(dc == 7): (
                    e.transpose(pv, ktok[:, ch, dc * 128:(dc + 1) * 128],
                                ident_b[:]).then_inc(sT.h, 1) if last else
                    e.transpose(pv, ktok[:, ch, dc * 128:(dc + 1) * 128],
                                ident_b[:])))
            tw2("vector", v_, (sT, tp5))
            vc9 = sV.take()
            v_.append(lambda e: (
                e.tensor_copy(out=kfeat[:, 0:4, :],
                              in_=psA[:, 6, 0:256].bitcast(BF16)
                              .rearrange("p (a b) -> p a b", a=4)),
                e.tensor_copy(out=kfeat[:, 4:8, :],
                              in_=psA[:, 7, 0:256].bitcast(BF16)
                              .rearrange("p (a b) -> p a b", a=4))
                .then_inc(sV.h, 1)))
            bankF[6] = (sV, vc9)
            bankF[7] = (sV, vc9)
            pre_done_v = sV.n
            pre_done_s = sS.n

            for sb in range(scan_sbs):
                par, grp = sb // 2, sb % 2
                heads = [2 * (grp * 4 + j) + par for j in range(4)]

                def pcols(buf, par=par, grp=grp):
                    return buf.rearrange("p (a two) -> p a two", two=2)[
                        :, grp * 4:(grp + 1) * 4, par]

                def ptok(buf, ch, par=par, grp=grp):
                    return buf[:, ch, :].rearrange(
                        "p (a two c) -> p a two c", two=2, c=64)[
                        :, grp * 4:(grp + 1) * 4, par, :]
                hsl = lambda h: (slice((h % 2) * 64, (h % 2) * 64 + 64), h // 2)
                tw2("tensor", t_, bf2(0), bf2(1), (sV, pre_done_v))
                tck = sT.take(2)
                for hi, h in enumerate(heads):
                    sub, hc = (h % 2) * 64, h // 2
                    t_.append(lambda e, hi=hi, sub=sub, hc=hc, ch=ch,
                              last=(hi == 3): (
                        e.matmul(psA[:, 0, hi * 128:(hi + 1) * 128],
                                 kfeat[sub:sub + 64, hc, :],
                                 kfeat[sub:sub + 64, hc, :],
                                 start=True, stop=True).then_inc(sT.h, 1)
                        if last else
                        e.matmul(psA[:, 0, hi * 128:(hi + 1) * 128],
                                 kfeat[sub:sub + 64, hc, :],
                                 kfeat[sub:sub + 64, hc, :],
                                 start=True, stop=True)))
                for hi, h in enumerate(heads):
                    sub, hc = (h % 2) * 64, h // 2
                    t_.append(lambda e, hi=hi, sub=sub, hc=hc, ch=ch,
                              last=(hi == 3): (
                        e.matmul(psA[:, 1, hi * 128:(hi + 1) * 128],
                                 kfeat[sub:sub + 64, hc, :],
                                 qTf[sub:sub + 64, hc,
                                     ch * 128:(ch + 1) * 128],
                                 start=True, stop=True).then_inc(sT.h, 1)
                        if last else
                        e.matmul(psA[:, 1, hi * 128:(hi + 1) * 128],
                                 kfeat[sub:sub + 64, hc, :],
                                 qTf[sub:sub + 64, hc,
                                     ch * 128:(ch + 1) * 128],
                                 start=True, stop=True)))
                if SS <= 1:
                    continue
                tw2("tensor", t_, bf2(2), bf2(3), (sV, vc6))
                tbb = sT.take(2)
                for hi, h in enumerate(heads):
                    t_.append(lambda e, hi=hi, h=h, last=(hi == 3): (
                        e.matmul(psA[:, 2, hi * 128:(hi + 1) * 128],
                                 selrb[:, h * 128:(h + 1) * 128], Lgrow[:],
                                 start=True, stop=True).then_inc(sT.h, 1)
                        if last else
                        e.matmul(psA[:, 2, hi * 128:(hi + 1) * 128],
                                 selrb[:, h * 128:(h + 1) * 128], Lgrow[:],
                                 start=True, stop=True)))
                for hi, h in enumerate(heads):
                    t_.append(lambda e, hi=hi, h=h, last=(hi == 3): (
                        e.matmul(psA[:, 3, hi * 128:(hi + 1) * 128],
                                 selrb[:, h * 128:(h + 1) * 128], brow[:],
                                 start=True, stop=True).then_inc(sT.h, 1)
                        if last else
                        e.matmul(psA[:, 3, hi * 128:(hi + 1) * 128],
                                 selrb[:, h * 128:(h + 1) * 128], brow[:],
                                 start=True, stop=True)))
                if SS <= 2:
                    continue
                tw2("vector", v_, (sT, tbb))
                vd1 = sV.take(3)
                v_.append(lambda e, lgv=pcols(Lgtok_): (
                    e.tensor_tensor(
                        out=diffb[:], in0=psA[:, 2, :].rearrange(
                            "p (a b) -> p a b", a=4),
                        in1=lgv.unsqueeze(2)
                        .broadcast_to([128, 4, 128]), op=ALU.subtract),
                    e.tensor_tensor(
                        out=m1b[:], in0=psA[:, 0, :].rearrange(
                            "p (a b) -> p a b", a=4),
                        in1=smneg[:, 0:128].unsqueeze(1)
                        .broadcast_to([128, 4, 128]), op=ALU.mult),
                    e.tensor_tensor(
                        out=m2db[:], in0=psA[:, 0, :].rearrange(
                            "p (a b) -> p a b", a=4),
                        in1=smneg[:, 128:256].unsqueeze(1)
                        .broadcast_to([128, 4, 128]), op=ALU.mult)
                    .then_inc(sV.h, 3)))
                bankF[0] = (sV, vd1)
                vd2 = sV.take(3)
                v_.append(lambda e: (
                    e.tensor_tensor(
                        out=AqTb[:], in0=psA[:, 1, :].rearrange(
                            "p (a b) -> p a b", a=4),
                        in1=smneg[:, 256:384].unsqueeze(1)
                        .broadcast_to([128, 4, 128]), op=ALU.mult),
                    e.tensor_scalar(out=dmaxb[:], in0=diffb[:], scalar1=0.0,
                                    scalar2=None, op0=ALU.max),
                    e.tensor_scalar(out=diffb[:], in0=diffb[:], scalar1=0.0,
                                    scalar2=None, op0=ALU.min)
                    .then_inc(sV.h, 3)))
                bankF[1] = (sV, vd2)
                bankF[2] = (sV, vd2)
                tw2("scalar", s_, (sV, vd2))
                se1 = sS.take(2)
                s_.append(lambda e: (
                    e.activation(out=Db[:], in_=dmaxb[:], func=AF.Exp,
                                 scale=-1.0),
                    e.activation(out=Dtb[:], in_=diffb[:], func=AF.Exp)
                    .then_inc(sS.h, 2)))
                tw2("vector", v_, (sS, se1))
                vd3 = sV.take(2)
                v_.append(lambda e: (
                    e.tensor_tensor(out=m1b[:], in0=m1b[:], in1=Db[:],
                                    op=ALU.mult),
                    e.tensor_tensor(out=m2db[:], in0=m2db[:], in1=Dtb[:],
                                    op=ALU.mult).then_inc(sV.h, 2)))
                tw2("vector", v_, (sV, vd3))
                vd4 = sV.take(3)
                v_.append(lambda e, bpv=pcols(bpos): (
                    e.tensor_tensor(
                        out=Tb[:], in0=m1b[:],
                        in1=bpv.unsqueeze(2)
                        .broadcast_to([128, 4, 128]), op=ALU.mult),
                    e.tensor_tensor(out=AqTb[:], in0=AqTb[:], in1=Dtb[:],
                                    op=ALU.mult),
                    e.tensor_tensor(
                        out=Ttb[:], in0=m2db[:],
                        in1=psA[:, 3, :].rearrange("p (a b) -> p a b", a=4),
                        op=ALU.mult).then_inc(sV.h, 3)))
                bankF[3] = (sV, vd4)
                if SS <= 3:
                    continue
                tw2("tensor", t_, bf2(4), (sS, pre_done_s), (sV, pre_done_v),
                    bf2(("sb", sb)))
                tks = sT.take()
                for hi, h in enumerate(heads):
                    sub, hc = (h % 2) * 64, h // 2
                    t_.append(lambda e, hi=hi, h=h, sub=sub, hc=hc,
                              last=(hi == 3): (
                        e.matmul(psA[:, 4, hi * 64:(hi + 1) * 64],
                                 kfeat[sub:sub + 64, hc, :],
                                 Sb_[sub:sub + 64, hc, :],
                                 start=True, stop=True).then_inc(sT.h, 1)
                        if last else
                        e.matmul(psA[:, 4, hi * 64:(hi + 1) * 64],
                                 kfeat[sub:sub + 64, hc, :],
                                 Sb_[sub:sub + 64, hc, :],
                                 start=True, stop=True)))
                tw2("vector", v_, (sT, tks), (sV, vd4))
                vb1 = sV.take(2)
                v_.append(lambda e, cpv=pcols(cpos), vtv=ptok(vtok, ch): (
                    e.tensor_tensor(
                        out=t5b[:], in0=psA[:, 4, 0:256].rearrange(
                            "p (a b) -> p a b", a=4),
                        in1=cpv.unsqueeze(2)
                        .broadcast_to([128, 4, 64]), op=ALU.mult),
                    e.tensor_tensor(
                        out=Uf[:], in0=vtv,
                        in1=t5b[:], op=ALU.subtract).then_inc(sV.h, 2)))
                bankF[4] = (sV, vb1)
                tw2("scalar", s_, (sV, vb1))
                sb0 = sS.take()
                s_.append(lambda e: e.copy(out=Ub[:], in_=Uf[:])
                          .then_inc(sS.h, 1))
                vb1 = (sV, vb1)
                if SS <= 4:
                    continue
                last_apply = (sS, sb0)
                for j in range(7):
                    tw2("tensor", t_, last_apply, bf2(5), bf2(6))
                    ta = sT.take()
                    for hi in range(4):
                        t_.append(lambda e, hi=hi, last=(hi == 3): (
                            e.matmul(psA[:, 5, hi * 64:(hi + 1) * 64],
                                     Ttb[:, hi, :], Ub[:, hi, :],
                                     start=True, stop=True).then_inc(sT.h, 1)
                            if last else
                            e.matmul(psA[:, 5, hi * 64:(hi + 1) * 64],
                                     Ttb[:, hi, :], Ub[:, hi, :],
                                     start=True, stop=True)))
                    tw2("vector", v_, (sT, ta))
                    vu = sV.take()
                    v_.append(lambda e: e.tensor_tensor(
                        out=Uf[:], in0=Uf[:],
                        in1=psA[:, 5, 0:256].rearrange("p (a b) -> p a b", a=4),
                        op=ALU.add).then_inc(sV.h, 1))
                    bankF[5] = (sV, vu)
                    tw2("scalar", s_, (sV, vu))
                    su = sS.take()
                    s_.append(lambda e: e.copy(out=Ub[:], in_=Uf[:])
                              .then_inc(sS.h, 1))
                    last_apply = (sS, su)
                    if j < 6:
                        tw2("tensor", t_, bf2(6), bf2(7))
                        tsq = sT.take(2)
                        for hi in range(4):
                            t_.append(lambda e, hi=hi, last=(hi == 3): (
                                e.matmul(psA[:, 6, hi * 128:(hi + 1) * 128],
                                         Ttb[:, hi, :], Tb[:, hi, :],
                                         start=True, stop=True)
                                .then_inc(sT.h, 1) if last else
                                e.matmul(psA[:, 6, hi * 128:(hi + 1) * 128],
                                         Ttb[:, hi, :], Tb[:, hi, :],
                                         start=True, stop=True)))
                        for hi in range(4):
                            t_.append(lambda e, hi=hi, last=(hi == 3): (
                                e.matmul(psA[:, 7, hi * 128:(hi + 1) * 128],
                                         Tb[:, hi, :], Ttb[:, hi, :],
                                         start=True, stop=True)
                                .then_inc(sT.h, 1) if last else
                                e.matmul(psA[:, 7, hi * 128:(hi + 1) * 128],
                                         Tb[:, hi, :], Ttb[:, hi, :],
                                         start=True, stop=True)))
                        tw2("vector", v_, (sT, tsq))
                        vsq = sV.take(2)
                        v_.append(lambda e: (
                            e.tensor_copy(out=Tb[:],
                                          in_=psA[:, 6, :].rearrange(
                                              "p (a b) -> p a b", a=4)),
                            e.tensor_copy(out=Ttb[:],
                                          in_=psA[:, 7, :].rearrange(
                                              "p (a b) -> p a b", a=4))
                            .then_inc(sV.h, 2)))
                        bankF[6] = (sV, vsq)
                        bankF[7] = (sV, vsq)
                if SS <= 5:
                    continue
                tw2("tensor", t_, bf2(4), bf2(("sb", sb)))
                tq0 = sT.take()
                for hi, h in enumerate(heads):
                    sub, hc = (h % 2) * 64, h // 2
                    t_.append(lambda e, hi=hi, h=h, sub=sub, hc=hc, ch=ch,
                              last=(hi == 3): (
                        e.matmul(psA[:, 4, hi * 64:(hi + 1) * 64],
                                 qTf[sub:sub + 64, hc,
                                     ch * 128:(ch + 1) * 128],
                                 Sb_[sub:sub + 64, hc, :],
                                 start=True, stop=True).then_inc(sT.h, 1)
                        if last else
                        e.matmul(psA[:, 4, hi * 64:(hi + 1) * 64],
                                 qTf[sub:sub + 64, hc,
                                     ch * 128:(ch + 1) * 128],
                                 Sb_[sub:sub + 64, hc, :],
                                 start=True, stop=True)))
                tw2("tensor", t_, bf2(5), last_apply)
                tq1 = sT.take()
                for hi in range(4):
                    t_.append(lambda e, hi=hi, last=(hi == 3): (
                        e.matmul(psA[:, 5, hi * 64:(hi + 1) * 64],
                                 AqTb[:, hi, :], Ub[:, hi, :],
                                 start=True, stop=True).then_inc(sT.h, 1)
                        if last else
                        e.matmul(psA[:, 5, hi * 64:(hi + 1) * 64],
                                 AqTb[:, hi, :], Ub[:, hi, :],
                                 start=True, stop=True)))
                tw2("vector", v_, (sT, tq1))
                vo1 = sV.take(2)
                v_.append(lambda e, lmv=pcols(Lam), vtv=ptok(vtok, ch): (
                    e.tensor_tensor(
                        out=t6b[:], in0=psA[:, 4, 0:256].rearrange(
                            "p (a b) -> p a b", a=4),
                        in1=lmv.unsqueeze(2)
                        .broadcast_to([128, 4, 64]), op=ALU.mult),
                    e.tensor_tensor(
                        out=vtv, in0=t6b[:],
                        in1=psA[:, 5, 0:256].rearrange("p (a b) -> p a b", a=4),
                        op=ALU.add).then_inc(sV.h, 2)))
                bankF[5] = (sV, vo1)
                if SS <= 6:
                    continue
                tw2("vector", v_, (sV, vo1))
                vk1 = sV.take()
                v_.append(lambda e, ktv=ptok(ktok, ch), kfv=pcols(kfac):
                          e.tensor_tensor(
                    out=kpb[:], in0=ktv,
                    in1=kfv.unsqueeze(2)
                    .broadcast_to([128, 4, 64]), op=ALU.mult)
                    .then_inc(sV.h, 1))
                tw2("tensor", t_, (sV, vk1), bf2(4))
                tsu = sT.take()
                for hi in range(4):
                    t_.append(lambda e, hi=hi, last=(hi == 3): (
                        e.matmul(psA[0:64, 4, hi * 64:(hi + 1) * 64],
                                 kpb[:, hi, :], Ub[:, hi, :],
                                 start=True, stop=True).then_inc(sT.h, 1)
                        if last else
                        e.matmul(psA[0:64, 4, hi * 64:(hi + 1) * 64],
                                 kpb[:, hi, :], Ub[:, hi, :],
                                 start=True, stop=True)))
                tw2("vector", v_, (sT, tsu))
                vsu = sV.take(4)
                for hi, h in enumerate(heads):
                    v_.append(lambda e, hi=hi, h=h, last=(hi == 3): (
                        e.scalar_tensor_tensor(
                            out=Sf[:, h, :], in0=Sf[:, h, :],
                            scalar=eGa[0:64, h:h + 1],
                            in1=psA[0:64, 4, hi * 64:(hi + 1) * 64],
                            op0=ALU.mult, op1=ALU.add).then_inc(sV.h, 4)
                        if last else
                        e.scalar_tensor_tensor(
                            out=Sf[:, h, :], in0=Sf[:, h, :],
                            scalar=eGa[0:64, h:h + 1],
                            in1=psA[0:64, 4, hi * 64:(hi + 1) * 64],
                            op0=ALU.mult, op1=ALU.add)))
                bankF[4] = (sV, vsu)
                tw2("vector", v_, (sV, vsu))
                ssb = sV.take(4)
                for hi, h in enumerate(heads):
                    sub, hc = (h % 2) * 64, h // 2
                    v_.append(lambda e, h=h, sub=sub, hc=hc,
                              last=(hi == 3), hi=hi: (
                        e.tensor_copy(out=Sb_[sub:sub + 64, hc, :],
                                      in_=Sf[0:64, h, :]).then_inc(sV.h, 4)
                        if last else
                        e.tensor_copy(out=Sb_[sub:sub + 64, hc, :],
                                      in_=Sf[0:64, h, :])))
                bankF[("sb", sb)] = (sV, ssb)
        kb.block("l1_scan", ops)
        dbg_dump("otok", vtok[:], (128, NT, D), BF16)
        for nm, bufv, shp, dt_ in (
                ("sc_Lgtok", Lgtok_, (128, 16), F32),
                ("sc_Lgrow", Lgrow, (16, 128), F32),
                ("sc_bpos", bpos, (128, 16), F32),
                ("sc_cpos", cpos, (128, 16), F32),
                ("sc_kfac", kfac, (128, 16), F32),
                ("sc_eGa", eGa, (128, 16), F32),
                ("sc_Lam", Lam, (128, 16), F32),
                ("sc_Db", Db, (128, 4, 128), BF16),
                ("sc_Dtb", Dtb, (128, 4, 128), BF16),
                ("sc_Tb", Tb, (128, 4, 128), BF16),
                ("sc_Ttb", Ttb, (128, 4, 128), BF16),
                ("sc_AqTb", AqTb, (128, 4, 128), BF16),
                ("sc_Ub", Ub, (128, 4, 64), BF16),
                ("sc_kfeat", kfeat, (128, 8, 128), BF16),
                ("sc_Sf", Sf, (64, 16, 64), F32),
                ("sc_brow", brow, (16, 128), F32)):
            dbg_dump(nm, bufv[:] if not isinstance(bufv, type(Lgtok_)) else bufv, shp, dt_)

    if stages >= 9:
        wg2b = carve(R2, (64, D), BF16)
        gscr = carve(R2 + 2 * 1024, (128, 2, D), BF16)
        normb = carve(R2 + 6 * 1024, (128, D), BF16)
        nwrow = carve(R2 + 8 * 1024, (1, D), F32)
        wob = carve(R2 + 12 * 1024, (128, 2, 8, 512), BF16)
        sT_f = carve(R3, (64, T), BF16)
        gTb = carve(R3 + 4 * 1024, (64, T), BF16)
        gTq4 = carve(R3 + 8 * 1024, (64, 4, 512), F32)
        hscr = carve(R3 + 16 * 1024, (64, 2, 4, 512), BF16)
        w1c = carve(R3 + 24 * 1024, (64, 16), F32)
        b1c = carve(R3 + 24 * 1024 + 64, (64, 16), F32)
        w2c = carve(R3 + 24 * 1024 + 128, (64, 16), F32)
        b2c = carve(R3 + 24 * 1024 + 192, (64, 1), F32)
        woutb = carve(R3 + 25 * 1024, (64, D), BF16)
        boutb = carve(R3 + 27 * 1024, (1, D), BF16)

        ops = {"gpsimd": [], "tensor": [], "scalar": [], "vector": []}
        g_, t_, s_, v_ = (ops["gpsimd"], ops["tensor"], ops["scalar"],
                          ops["vector"])
        gw = sG.take(16)
        g_.append(lambda e: e.dma_start(out=wg2b[:], in_=Wg2)
                  .then_inc(sG.h, 16))
        t_.append(lambda e, v=gw: kb.wait("tensor", e, sG, v))
        bgate = {}
        for tt in range(NT):
            b0 = 2 * (tt % 2)
            if b0 in bgate:
                t_.append(lambda e, v=bgate[b0]: kb.wait("tensor", e, sS, v))
            tn = sT.take(2)
            for half in range(2):
                t_.append(lambda e, tt=tt, b0=b0, half=half: e.matmul(
                    psA[:, b0 + half, :], t1s[:, tt * 128:(tt + 1) * 128],
                    wg2b[:, half * 512:(half + 1) * 512],
                    start=True, stop=True).then_inc(sT.h, 1))
            s_.append(lambda e, v=tn: kb.wait("scalar", e, sT, v))
            sn = sS.take(2)
            for half in range(2):
                s_.append(lambda e, tt=tt, b0=b0, half=half: e.activation(
                    out=gscr[:, tt % 2, half * 512:(half + 1) * 512],
                    in_=psA[:, b0 + half, :], func=AF.Sigmoid)
                    .then_inc(sS.h, 1))
            bgate[b0] = sn
            v_.append(lambda e, v=sn: kb.wait("vector", e, sS, v))
            vn = sV.take()
            v_.append(lambda e, tt=tt: e.tensor_tensor(
                out=vtok[:, tt, :], in0=vtok[:, tt, :],
                in1=gscr[:, tt % 2, :], op=ALU.mult).then_inc(sV.h, 1))
            s_.append(lambda e, v=vn: kb.wait("scalar", e, sV, v))
        kb.block("l1_gate", ops)
        dbg_dump("ogtok", vtok[:], (128, NT, D), BF16)

        ops = {"sync": [], "tensor": [], "vector": []}
        sy_, t_, v_ = ops["sync"], ops["tensor"], ops["vector"]
        nw = sW.take(16)
        sy_.append(lambda e: e.dma_start(out=nwrow[0:1, :], in_=norm_w[None, :])
                   .then_inc(sW.h, 16))
        t_.append(lambda e, v=nw: kb.wait("tensor", e, sW, v))
        tn = sT.take(2)
        for half in range(2):
            t_.append(lambda e, half=half: e.matmul(
                psA[:, half, :], onesf[0:1, 0:128],
                nwrow[0:1, half * 512:(half + 1) * 512],
                start=True, stop=True).then_inc(sT.h, 1))
        v_.append(lambda e, v=tn: kb.wait("vector", e, sT, v))
        sV.take()
        v_.append(lambda e: e.tensor_copy(
            out=normb[:], in_=psA[:, 0:2, :].rearrange("p a b -> p (a b)"))
            .then_inc(sV.h, 1))
        kb.block("l1_nw", ops)
        ln_pass("l1_on", lambda t: vtok[:, t, :], lambda t: vtok[:, t, :],
                gb=(normb[:], None))
        dbg_dump("ogntok", vtok[:], (128, NT, D), BF16)

        ognT = carve(R1, (128, NC, T), BF16)
        transpose_tok_to_feat("l1_wo_tr", vtok, ognT)
        ops = {"gpsimd": [], "tensor": [], "vector": []}
        g_, t_, v_ = ops["gpsimd"], ops["tensor"], ops["vector"]
        half_used = {0: (None, 0), 1: (None, 0)}
        bwo = {}
        for dq in range(2):
            hw = dq % 2
            if half_used[hw][0] is not None:
                g_.append(lambda e, v=half_used[hw][1]:
                          kb.wait("gpsimd", e, sT, v))
            gv = sG.take(16)
            g_.append(lambda e, dq=dq, hw=hw: e.dma_start(
                out=wob[:, hw, :, :],
                in_=Wo1.rearrange("(c p) n -> p c n", p=128)
                [:, :, dq * 512:(dq + 1) * 512]).then_inc(sG.h, 16))
            t_.append(lambda e, v=gv: kb.wait("tensor", e, sG, v))
            for tt in range(NT):
                b0 = 4 + tt % 4
                if b0 in bwo:
                    t_.append(lambda e, v=bwo[b0]: kb.wait("tensor", e, sV, v))
                tn = sT.take()
                for c in range(NC):
                    t_.append(lambda e, c=c, tt=tt, b0=b0, hw=hw,
                              last=(c == NC - 1): (
                        e.matmul(psA[:, b0, :],
                                 ognT[:, c, tt * 128:(tt + 1) * 128],
                                 wob[:, hw, c, :], start=(c == 0), stop=last)
                        .then_inc(sT.h, 1) if last else
                        e.matmul(psA[:, b0, :],
                                 ognT[:, c, tt * 128:(tt + 1) * 128],
                                 wob[:, hw, c, :], start=(c == 0), stop=False)))
                v_.append(lambda e, v=tn: kb.wait("vector", e, sT, v))
                vn = sV.take()
                v_.append(lambda e, tt=tt, dq=dq, b0=b0: e.tensor_tensor(
                    out=acc[:, tt, dq * 512:(dq + 1) * 512],
                    in0=acc[:, tt, dq * 512:(dq + 1) * 512],
                    in1=psA[:, b0, :], op=ALU.add).then_inc(sV.h, 1))
                bwo[b0] = vn
            half_used[hw] = (sT, sT.n)
        kb.block("l1_wo", ops)

        ops = {"sync": [], "gpsimd": [], "tensor": [], "scalar": [],
               "vector": []}
        sy_, g_, t_, s_, v_ = (ops["sync"], ops["gpsimd"], ops["tensor"],
                               ops["scalar"], ops["vector"])
        fw = sW.take(16 * 4)
        sy_.append(lambda e: (
            e.dma_start(out=w1c[:], in_=W1f).then_inc(sW.h, 16),
            e.dma_start(out=b1c[:], in_=b1f).then_inc(sW.h, 16),
            e.dma_start(out=w2c[:], in_=W2f).then_inc(sW.h, 16),
            e.dma_start(out=b2c[:], in_=b2f[:, None]).then_inc(sW.h, 16)))
        gw2 = sG.take(16 * 2)
        g_.append(lambda e: (
            e.dma_start(out=woutb[:], in_=Wout).then_inc(sG.h, 16),
            e.dma_start(out=boutb[0:1, :], in_=bout[None, :])
            .then_inc(sG.h, 16)))
        for tg in range(2):
            bank = tg
            tn = sT.take()
            for k8 in range(8):
                tt = tg * 8 + k8
                pv = psA[:, bank, :].bitcast(BF16)[0:64, k8 * 128:(k8 + 1) * 128]
                t_.append(lambda e, pv=pv, tt=tt, last=(k8 == 7): (
                    e.transpose(pv, sffn[:, tt, :], ident_b[:])
                    .then_inc(sT.h, 1) if last else
                    e.transpose(pv, sffn[:, tt, :], ident_b[:])))
            v_.append(lambda e, v=tn: kb.wait("vector", e, sT, v))
            vn = sV.take()
            v_.append(lambda e, tg=tg, bank=bank: e.tensor_copy(
                out=sT_f[:, tg * 1024:(tg + 1) * 1024],
                in_=psA[:, bank, :].bitcast(BF16)[0:64, :])
                .then_inc(sV.h, 1))
            bwo[bank] = vn
        vi = sV.take()
        v_.append(lambda e, v=fw: kb.wait("vector", e, sW, v))
        v_.append(lambda e: (
            e.tensor_copy(out=gTq4[:, 0, :],
                          in_=b2c[:, 0:1].broadcast_to([64, 512])),
            e.tensor_copy(out=gTq4[:, 1, :],
                          in_=b2c[:, 0:1].broadcast_to([64, 512])),
            e.tensor_copy(out=gTq4[:, 2, :],
                          in_=b2c[:, 0:1].broadcast_to([64, 512])),
            e.tensor_copy(out=gTq4[:, 3, :],
                          in_=b2c[:, 0:1].broadcast_to([64, 512]))
            .then_inc(sV.h, 1)))
        s_.append(lambda e, v=fw: kb.wait("scalar", e, sW, v))
        s_.append(lambda e, v=sV.n: kb.wait("scalar", e, sV, v))
        v_.append(lambda e, v=vi: kb.wait("vector", e, sV, v))
        kchain = {}
        for kk in range(16):
            if (kk - 2) in kchain:
                s_.append(lambda e, v=kchain[kk - 2]:
                          kb.wait("scalar", e, sV, v))
            sk = sS.take()
            for tq in range(4):
                s_.append(lambda e, kk=kk, tq=tq, last=(tq == 3): (
                    e.activation(out=hscr[:, kk % 2, tq, :],
                                 in_=sT_f[:, tq * 512:(tq + 1) * 512],
                                 func=AF.Relu, scale=w1c[:, kk:kk + 1],
                                 bias=b1c[:, kk:kk + 1]).then_inc(sS.h, 1)
                    if last else
                    e.activation(out=hscr[:, kk % 2, tq, :],
                                 in_=sT_f[:, tq * 512:(tq + 1) * 512],
                                 func=AF.Relu, scale=w1c[:, kk:kk + 1],
                                 bias=b1c[:, kk:kk + 1])))
            v_.append(lambda e, v=sk: kb.wait("vector", e, sS, v))
            vk = sV.take()
            for tq in range(4):
                v_.append(lambda e, kk=kk, tq=tq, last=(tq == 3): (
                    e.scalar_tensor_tensor(
                        out=gTq4[:, tq, :], in0=hscr[:, kk % 2, tq, :],
                        scalar=w2c[:, kk:kk + 1], in1=gTq4[:, tq, :],
                        op0=ALU.mult, op1=ALU.add).then_inc(sV.h, 1)
                    if last else
                    e.scalar_tensor_tensor(
                        out=gTq4[:, tq, :], in0=hscr[:, kk % 2, tq, :],
                        scalar=w2c[:, kk:kk + 1], in1=gTq4[:, tq, :],
                        op0=ALU.mult, op1=ALU.add)))
            kchain[kk] = vk
        v_.append(lambda e, v=kchain[15]: kb.wait("vector", e, sV, v))
        vg = sV.take()
        for tq in range(4):
            v_.append(lambda e, tq=tq, last=(tq == 3): (
                e.tensor_copy(out=gTb[:, tq * 512:(tq + 1) * 512],
                              in_=gTq4[:, tq, :]).then_inc(sV.h, 1)
                if last else
                e.tensor_copy(out=gTb[:, tq * 512:(tq + 1) * 512],
                              in_=gTq4[:, tq, :])))
        t_.append(lambda e, v=vg: kb.wait("tensor", e, sV, v))
        t_.append(lambda e, v=gw2: kb.wait("tensor", e, sG, v))
        for tt in range(NT):
            for half in range(2):
                b0 = (2 * tt + half) % 4
                if b0 in bwo:
                    t_.append(lambda e, v=bwo[b0]: kb.wait("tensor", e, sV, v))
                tn = sT.take()
                t_.append(lambda e, tt=tt, half=half, b0=b0: (
                    e.matmul(psA[:, b0, :], gTb[:, tt * 128:(tt + 1) * 128],
                             woutb[:, half * 512:(half + 1) * 512],
                             start=True, stop=False),
                    e.matmul(psA[:, b0, :], onesb[0:1, :],
                             boutb[0:1, half * 512:(half + 1) * 512],
                             start=False, stop=True).then_inc(sT.h, 1)))
                v_.append(lambda e, v=tn: kb.wait("vector", e, sT, v))
                vn = sV.take()
                v_.append(lambda e, tt=tt, half=half, b0=b0: e.tensor_tensor(
                    out=acc[:, tt, half * 512:(half + 1) * 512],
                    in0=acc[:, tt, half * 512:(half + 1) * 512],
                    in1=psA[:, b0, :], op=ALU.add).then_inc(sV.h, 1))
                bwo[b0] = vn
        kb.block("l1_ffn", ops)
        dbg_dump("acc_final", acc[:], (128, NT, D), F32)

    fv = sW.take(16)
    fin_ops = {"sync": [
        lambda e: e.dma_start(out=out.rearrange("(t p) d -> p t d", p=128),
                              in_=acc[:]).then_inc(sW.h, 16),
        lambda e, v=fv: kb.wait("sync", e, sW, v)]}
    kb.block("fin", fin_ops)

    nc._kb = kb
    return nc, kb


def kernel(**inputs):
    nc, kb = build_nc()
    xs = np.ascontiguousarray(inputs["L0_attn_x"])
    base = {
        "W_qkv": inputs["L0_attn_W_qkv"], "b_qkv": inputs["L0_attn_b_qkv"],
        "A_qkv": inputs["L0_attn_A_qkv"], "B_qkv": inputs["L0_attn_B_qkv"],
        "W_proj": inputs["L0_attn_W_proj"], "b_proj": inputs["L0_attn_b_proj"],
        "A_proj": inputs["L0_attn_A_proj"], "B_proj": inputs["L0_attn_B_proj"],
        "ln1_w": inputs["L0_ffn_ln1_w"], "ln1_b": inputs["L0_ffn_ln1_b"],
        "ln2_w": inputs["L0_ffn_ln2_w"], "ln2_b": inputs["L0_ffn_ln2_b"],
        "conv_w": inputs["L0_ffn_conv_w"], "attn_w": inputs["L0_ffn_attn_w"],
        "proj_w": inputs["L0_ffn_proj_w"], "router_w": inputs["L0_ffn_router_w"],
        "expert_wv": inputs["L0_ffn_expert_wv"],
        "expert_proj": inputs["L0_ffn_expert_proj"],
        "Wq1": inputs["L1_attn_Wq"], "Wk1": inputs["L1_attn_Wk"],
        "Wv1": inputs["L1_attn_Wv"], "Wgamma": inputs["L1_attn_Wgamma"],
        "Wf": inputs["L1_attn_Wf"], "Wg1": inputs["L1_attn_Wg1"],
        "Wg2": inputs["L1_attn_Wg2"], "Wo1": inputs["L1_attn_Wo"],
        "norm_w": inputs["L1_attn_norm_w"],
        "Win": inputs["L1_ffn_Win"], "bin_": inputs["L1_ffn_bin_"],
        "W1f": inputs["L1_ffn_W1"], "b1f": inputs["L1_ffn_b1"],
        "W2f": inputs["L1_ffn_W2"], "b2f": inputs["L1_ffn_b2"],
        "Wout": inputs["L1_ffn_Wout"], "bout": inputs["L1_ffn_bout"],
    }
    base = {k: np.ascontiguousarray(v) for k, v in base.items()}
    in_maps = [dict(base, x_in=np.ascontiguousarray(xs[i])) for i in range(8)]
    res = run_bass_kernel_spmd(nc, in_maps, core_ids=list(range(8)))
    return np.stack([r["out"] for r in res.results], axis=0)


# version 07
# speedup vs baseline: 1.0212x; 1.0212x over previous
import os
import numpy as np
import ml_dtypes
import concourse.bass as bass
import concourse.mybir as mybir
from concourse.bass_utils import run_bass_kernel_spmd

F32 = mybir.dt.float32
BF16 = mybir.dt.bfloat16
U8 = mybir.dt.uint8
AF = mybir.ActivationFunctionType
ALU = mybir.AluOpType
AX = mybir.AxisListType

T = 2048
D = 1024
NT = 16
NC = 8
H = 16
HD = 64
DFF = 4096
NE = 4

ARENA_KB = 128
R0, R1, R2, R3 = 0, 32 * 1024, 64 * 1024, 96 * 1024
STAGES = 99
DEBUG = ()


class Sem:
    def __init__(self, nc, name):
        self.h = nc.alloc_semaphore(name)
        self.n = 0

    def take(self, k=1):
        self.n += k
        return self.n


class KB:

    def __init__(self, nc):
        self.nc = nc
        self.sems = {}
        self.last_wait = {}
        self.dbg = {}
        self.dram = {}

    def sem(self, name):
        if name not in self.sems:
            self.sems[name] = Sem(self.nc, name)
        return self.sems[name]

    def wait(self, ename, eng, sem, v):
        key = (ename, sem.h.name)
        if self.last_wait.get(key, -1) < v:
            eng.wait_ge(sem.h, v)
            self.last_wait[key] = v

    def block(self, name, ops):
        with self.nc.Block(name) as blk:
            for ename in ("sync", "gpsimd", "tensor", "scalar", "vector"):
                lst = ops.get(ename)
                if lst:
                    getattr(blk, ename)(lambda e, lst=lst: [f(e) for f in lst])


def bf(x):
    return np.ascontiguousarray(x.astype(ml_dtypes.bfloat16))


def build_nc(stages=STAGES, debug=(), scan_chunks=NT, scan_sbs=4):
    SS = int(os.environ.get('SSECT', 99))
    nc = bass.Bass("TRN2", target_bir_lowering=False, detect_race_conditions=False)
    kb = KB(nc)
    kb.debug_names = set(debug)

    def din(name, shape):
        kb.dram[name] = nc.dram_tensor(name, list(shape), F32,
                                       kind="ExternalInput").ap()
        return kb.dram[name]

    x_in = din("x_in", (T, D))
    W_qkv = din("W_qkv", (D, 3 * D)); b_qkv = din("b_qkv", (3 * D,))
    A_qkv = din("A_qkv", (D, 8)); B_qkv = din("B_qkv", (8, 3 * D))
    W_proj = din("W_proj", (D, D)); b_proj = din("b_proj", (D,))
    A_proj = din("A_proj", (D, 8)); B_proj = din("B_proj", (8, D))
    ln1_w = din("ln1_w", (D,)); ln1_b = din("ln1_b", (D,))
    ln2_w = din("ln2_w", (D,)); ln2_b = din("ln2_b", (D,))
    conv_w = din("conv_w", (D, 1, 3))
    attn_w = din("attn_w", (D, 3 * D)); proj_w = din("proj_w", (D, D))
    router_w = din("router_w", (D, NE))
    expert_wv = din("expert_wv", (NE, D, 2 * DFF))
    expert_proj = din("expert_proj", (NE, DFF, D))
    Wq1 = din("Wq1", (D, D)); Wk1 = din("Wk1", (D, D)); Wv1 = din("Wv1", (D, D))
    Wgamma = din("Wgamma", (D, H)); Wf = din("Wf", (D, H))
    Wg1 = din("Wg1", (D, HD)); Wg2 = din("Wg2", (HD, D))
    Wo1 = din("Wo1", (D, D)); norm_w = din("norm_w", (D,))
    Win = din("Win", (D, 64)); bin_ = din("bin_", (64,))
    W1f = din("W1f", (64, 16)); b1f = din("b1f", (64, 16))
    W2f = din("W2f", (64, 16)); b2f = din("b2f", (64,))
    Wout = din("Wout", (64, D)); bout = din("bout", (D,))

    out = nc.dram_tensor("out", [T, D], F32, kind="ExternalOutput").ap()

    ident_f_c = nc.inline_tensor(np.eye(128, dtype=np.float32), name="c_idf")
    ident_b_c = nc.inline_tensor(np.eye(128, dtype=ml_dtypes.bfloat16), name="c_idb")
    bandm = (np.arange(896)[None, :] - np.arange(128)[:, None] >= 384)
    band_c = nc.inline_tensor(bf(bandm.astype(np.float32)), name="c_band")
    j = np.arange(128)[None, :]; p = np.arange(128)[:, None]
    smask = np.concatenate([-(j < p).astype(np.float32),
                            -(j > p).astype(np.float32),
                            (j >= p).astype(np.float32)], axis=1)
    smask_c = nc.inline_tensor(bf(smask), name="c_smask")
    invf = 1.0 / (10000.0 ** (np.arange(0, HD, 2) / HD))
    tt = np.arange(T)
    ang = tt[None, :] * invf[:, None]
    cs = np.cos(ang); sn = np.sin(ang)
    cos128 = np.concatenate([cs, cs, cs, cs], axis=0)
    sin128 = np.concatenate([sn, sn, sn, sn], axis=0)
    cos_c = nc.inline_tensor(bf(cos128), name="c_cos")
    sin_c = nc.inline_tensor(bf(sin128), name="c_sin")
    Pm = np.zeros((128, 128), dtype=np.float32)
    for blk0 in (0, 64):
        for d in range(32):
            Pm[blk0 + d + 32, blk0 + d] = -1.0
            Pm[blk0 + d, blk0 + d + 32] = 1.0
    rotp_c = nc.inline_tensor(bf(Pm), name="c_rotp")
    sel = np.zeros((16, 16, 128), dtype=np.float32)
    for h in range(16):
        sel[h, h, :] = 1.0
    sel_c = nc.inline_tensor(sel.transpose(1, 0, 2).reshape(16, 16 * 128),
                             name="c_sel")
    e127 = np.zeros((128, 128), dtype=np.float32); e127[127, :] = 1.0
    e127_c = nc.inline_tensor(e127, name="c_e127")
    onesf_c = nc.inline_tensor(np.ones((128, 128), dtype=np.float32), name="c_onef")
    onesb_c = nc.inline_tensor(np.ones((128, 128), dtype=ml_dtypes.bfloat16), name="c_oneb")

    acc = nc.alloc_sbuf_tensor("acc", [128, NT, D], F32)
    ident_f = nc.alloc_sbuf_tensor("ident_f", [128, 128], F32)
    ident_b = nc.alloc_sbuf_tensor("ident_b", [128, 128], BF16)
    smaskT = nc.alloc_sbuf_tensor("smaskT", [128, 384], BF16)
    e127s = nc.alloc_sbuf_tensor("e127s", [128, 128], F32)
    onesf = nc.alloc_sbuf_tensor("onesf", [128, 128], F32)
    onesb = nc.alloc_sbuf_tensor("onesb", [128, 128], BF16)
    cst = nc.alloc_sbuf_tensor("cst", [128, 8], F32)
    lnst = nc.alloc_sbuf_tensor("lnst", [128, NT, 8], F32)
    wslab = nc.alloc_sbuf_tensor("wslab", [128, 8, 256], BF16)
    t1s = nc.alloc_sbuf_tensor("t1s", [64, T], BF16)
    logftok = nc.alloc_sbuf_tensor("logftok", [128, NT, H], F32)
    gammatok = nc.alloc_sbuf_tensor("gammatok", [128, NT, H], BF16)
    sffn = nc.alloc_sbuf_tensor("sffn", [128, NT, 64], BF16)
    arena = nc.alloc_sbuf_tensor("arena", [128, ARENA_KB * 1024], U8)
    psA = nc.alloc_psum_tensor("psA", [128, 8, 512], F32)

    kb.acc = acc
    wsl = lambda hh: wslab[:, :, hh * 128:(hh + 1) * 128]

    def carve(off_b, shape, dtype):
        parts = shape[0]
        nbytes = int(np.prod(shape[1:])) * mybir.dt.size(dtype)
        assert off_b % 64 == 0 and off_b + nbytes <= ARENA_KB * 1024, (off_b, shape)
        v = arena[0:parts, off_b:off_b + nbytes].bitcast(dtype)
        if len(shape) > 2:
            pat = "p (" + " ".join(f"d{i}" for i in range(1, len(shape))) + ") -> p " + \
                  " ".join(f"d{i}" for i in range(1, len(shape)))
            v = v.rearrange(pat, **{f"d{i}": s for i, s in enumerate(shape) if i})
        return v

    def pbank(b, rows=128, cols=512, base=0):
        return psA[0:rows, b, base:base + cols]

    def dbg_dump(name, ap, shape, dtype=F32):
        if name not in kb.debug_names:
            return
        t = nc.dram_tensor("dbg_" + name, list(shape), dtype, kind="ExternalOutput").ap()
        s = kb.sem("dbgsem")
        v = s.take(16)
        ops = {"sync": [lambda e, t=t, ap=ap, v=v: (e.dma_start(out=t, in_=ap)
                                                    .then_inc(s.h, 16),
                                                    e.wait_ge(s.h, v))]}
        kb.block("dbg_" + name, ops)

    sW = kb.sem("sW")
    sG = kb.sem("sG")
    sT = kb.sem("sT")
    sS = kb.sem("sS")
    sV = kb.sem("sV")

    def ph0():
        ops = {"sync": [], "gpsimd": []}
        o = ops["sync"]
        sW.take(16)
        o.append(lambda e: e.dma_start(
            out=acc[:], in_=x_in.rearrange("(t p) d -> p t d", p=128))
            .then_inc(sW.h, 16))
        for src, dst in ((ident_f_c, ident_f), (ident_b_c, ident_b),
                         (smask_c, smaskT), (e127_c, e127s),
                         (onesf_c, onesf), (onesb_c, onesb)):
            sW.take(16)
            o.append(lambda e, s=src, d=dst: e.dma_start(
                out=d[:], in_=s.ap()).then_inc(sW.h, 16))
        g = ops["gpsimd"]
        g.append(lambda e: e.memset(cst[:, 0:1], 1e-5))
        g.append(lambda e: e.memset(cst[:, 1:2], 1e-12))
        g.append(lambda e: e.memset(cst[:, 2:3], 0.0))
        g.append(lambda e: e.memset(cst[:, 3:4], 1.0))
        wv = sW.n
        o.append(lambda e, v=wv: kb.wait("sync", e, sW, v))
        kb.block("ph0", ops)

    def ln_pass(name, src_tile, dst_tile, gb=None, n_tiles=NT):
        st = lnst
        ops = {"vector": [], "scalar": []}
        v_, s_ = ops["vector"], ops["scalar"]
        pdump = psA[:, 6:8, :].rearrange("p a b -> p (a b)")
        C = lambda t, i: st[:, t, i:i + 1]
        vpre = sV.n
        s_.append(lambda e, v=vpre: kb.wait("scalar", e, sS if False else sV, v))
        for t in range(n_tiles):
            if t < n_tiles - 1:
                s_.append(lambda e, t=t: e.activation(
                    out=pdump, in_=src_tile(t), func=AF.Square, accum_out=C(t, 1)))
        v1 = sS.take()
        s_.append(lambda e, t=n_tiles - 1: e.activation(
            out=pdump, in_=src_tile(t), func=AF.Square,
            accum_out=C(t, 1)).then_inc(sS.h, 1))
        for t in range(n_tiles):
            v_.append(lambda e, t=t: e.reduce_sum(
                out=C(t, 0), in_=src_tile(t), axis=AX.X))
        v_.append(lambda e, v=v1: kb.wait("vector", e, sS, v))
        for t in range(n_tiles):
            v_.append(lambda e, t=t: e.tensor_scalar(
                out=C(t, 3), in0=C(t, 1), scalar1=1.0 / D, scalar2=None,
                op0=ALU.mult))
        for t in range(n_tiles):
            v_.append(lambda e, t=t: e.tensor_tensor(
                out=C(t, 4), in0=C(t, 0), in1=C(t, 0), op=ALU.mult))
        for t in range(n_tiles):
            v_.append(lambda e, t=t: e.tensor_scalar(
                out=C(t, 4), in0=C(t, 4), scalar1=1.0 / (D * D), scalar2=None,
                op0=ALU.mult))
        for t in range(n_tiles):
            v_.append(lambda e, t=t: e.tensor_tensor(
                out=C(t, 4), in0=C(t, 3), in1=C(t, 4), op=ALU.subtract))
        v2 = sV.take()
        for t in range(n_tiles):
            v_.append(lambda e, t=t, last=(t == n_tiles - 1): (
                e.tensor_scalar(out=C(t, 2), in0=C(t, 0), scalar1=1.0 / D,
                                scalar2=None, op0=ALU.mult)
                .then_inc(sV.h, 1) if last else
                e.tensor_scalar(out=C(t, 2), in0=C(t, 0), scalar1=1.0 / D,
                                scalar2=None, op0=ALU.mult)))
        s_.append(lambda e, v=v2: kb.wait("scalar", e, sV, v))
        v3 = sS.take()
        for t in range(n_tiles):
            v_dummy = None
            s_.append(lambda e, t=t, last=(t == n_tiles - 1): (
                e.activation(out=C(t, 5), in_=C(t, 4), func=AF.Sqrt,
                             scale=1.0, bias=cst[:, 0:1])
                .then_inc(sS.h, 1) if last else
                e.activation(out=C(t, 5), in_=C(t, 4), func=AF.Sqrt,
                             scale=1.0, bias=cst[:, 0:1])))
        v_.append(lambda e, v=v3: kb.wait("vector", e, sS, v))
        for t in range(n_tiles):
            v_.append(lambda e, t=t: e.reciprocal(out=C(t, 6), in_=C(t, 5)))
        for t in range(n_tiles):
            v_.append(lambda e, t=t: e.tensor_scalar(
                out=C(t, 7), in0=C(t, 2), scalar1=C(t, 6), scalar2=None,
                op0=ALU.mult))
        v4 = sV.take()
        for t in range(n_tiles):
            v_.append(lambda e, t=t, last=(t == n_tiles - 1): (
                e.tensor_scalar(out=C(t, 7), in0=C(t, 7), scalar1=-1.0,
                                scalar2=None, op0=ALU.mult)
                .then_inc(sV.h, 1) if last else
                e.tensor_scalar(out=C(t, 7), in0=C(t, 7), scalar1=-1.0,
                                scalar2=None, op0=ALU.mult)))
        s_.append(lambda e, v=v4: kb.wait("scalar", e, sV, v))
        v5 = sS.take()
        for t in range(n_tiles):
            s_.append(lambda e, t=t, last=(t == n_tiles - 1): (
                e.activation(out=dst_tile(t), in_=src_tile(t),
                             func=AF.Identity, scale=C(t, 6), bias=C(t, 7))
                .then_inc(sS.h, 1) if last else
                e.activation(out=dst_tile(t), in_=src_tile(t),
                             func=AF.Identity, scale=C(t, 6), bias=C(t, 7))))
        if gb is not None:
            gba, bba = gb
            v_.append(lambda e, v=v5: kb.wait("vector", e, sS, v))
            if bba is None:
                sV.take()
                for t in range(n_tiles):
                    v_.append(lambda e, t=t, last=(t == n_tiles - 1): (
                        e.tensor_tensor(out=dst_tile(t), in0=dst_tile(t),
                                        in1=gba, op=ALU.mult)
                        .then_inc(sV.h, 1) if last else
                        e.tensor_tensor(out=dst_tile(t), in0=dst_tile(t),
                                        in1=gba, op=ALU.mult)))
            else:
                for t in range(n_tiles):
                    v_.append(lambda e, t=t: e.tensor_tensor(
                        out=dst_tile(t), in0=dst_tile(t), in1=gba, op=ALU.mult))
                sV.take()
                for t in range(n_tiles):
                    v_.append(lambda e, t=t, last=(t == n_tiles - 1): (
                        e.tensor_tensor(out=dst_tile(t), in0=dst_tile(t), in1=bba,
                                        op=ALU.add).then_inc(sV.h, 1) if last else
                        e.tensor_tensor(out=dst_tile(t), in0=dst_tile(t), in1=bba,
                                        op=ALU.add)))
        kb.block(name, ops)

    def transpose_tok_to_feat(name, src, dst, n_tiles=NT, nchunks=NC):
        ops = {"tensor": [], "vector": []}
        t_, v_ = ops["tensor"], ops["vector"]
        ngrp = n_tiles // 8
        for c in range(nchunks):
            for tg in range(ngrp):
                bank = (c * ngrp + tg) % 2
                tn = sT.take()
                for k8 in range(8):
                    t = tg * 8 + k8
                    pv = psA[:, bank, k8 * 64:(k8 + 1) * 64].bitcast(BF16)
                    t_.append(lambda e, pv=pv, src=src, t=t, c=c,
                              last=(k8 == 7): (
                        e.transpose(pv, src[:, t, c * 128:(c + 1) * 128],
                                    ident_b[:]).then_inc(sT.h, 1) if last else
                        e.transpose(pv, src[:, t, c * 128:(c + 1) * 128],
                                    ident_b[:])))
                v_.append(lambda e, v=tn: kb.wait("vector", e, sT, v))
                pvfull = psA[:, bank, :].bitcast(BF16)
                dsl = dst[:, c, tg * 1024:(tg + 1) * 1024]
                vn = sV.take()
                v_.append(lambda e, pvfull=pvfull, dsl=dsl: e.tensor_copy(
                    out=dsl, in_=pvfull).then_inc(sV.h, 1))
                t_.append(lambda e, v=vn: kb.wait("tensor", e, sV, v))
        kb.block(name, ops)

    def attention_phase(name, inT, qkv_w, proj_w, out_tiles,
                        qreg, oreg, causal, use_rope, lora,
                        rowbound=False):
        QT = carve(qreg, (128, 2, T), BF16)
        KT = carve(qreg + 8 * 1024, (128, 2, T), BF16)
        Vt = carve(qreg + 16 * 1024, (128, NT, 4 * 66), BF16)
        if rowbound:
            QT65 = carve(qreg, (65, 2, T), BF16)
            KT65 = carve(qreg + 8 * 1024, (65, 2, T), BF16)
            qabs_s = carve(qreg + 25 * 1024, (64, 512), BF16)
            aKbn = carve(qreg + 26 * 1024, (64, 8), BF16)
            aKf = carve(qreg + 26 * 1024 + 64, (64, 8), F32)
            scrb_l2 = carve(qreg + 27 * 1024, (128, 2, 512), BF16)
            negbrow = carve(qreg + 29 * 1024, (1, 512), BF16)
            assert not use_rope and not causal
        OT = carve(oreg, (128, 2, T), BF16)
        Pbuf = carve(oreg + 8 * 1024, (128, 4, 512), BF16)
        scr = carve(oreg + 12 * 1024, (128, 512), F32)
        cols = carve(oreg + 14 * 1024, (128, 16), F32)
        bcol = carve(oreg + 14 * 1024 + 64, (128, 24), F32)
        Aq_l = carve(oreg + 14 * 1024 + 192, (128, 8, 8), BF16)
        Ap_l = carve(oreg + 14 * 1024 + 320, (128, 8, 8), BF16)
        fstage = carve(oreg + 27 * 1024, (128, 8, 128), F32)
        frow = carve(oreg + 27 * 1024, (1, 1024), F32)
        fstage_p = carve(oreg + 27 * 1024, (128, 2, 512), F32)
        wproj = wslab[:].rearrange("p a b -> p (a b)").rearrange(
            "p (c h x) -> p c h x", c=2, h=2)
        wsl = lambda hh: wslab[:, :, hh * 128:(hh + 1) * 128]
        wsl_v = wslab[:]
        if lora is not None:
            tmpq = carve(oreg + 14 * 1024 + 512, (8, T), BF16)
            Bproj_s = carve(oreg + 14 * 1024 + 512, (8, D), BF16)
            bprow = carve(oreg + 16 * 1024 + 512, (1, D), BF16)
            tmpP = carve(oreg + 18 * 1024 + 512, (8, T), BF16)
            Bqs = carve(oreg + 22 * 1024 + 512, (8, 3, 256), BF16)
            bvrow = carve(oreg + 24 * 1024 + 128, (1, D), BF16)
        if use_rope:
            scrb = carve(oreg + 14 * 1024 + 512, (128, 2, 512), BF16)
            cosb = carve(oreg + 16 * 1024 + 512, (128, T), BF16)
            sinb = carve(oreg + 20 * 1024 + 512, (128, T), BF16)
            bandb = carve(oreg + 25 * 1024, (128, 896), BF16)
            rotpb = carve(oreg + 24 * 1024 + 512, (128, 128), BF16)
        ops = {"sync": [], "gpsimd": [], "tensor": [], "scalar": [], "vector": []}
        sy_, g_, t_, s_, v_ = (ops["sync"], ops["gpsimd"], ops["tensor"],
                               ops["scalar"], ops["vector"])
        bank_free = {}

        def tw(ename, lst, *gates):
            for sem, val in gates:
                if sem is not None:
                    lst.append(lambda e, s=sem, v=val: kb.wait(ename, e, s, v))

        def bfree(b):
            return bank_free.get(b, (None, 0))

        if rowbound:
            v_.append(lambda e: e.memset(KT65[64:65, :, :], 1.0))
        if use_rope:
            for src, dst in ((cos_c, cosb), (sin_c, sinb), (band_c, bandb),
                             (rotp_c, rotpb)):
                sW.take(16)
                sy_.append(lambda e, s=src, d=dst: e.dma_start(
                    out=d[:], in_=s.ap()).then_inc(sW.h, 16))
        fstage_free = (None, 0)
        if lora is not None:
            for src, dst, pat in (
                    (lora["A_qkv"], Aq_l, "(c p) r -> p c r"),
                    (lora["A_proj"], Ap_l, "(c p) r -> p c r")):
                sG.take(16)
                g_.append(lambda e, s=src, d=dst, pat=pat: e.dma_start(
                    out=d[:], in_=s.rearrange(pat, p=128)).then_inc(sG.h, 16))
            sG.take(16)
            g_.append(lambda e: e.dma_start(
                out=bvrow[0:1, :],
                in_=lora["b_qkv"][None, 2 * D:3 * D]).then_inc(sG.h, 16))
            setup_g = sG.n
            for part in range(2):
                wv0 = sW.take(16)
                fs, fv = fstage_free
                tw("sync", sy_, (fs, fv))
                sy_.append(lambda e, part=part: e.dma_start(
                    out=frow[0:1, :],
                    in_=lora["b_qkv"][None, part * D:(part + 1) * D])
                    .then_inc(sW.h, 16))
                tw("tensor", t_, (sW, wv0), bfree(5))
                tn = sT.take()
                for c in range(8):
                    t_.append(lambda e, part=part, c=c, last=(c == 7): (
                        e.transpose(psA[:, 5, part * 8 + c:part * 8 + c + 1],
                                    frow[0:1, c * 128:(c + 1) * 128],
                                    ident_f[0:1, 0:1]).then_inc(sT.h, 1)
                        if last else
                        e.transpose(psA[:, 5, part * 8 + c:part * 8 + c + 1],
                                    frow[0:1, c * 128:(c + 1) * 128],
                                    ident_f[0:1, 0:1])))
                fstage_free = (sT, tn)
            tw("vector", v_, (sT, sT.n))
            vb = sV.take()
            v_.append(lambda e: e.tensor_copy(
                out=bcol[:, 0:16], in_=psA[:, 5, 0:16]).then_inc(sV.h, 1))
            bank_free[5] = (sV, vb)
            tw("tensor", t_, (sG, setup_g))
            for tq in range(4):
                tw("tensor", t_, bfree(6))
                tn0 = sT.take()
                for c in range(8):
                    t_.append(lambda e, tq=tq, c=c, last=(c == 7): (
                        e.matmul(psA[0:8, 6, :], Aq_l[:, c, :],
                                 inT[:, c, tq * 512:(tq + 1) * 512],
                                 start=(c == 0), stop=last).then_inc(sT.h, 1)
                        if last else
                        e.matmul(psA[0:8, 6, :], Aq_l[:, c, :],
                                 inT[:, c, tq * 512:(tq + 1) * 512],
                                 start=(c == 0), stop=False)))
                tw("scalar", s_, (sT, tn0))
                ssn = sS.take()
                s_.append(lambda e, tq=tq: e.activation(
                    out=tmpq[:, tq * 512:(tq + 1) * 512],
                    in_=psA[0:8, 6, :], func=AF.Identity,
                    scale=2.0).then_inc(sS.h, 1))
                bank_free[6] = (sS, ssn)
            tmpq_ready = (sS, sS.n)

        half_ready = {0: (None, 0), 1: (None, 0)}
        half_busy = {0: (None, 0), 1: (None, 0)}

        def wload(coff, half):
            nonlocal fstage_free
            wv0 = sW.take(16)
            tw("sync", sy_, fstage_free)
            sy_.append(lambda e, coff=coff: e.dma_start(
                out=fstage[:],
                in_=qkv_w.rearrange("(c p) n -> p c n", p=128)
                [:, :, coff:coff + 128]).then_inc(sW.h, 16))
            tw("vector", v_, (sW, wv0), half_busy[half])
            vn = sV.take()
            v_.append(lambda e, half=half: e.tensor_copy(
                out=wsl(half), in_=fstage[:]).then_inc(sV.h, 1))
            fstage_free = (sV, vn)
            half_ready[half] = (sV, vn)

        score_gate_prev = (None, 0)
        scrb_gate = (None, 0)
        scrb_gate2 = {}
        bqs_waru = (None, 0)
        for hg in range(4):
            qcol = hg * 256
            if lora is not None:
                tw("gpsimd", g_, bqs_waru)
                bq_gate = sG.take(16)
                g_.append(lambda e, hg=hg: e.dma_start(
                    out=Bqs[:],
                    in_=bass.AP(lora["B_qkv"].tensor, hg * 256,
                                [[3 * D, 8], [D, 3], [1, 256]]))
                    .then_inc(sG.h, 16))
            else:
                bq_gate = 0
            if rowbound:
                wload(2 * D + qcol, 0)
                wload(2 * D + qcol + 128, 1)
                vn_ones = sV.take()
                v_.append(lambda e: e.memset(
                    Vt[:].rearrange("p t (h w) -> p t h w", h=4)
                    [:, :, :, 64:66], 1.0).then_inc(sV.h, 1))
                for tt in range(NT):
                    bsel = tt % 2
                    tw("tensor", t_, half_ready[0], half_ready[1],
                       bfree(bsel))
                    tn0 = sT.take()
                    for c in range(8):
                        first = (c == 0)
                        last = (c == 7 and lora is None)
                        t_.append(lambda e, c=c, tt=tt, bsel=bsel,
                                  first=first, last=last: (
                            e.matmul(psA[:, bsel, 0:256],
                                     inT[:, c, tt * 128:(tt + 1) * 128],
                                     wsl_v[:, c, :], start=first, stop=last)
                            .then_inc(sT.h, 1) if last else
                            e.matmul(psA[:, bsel, 0:256],
                                     inT[:, c, tt * 128:(tt + 1) * 128],
                                     wsl_v[:, c, :], start=first,
                                     stop=False)))
                    if lora is not None:
                        tw("tensor", t_, (sG, bq_gate), tmpq_ready)
                        t_.append(lambda e, bsel=bsel, qcol=qcol: e.matmul(
                            psA[:, bsel, 0:256], onesb[0:1, :],
                            bvrow[0:1, qcol:qcol + 256], start=False,
                            stop=False))
                        t_.append(lambda e, tt=tt, bsel=bsel: e.matmul(
                            psA[:, bsel, 0:256],
                            tmpq[:, tt * 128:(tt + 1) * 128],
                            Bqs[:, 2, :], start=False, stop=True)
                            .then_inc(sT.h, 1))
                    tw("vector", v_, (sT, tn0))
                    vn = sV.take()
                    v_.append(lambda e, tt=tt, bsel=bsel: e.tensor_copy(
                        out=Vt[:, tt, :].rearrange("p (h w) -> p h w", h=4)
                        [:, :, 0:64],
                        in_=psA[:, bsel, 0:256]
                        .rearrange("p (h w) -> p h w", h=4))
                        .then_inc(sV.h, 1))
                    bank_free[bsel] = (sV, vn)
                half_busy[0] = (sT, sT.n)
                half_busy[1] = (sT, sT.n)
                v_all = sV.n
                for pair in range(2):
                    negb_gate = {}
                    for dst_i, buf65 in ((0, QT65), (1, KT65)):
                        half = dst_i
                        wload(dst_i * D + qcol + pair * 128, half)
                        for tq in range(4):
                            bsel = 6 + tq % 2
                            tw("tensor", t_, half_ready[half], bfree(bsel))
                            tn0 = sT.take()
                            for c in range(8):
                                first = (c == 0)
                                last = (c == 7 and lora is None)
                                t_.append(lambda e, c=c, half=half, tq=tq,
                                          bsel=bsel, first=first,
                                          last=last: (
                                    e.matmul(psA[:, bsel, :],
                                             wsl(half)[:, c, :],
                                             inT[:, c,
                                                 tq * 512:(tq + 1) * 512],
                                             start=first, stop=last)
                                    .then_inc(sT.h, 1) if last else
                                    e.matmul(psA[:, bsel, :],
                                             wsl(half)[:, c, :],
                                             inT[:, c,
                                                 tq * 512:(tq + 1) * 512],
                                             start=first, stop=False)))
                            if lora is not None:
                                tw("tensor", t_, (sG, bq_gate))
                                t_.append(lambda e, dst_i=dst_i, pair=pair,
                                          tq=tq, bsel=bsel: e.matmul(
                                    psA[:, bsel, :],
                                    Bqs[:, dst_i,
                                        pair * 128:pair * 128 + 128],
                                    tmpq[:, tq * 512:(tq + 1) * 512],
                                    start=False, stop=True)
                                    .then_inc(sT.h, 1))
                            sl = tq % 2
                            tw("scalar", s_, (sT, tn0),
                               scrb_gate2.get(sl, (None, 0)))
                            ssn = sS.take()
                            bidx = dst_i * 8 + (qcol + pair * 128) // 128
                            if lora is not None:
                                s_.append(lambda e, bsel=bsel, bidx=bidx,
                                          sl=sl: e.activation(
                                    out=scrb_l2[:, sl, :],
                                    in_=psA[:, bsel, :],
                                    func=AF.Identity,
                                    bias=bcol[:, bidx:bidx + 1])
                                    .then_inc(sS.h, 1))
                            else:
                                s_.append(lambda e, bsel=bsel, sl=sl:
                                          e.activation(
                                    out=scrb_l2[:, sl, :],
                                    in_=psA[:, bsel, :],
                                    func=AF.Identity).then_inc(sS.h, 1))
                            bank_free[bsel] = (sS, ssn)
                            tw("vector", v_, (sS, ssn), score_gate_prev)
                            vcp = sV.take()
                            v_.append(lambda e, buf65=buf65, tq=tq, sl=sl: (
                                e.tensor_copy(
                                    out=buf65[0:64, 0,
                                              tq * 512:(tq + 1) * 512],
                                    in_=scrb_l2[0:64, sl, :]),
                                e.tensor_copy(
                                    out=buf65[0:64, 1,
                                              tq * 512:(tq + 1) * 512],
                                    in_=scrb_l2[64:128, sl, :])
                                .then_inc(sV.h, 1)))
                            scrb_gate2[sl] = (sV, vcp)
                            scrb_gate = (sV, vcp)
                        half_busy[half] = (sT, sT.n)
                    vaK0 = sV.take()
                    v_.append(lambda e: (
                        e.reduce_max(out=aKf[:, 0:1], in_=KT65[0:64, 0, :],
                                     axis=AX.X, apply_absolute_value=True),
                        e.reduce_max(out=aKf[:, 1:2], in_=KT65[0:64, 1, :],
                                     axis=AX.X, apply_absolute_value=True)
                        .then_inc(sV.h, 1)))
                    tw("vector", v_, (sV, vaK0))
                    vaK = sV.take()
                    v_.append(lambda e: e.tensor_scalar(
                        out=aKbn[:, 0:2], in0=aKf[:, 0:2], scalar1=-1.0,
                        scalar2=None, op0=ALU.mult).then_inc(sV.h, 1))
                    qa_gate = scrb_gate
                    B_prev = (None, 0)
                    for j in range(2):
                        for tq in range(4):
                            tw("scalar", s_, qa_gate, B_prev)
                            sqa = sS.take()
                            s_.append(lambda e, j=j, tq=tq: e.activation(
                                out=qabs_s[:],
                                in_=QT65[0:64, j, tq * 512:(tq + 1) * 512],
                                func=AF.Abs).then_inc(sS.h, 1))
                            tw("tensor", t_, (sS, sqa), bfree(5),
                               (sV, vaK))
                            tB = sT.take()
                            t_.append(lambda e, j=j: e.matmul(
                                psA[0:1, 5, 0:512], aKbn[:, j:j + 1],
                                qabs_s[:], start=True, stop=True)
                                .then_inc(sT.h, 1))
                            tw("vector", v_, (sT, tB))
                            vnb0 = sV.take()
                            v_.append(lambda e: e.tensor_copy(
                                out=negbrow[0:1, :],
                                in_=psA[0:1, 5, 0:512]).then_inc(sV.h, 1))
                            tw("vector", v_, (sV, vnb0))
                            vnb = sV.take()
                            v_.append(lambda e, j=j, tq=tq: e.tensor_copy(
                                out=QT65[64:65, j,
                                         tq * 512:(tq + 1) * 512],
                                in_=negbrow[0:1, :]).then_inc(sV.h, 1))
                            bank_free[5] = (sV, vnb)
                            negb_gate[(j, tq)] = (sV, vnb)
                            B_prev = (sT, tB)
                    tw("tensor", t_, (sV, v_all))
                    for j in range(2):
                        hh = pair * 2 + j
                        for tq in range(4):
                            tw("tensor", t_, negb_gate[(j, tq)])
                            groups = [list(range(g4, g4 + 4))
                                      for g4 in range(0, 16, 4)]
                            first_pv = True
                            for gi, grp in enumerate(groups):
                                for kk, kc in enumerate(grp):
                                    tw("tensor", t_, bfree(kk))
                                tn0 = sT.take()
                                for kk, kc in enumerate(grp):
                                    last = (kk == len(grp) - 1)
                                    t_.append(lambda e, kk=kk, kc=kc, j=j,
                                              tq=tq, last=last: (
                                        e.matmul(
                                            psA[:, kk, :],
                                            KT65[0:65, j,
                                                 kc * 128:(kc + 1) * 128],
                                            QT65[0:65, j,
                                                 tq * 512:(tq + 1) * 512],
                                            start=True, stop=True)
                                        .then_inc(sT.h, 1) if last else
                                        e.matmul(
                                            psA[:, kk, :],
                                            KT65[0:65, j,
                                                 kc * 128:(kc + 1) * 128],
                                            QT65[0:65, j,
                                                 tq * 512:(tq + 1) * 512],
                                            start=True, stop=True)))
                                ng = len(grp)
                                tw("scalar", s_, (sT, tn0))
                                ssn = sS.take()
                                s_.append(lambda e, ng=ng: e.activation(
                                    out=Pbuf[:, 0:ng, :],
                                    in_=psA[:, 0:ng, :],
                                    func=AF.Exp, scale=0.125,
                                    bias=0.0).then_inc(sS.h, 1))
                                pv_gate = (sS, ssn)
                                tw("tensor", t_, pv_gate)
                                if first_pv:
                                    tw("tensor", t_, bfree(4))
                                for kk, kc in enumerate(grp):
                                    islast = (gi == len(groups) - 1) and \
                                        (kk == len(grp) - 1)
                                    tpv = sT.take()
                                    t_.append(lambda e, kk=kk, kc=kc, hh=hh,
                                              fp=(first_pv and kk == 0),
                                              lo=islast:
                                        e.matmul(psA[0:65, 4, :],
                                                 Vt[:, kc,
                                                    hh * 66:hh * 66 + 65],
                                                 Pbuf[:, kk, :],
                                                 start=fp, stop=lo)
                                        .then_inc(sT.h, 1))
                                    bank_free[kk] = (sT, tpv)
                                first_pv = False
                            odone = sT.n
                            tw("vector", v_, (sT, odone))
                            vr = sV.take()
                            v_.append(lambda e: e.reciprocal(
                                out=scr[64:65, 0:512],
                                in_=psA[64:65, 4, :]).then_inc(sV.h, 1))
                            tw("tensor", t_, (sV, vr), bfree(5))
                            tb = sT.take()
                            t_.append(lambda e: e.matmul(
                                psA[0:64, 5, :], onesf[64:65, 0:64],
                                scr[64:65, 0:512], start=True, stop=True)
                                .then_inc(sT.h, 1))
                            tw("vector", v_, (sT, tb))
                            vc1 = sV.take()
                            v_.append(lambda e: e.tensor_copy(
                                out=scr[0:64, :], in_=psA[0:64, 5, :])
                                .then_inc(sV.h, 1))
                            tw("vector", v_, (sV, vc1))
                            vv2 = sV.take()
                            v_.append(lambda e, hh=hh, tq=tq:
                                e.tensor_tensor(
                                    out=OT[(hh % 2) * 64:(hh % 2) * 64 + 64,
                                           hh // 2,
                                           tq * 512:(tq + 1) * 512],
                                    in0=psA[0:64, 4, :], in1=scr[0:64, :],
                                    op=ALU.mult).then_inc(sV.h, 1))
                            bank_free[5] = (sV, vv2)
                            bank_free[4] = (sV, vv2)
                    score_gate_prev = (sT, sT.n)
            else:
                for dst_i, buf in ((0, QT), (1, KT)):
                    for c2 in range(2):
                        wload(dst_i * D + qcol + c2 * 128, c2)
                        for tq in range(4):
                            bsel = (c2 * 4 + tq) % 2
                            tw("tensor", t_, half_ready[c2], bfree(bsel))
                            tn0 = sT.take()
                            for c in range(8):
                                first = (c == 0)
                                last = (c == 7 and lora is None)
                                t_.append(lambda e, c=c, c2=c2, tq=tq, bsel=bsel,
                                          first=first, last=last: (
                                    e.matmul(psA[:, bsel, :], wsl(c2)[:, c, :],
                                             inT[:, c, tq * 512:(tq + 1) * 512],
                                             start=first, stop=last)
                                    .then_inc(sT.h, 1) if last else
                                    e.matmul(psA[:, bsel, :], wsl(c2)[:, c, :],
                                             inT[:, c, tq * 512:(tq + 1) * 512],
                                             start=first, stop=False)))
                            if lora is not None:
                                tw("tensor", t_, (sG, bq_gate))
                                t_.append(lambda e, dst_i=dst_i, c2=c2, tq=tq,
                                          bsel=bsel: e.matmul(
                                    psA[:, bsel, :],
                                    Bqs[:, dst_i, c2 * 128:c2 * 128 + 128],
                                    tmpq[:, tq * 512:(tq + 1) * 512],
                                    start=False, stop=True).then_inc(sT.h, 1))
                            tw("scalar", s_, (sT, tn0))
                            ssn = sS.take()
                            bidx = dst_i * 8 + (qcol + c2 * 128) // 128
                            if lora is not None:
                                s_.append(lambda e, buf=buf, c2=c2, tq=tq,
                                          bsel=bsel, bidx=bidx: e.activation(
                                    out=buf[:, c2, tq * 512:(tq + 1) * 512],
                                    in_=psA[:, bsel, :], func=AF.Identity,
                                    bias=bcol[:, bidx:bidx + 1]).then_inc(sS.h, 1))
                            else:
                                s_.append(lambda e, buf=buf, c2=c2, tq=tq,
                                          bsel=bsel: e.activation(
                                    out=buf[:, c2, tq * 512:(tq + 1) * 512],
                                    in_=psA[:, bsel, :], func=AF.Identity)
                                    .then_inc(sS.h, 1))
                            bank_free[bsel] = (sS, ssn)
                            if use_rope:
                                tw("tensor", t_, (sS, ssn), bfree(5))
                                rn = sT.take()
                                t_.append(lambda e, buf=buf, c2=c2, tq=tq:
                                    e.matmul(psA[:, 5, :], rotpb[:],
                                             buf[:, c2, tq * 512:(tq + 1) * 512],
                                             start=True, stop=True)
                                    .then_inc(sT.h, 1))
                                tw("vector", v_, (sT, rn))
                                vn1 = sV.take(2)
                                v_.append(lambda e, buf=buf, c2=c2, tq=tq: (
                                    e.tensor_tensor(
                                        out=scrb[:, 0, :],
                                        in0=buf[:, c2, tq * 512:(tq + 1) * 512],
                                        in1=cosb[:, tq * 512:(tq + 1) * 512],
                                        op=ALU.mult),
                                    e.tensor_tensor(
                                        out=scrb[:, 1, :], in0=psA[:, 5, :],
                                        in1=sinb[:, tq * 512:(tq + 1) * 512],
                                        op=ALU.mult).then_inc(sV.h, 2)))
                                bank_free[5] = (sV, vn1)
                                vn2 = sV.take()
                                v_.append(lambda e, buf=buf, c2=c2, tq=tq:
                                    e.tensor_tensor(
                                        out=buf[:, c2, tq * 512:(tq + 1) * 512],
                                        in0=scrb[:, 0, :], in1=scrb[:, 1, :],
                                        op=ALU.add).then_inc(sV.h, 1))
                        half_busy[c2] = (sT, sT.n)
                wload(2 * D + qcol, 0)
                wload(2 * D + qcol + 128, 1)
                vn_ones = sV.take()
                v_.append(lambda e: e.memset(
                    Vt[:].rearrange("p t (h w) -> p t h w", h=4)[:, :, :, 64:66],
                    1.0).then_inc(sV.h, 1))
                for tt in range(NT):
                    bsel = tt % 2
                    tw("tensor", t_, half_ready[0], half_ready[1], bfree(bsel))
                    tn0 = sT.take()
                    for c in range(8):
                        first = (c == 0)
                        last = (c == 7 and lora is None)
                        t_.append(lambda e, c=c, tt=tt, bsel=bsel,
                                  first=first, last=last: (
                            e.matmul(psA[:, bsel, 0:256],
                                     inT[:, c, tt * 128:(tt + 1) * 128],
                                     wsl_v[:, c, :], start=first, stop=last)
                            .then_inc(sT.h, 1) if last else
                            e.matmul(psA[:, bsel, 0:256],
                                     inT[:, c, tt * 128:(tt + 1) * 128],
                                     wsl_v[:, c, :], start=first, stop=False)))
                    if lora is not None:
                        t_.append(lambda e, bsel=bsel, qcol=qcol: e.matmul(
                            psA[:, bsel, 0:256], onesb[0:1, :],
                            bvrow[0:1, qcol:qcol + 256], start=False, stop=False))
                        t_.append(lambda e, tt=tt, bsel=bsel: e.matmul(
                            psA[:, bsel, 0:256], tmpq[:, tt * 128:(tt + 1) * 128],
                            Bqs[:, 2, :], start=False, stop=True).then_inc(sT.h, 1))
                    tw("vector", v_, (sT, tn0))
                    vn = sV.take()
                    v_.append(lambda e, tt=tt, bsel=bsel: e.tensor_copy(
                        out=Vt[:, tt, :].rearrange("p (h w) -> p h w", h=4)
                        [:, :, 0:64],
                        in_=psA[:, bsel, 0:256].rearrange("p (h w) -> p h w", h=4))
                        .then_inc(sV.h, 1))
                    bank_free[bsel] = (sV, vn)
                half_busy[0] = (sT, sT.n)
                half_busy[1] = (sT, sT.n)
                v_all = sV.n
                qk_gate = (sV, sV.n) if use_rope else (sS, sS.n)
                tw("vector", v_, qk_gate)
                vr1 = sV.take(2)
                v_.append(lambda e: (
                    e.reduce_max(out=cols[:, 0:1], in_=QT[:], axis=AX.XY,
                                 apply_absolute_value=True),
                    e.reduce_max(out=cols[:, 1:2], in_=KT[:], axis=AX.XY,
                                 apply_absolute_value=True).then_inc(sV.h, 2)))
                tw("tensor", t_, (sV, vr1), bfree(5))
                tn = sT.take()
                t_.append(lambda e: (
                    e.transpose(psA[0:1, 5, 0:128], cols[:, 0:1], ident_f[:]),
                    e.transpose(psA[0:1, 5, 128:256], cols[:, 1:2], ident_f[:])
                    .then_inc(sT.h, 1)))
                tw("vector", v_, (sT, tn))
                vr2 = sV.take(2)
                v_.append(lambda e: (
                    e.reduce_max(out=scr[0:1, 0:1], in_=psA[0:1, 5, 0:128],
                                 axis=AX.X),
                    e.reduce_max(out=scr[0:1, 1:2], in_=psA[0:1, 5, 128:256],
                                 axis=AX.X).then_inc(sV.h, 2)))
                tw("vector", v_, (sV, vr2))
                vr3 = sV.take()
                v_.append(lambda e: e.tensor_tensor(
                    out=scr[0:1, 2:3], in0=scr[0:1, 0:1], in1=scr[0:1, 1:2],
                    op=ALU.mult).then_inc(sV.h, 1))
                tw("tensor", t_, (sV, vr3))
                tn2 = sT.take()
                t_.append(lambda e: e.matmul(
                    psA[:, 5, 0:1], onesf[0:1, :], scr[0:1, 2:3],
                    start=True, stop=True).then_inc(sT.h, 1))
                tw("vector", v_, (sT, tn2))
                vneg = sV.take()
                v_.append(lambda e: e.tensor_scalar(
                    out=cols[:, 2:3], in0=psA[:, 5, 0:1], scalar1=-8.0,
                    scalar2=None, op0=ALU.mult).then_inc(sV.h, 1))
                bank_free[5] = (sV, vneg)
                tw("tensor", t_, (sV, v_all))
                for hh in range(4):
                    sub = (hh % 2) * 64
                    hc = hh // 2
                    for tq in range(4):
                        kcs = list(range(16)) if not causal else \
                            list(range(4 * tq + 4))
                        groups = [kcs[i:i + 4] for i in range(0, len(kcs), 4)]
                        first_pv = True
                        for gi, grp in enumerate(groups):
                            for kk, kc in enumerate(grp):
                                tw("tensor", t_, bfree(kk))
                            tn0 = sT.take()
                            for kk, kc in enumerate(grp):
                                last = (kk == len(grp) - 1)
                                t_.append(lambda e, kk=kk, kc=kc, sub=sub, hc=hc,
                                          tq=tq, last=last: (
                                    e.matmul(psA[:, kk, :],
                                             KT[sub:sub + 64, hc,
                                                kc * 128:(kc + 1) * 128],
                                             QT[sub:sub + 64, hc,
                                                tq * 512:(tq + 1) * 512],
                                             start=True, stop=True)
                                    .then_inc(sT.h, 1) if last else
                                    e.matmul(psA[:, kk, :],
                                             KT[sub:sub + 64, hc,
                                                kc * 128:(kc + 1) * 128],
                                             QT[sub:sub + 64, hc,
                                                tq * 512:(tq + 1) * 512],
                                             start=True, stop=True)))
                            ng = len(grp)
                            tw("scalar", s_, (sT, tn0), (sV, vneg))
                            ssn = sS.take()
                            s_.append(lambda e, ng=ng: e.activation(
                                out=Pbuf[:, 0:ng, :], in_=psA[:, 0:ng, :],
                                func=AF.Exp, scale=0.125,
                                bias=cols[:, 2:3]).then_inc(sS.h, 1))
                            pv_gate = (sS, ssn)
                            if causal and grp[-1] >= 4 * tq:
                                mask_ops = tuple(
                                    (kk, 384 - 128 * (kc - 4 * tq))
                                    for kk, kc in enumerate(grp) if kc >= 4 * tq)
                                tw("vector", v_, (sS, ssn))
                                vm = sV.take()
                                def mk_mask(e, mo=mask_ops):
                                    for i, (kk, sh) in enumerate(mo):
                                        r = e.tensor_tensor(
                                            out=Pbuf[:, kk, :],
                                            in0=Pbuf[:, kk, :],
                                            in1=bandb[:, sh:sh + 512],
                                            op=ALU.mult)
                                        if i == len(mo) - 1:
                                            r.then_inc(sV.h, 1)
                                v_.append(mk_mask)
                                pv_gate = (sV, vm)
                            tw("tensor", t_, pv_gate)
                            if first_pv:
                                tw("tensor", t_, bfree(4))
                            for kk, kc in enumerate(grp):
                                islast = (gi == len(groups) - 1) and \
                                    (kk == len(grp) - 1)
                                tpv = sT.take()
                                t_.append(lambda e, kk=kk, kc=kc, hh=hh,
                                          fp=(first_pv and kk == 0), lo=islast:
                                    e.matmul(psA[0:65, 4, :],
                                             Vt[:, kc, hh * 66:hh * 66 + 65],
                                             Pbuf[:, kk, :], start=fp, stop=lo)
                                    .then_inc(sT.h, 1))
                                bank_free[kk] = (sT, tpv)
                            first_pv = False
                        odone = sT.n
                        tw("vector", v_, (sT, odone))
                        vr = sV.take()
                        v_.append(lambda e: e.reciprocal(
                            out=scr[64:65, 0:512], in_=psA[64:65, 4, :])
                            .then_inc(sV.h, 1))
                        tw("tensor", t_, (sV, vr), bfree(5))
                        tb = sT.take()
                        t_.append(lambda e: e.matmul(
                            psA[0:64, 5, :], onesf[64:65, 0:64],
                            scr[64:65, 0:512], start=True, stop=True)
                            .then_inc(sT.h, 1))
                        tw("vector", v_, (sT, tb))
                        vc1 = sV.take()
                        v_.append(lambda e: e.tensor_copy(
                            out=scr[0:64, :], in_=psA[0:64, 5, :])
                            .then_inc(sV.h, 1))
                        tw("vector", v_, (sV, vc1))
                        vv2 = sV.take()
                        v_.append(lambda e, hh=hh, tq=tq: e.tensor_tensor(
                            out=OT[(hh % 2) * 64:(hh % 2) * 64 + 64, hh // 2,
                                   tq * 512:(tq + 1) * 512],
                            in0=psA[0:64, 4, :], in1=scr[0:64, :],
                            op=ALU.mult).then_inc(sV.h, 1))
                        bank_free[5] = (sV, vv2)
                        bank_free[4] = (sV, vv2)

            pl_done = []
            for half in range(2):
                nonlocal_dummy = 0
                wv0 = sW.take(16)
                tw("sync", sy_, fstage_free)
                sy_.append(lambda e, half=half, hg=hg: e.dma_start(
                    out=fstage_p[:],
                    in_=proj_w.rearrange("(c p) n -> p c n", p=128)
                    [:, 2 * hg:2 * hg + 2, half * 512:(half + 1) * 512])
                    .then_inc(sW.h, 16))
                tw("vector", v_, (sW, wv0), half_busy[0], half_busy[1])
                vn = sV.take()
                v_.append(lambda e, half=half: e.tensor_copy(
                    out=wproj[:, :, half, :], in_=fstage_p[:])
                    .then_inc(sV.h, 1))
                fstage_free = (sV, vn)
                pl_done.append(vn)
            half_ready[0] = (sV, pl_done[1])
            half_ready[1] = (sV, pl_done[1])
            ot_done = sV.n
            for tt in range(NT):
                for half in range(2):
                    bsel = 6 + (tt * 2 + half) % 2
                    tw("tensor", t_, (sV, max(pl_done[half], ot_done)),
                       bfree(bsel))
                    tn0 = sT.take()
                    for c2 in range(2):
                        first, last = (c2 == 0), (c2 == 1)
                        t_.append(lambda e, tt=tt, half=half, c2=c2,
                                  bsel=bsel, first=first, last=last: (
                            e.matmul(psA[:, bsel, :],
                                     OT[:, c2, tt * 128:(tt + 1) * 128],
                                     wproj[:, c2, half, :],
                                     start=first, stop=last)
                            .then_inc(sT.h, 1) if last else
                            e.matmul(psA[:, bsel, :],
                                     OT[:, c2, tt * 128:(tt + 1) * 128],
                                     wproj[:, c2, half, :],
                                     start=first, stop=False)))
                    tw("vector", v_, (sT, tn0))
                    vn = sV.take()
                    v_.append(lambda e, tt=tt, half=half, bsel=bsel:
                        e.tensor_tensor(
                            out=out_tiles(tt)[:, half * 512:(half + 1) * 512],
                            in0=out_tiles(tt)[:, half * 512:(half + 1) * 512],
                            in1=psA[:, bsel, :], op=ALU.add).then_inc(sV.h, 1))
                    bank_free[bsel] = (sV, vn)
            half_busy[0] = (sT, sT.n)
            half_busy[1] = (sT, sT.n)
            if lora is not None:
                for tq in range(4):
                    bsel = 6 + tq % 2
                    tw("tensor", t_, bfree(bsel))
                    tn0 = sT.take()
                    for c2 in range(2):
                        first, last = (c2 == 0), (c2 == 1)
                        t_.append(lambda e, tq=tq, c2=c2, bsel=bsel, hg=hg,
                                  first=first, last=last: (
                            e.matmul(psA[0:8, bsel, :],
                                     Ap_l[:, hg * 2 + c2, :],
                                     OT[:, c2, tq * 512:(tq + 1) * 512],
                                     start=first, stop=last)
                            .then_inc(sT.h, 1) if last else
                            e.matmul(psA[0:8, bsel, :],
                                     Ap_l[:, hg * 2 + c2, :],
                                     OT[:, c2, tq * 512:(tq + 1) * 512],
                                     start=first, stop=False)))
                    tw("vector", v_, (sT, tn0))
                    vn = sV.take()
                    if hg == 0:
                        v_.append(lambda e, tq=tq, bsel=bsel: e.tensor_copy(
                            out=tmpP[:, tq * 512:(tq + 1) * 512],
                            in_=psA[0:8, bsel, :]).then_inc(sV.h, 1))
                    else:
                        v_.append(lambda e, tq=tq, bsel=bsel: e.tensor_tensor(
                            out=tmpP[:, tq * 512:(tq + 1) * 512],
                            in0=tmpP[:, tq * 512:(tq + 1) * 512],
                            in1=psA[0:8, bsel, :], op=ALU.add)
                            .then_inc(sV.h, 1))
                    bank_free[bsel] = (sV, vn)
            bqs_waru = (sT, sT.n)
        if lora is not None:
            tmpq_done = (sT, sT.n)
            tw("gpsimd", g_, tmpq_done)
            gq = sG.take(32)
            g_.append(lambda e: (
                e.dma_start(out=Bproj_s[:], in_=lora["B_proj"])
                .then_inc(sG.h, 16),
                e.dma_start(out=bprow[0:1, :], in_=lora["b_proj"][None, :])
                .then_inc(sG.h, 16)))
            tw("vector", v_, (sG, gq))
            vsc = sV.take()
            v_.append(lambda e: e.tensor_scalar(
                out=Bproj_s[:], in0=Bproj_s[:], scalar1=2.0, scalar2=None,
                op0=ALU.mult).then_inc(sV.h, 1))
            for tt in range(NT):
                for half in range(2):
                    bsel = 6 + (tt * 2 + half) % 2
                    tw("tensor", t_, (sV, vsc), bfree(bsel))
                    tn0 = sT.take()
                    t_.append(lambda e, tt=tt, half=half, bsel=bsel: (
                        e.matmul(psA[:, bsel, :],
                                 tmpP[:, tt * 128:(tt + 1) * 128],
                                 Bproj_s[:, half * 512:(half + 1) * 512],
                                 start=True, stop=False),
                        e.matmul(psA[:, bsel, :], onesb[0:1, :],
                                 bprow[0:1, half * 512:(half + 1) * 512],
                                 start=False, stop=True).then_inc(sT.h, 1)))
                    tw("vector", v_, (sT, tn0))
                    vn = sV.take()
                    v_.append(lambda e, tt=tt, half=half, bsel=bsel:
                        e.tensor_tensor(
                            out=out_tiles(tt)[:, half * 512:(half + 1) * 512],
                            in0=out_tiles(tt)[:, half * 512:(half + 1) * 512],
                            in1=psA[:, bsel, :], op=ALU.add).then_inc(sV.h, 1))
                    bank_free[bsel] = (sV, vn)
        kb.block(name, ops)

    ph0()

    xn_tok = carve(0, (128, NT, D), BF16)
    dbg_dump("acc0", acc[:], (128, NT, D), F32)
    if stages >= 1:
        ln_pass("l0_ln", lambda t: acc[:, t, :], lambda t: xn_tok[:, t, :])
        dbg_dump("xn", xn_tok[:], (128, NT, D), BF16)
        dbg_dump("lnst", lnst, (128, NT, 8), F32)

    xnT = carve(R1, (128, NC, T), BF16)
    if stages >= 2:
        transpose_tok_to_feat("l0_tr", xn_tok, xnT)
        dbg_dump("xnT", xnT[:], (128, NC, T), BF16)

    if stages >= 3:
        attention_phase("l0_attn", xnT, W_qkv, W_proj,
                        lambda t: acc[:, t, :],
                        qreg=R2, oreg=R3, causal=False, use_rope=False,
                        lora=dict(A_qkv=A_qkv, B_qkv=B_qkv, b_qkv=b_qkv,
                                  A_proj=A_proj, B_proj=B_proj,
                                  b_proj=b_proj),
                        rowbound=True)
        dbg_dump("acc_l03", acc[:], (128, NT, D), F32)
        dbg_dump("l0_QT", carve(R2, (65, 2, T), BF16), (65, 2, T), BF16)
        dbg_dump("l0_KT", carve(R2 + 8 * 1024, (65, 2, T), BF16), (65, 2, T), BF16)
        dbg_dump("l0_Vt", carve(R2 + 16 * 1024, (128, NT, 264), BF16), (128, NT, 264), BF16)
        dbg_dump("l0_OT", carve(R3, (128, 2, T), BF16), (128, 2, T), BF16)
        dbg_dump("l0_tmpq", carve(R3 + 14 * 1024 + 512, (8, T), BF16), (8, T), BF16)
        dbg_dump("l0_bcol", carve(R3 + 14 * 1024 + 64, (128, 16), F32), (128, 16), F32)
        dbg_dump("l0_cols", carve(R3 + 14 * 1024, (128, 3), F32), (128, 3), F32)
        dbg_dump("l0_Aql", carve(R3 + 14 * 1024 + 192, (128, 8, 8), BF16), (128, 8, 8), BF16)
        dbg_dump("l0_Bqs", carve(R3 + 22 * 1024 + 512, (8, 3, 256), BF16), (8, 3, 256), BF16)
        dbg_dump("l0_bvrow", carve(R3 + 24 * 1024 + 64, (1, D), BF16), (1, D), BF16)

    ln1out = carve(R2, (128, NT, D), BF16)
    ln1T = carve(R1, (128, NC, T), BF16)
    hconvT = carve(R2, (128, NC, T), BF16)
    if stages >= 4:
        g_row = carve(R3, (1, D), BF16)
        b_row = carve(R3 + 2 * 1024, (1, D), BF16)
        g_til = carve(R3 + 4 * 1024, (128, D), BF16)
        b_til = carve(R3 + 6 * 1024, (128, D), BF16)
        wcols = carve(R3 + 8 * 1024, (128, 8, 3), F32)
        ops = {"sync": [], "gpsimd": [], "tensor": [], "vector": []}
        gv0 = sG.take(32)
        ops["gpsimd"].append(lambda e: (
            e.dma_start(out=g_row[0:1, :], in_=ln1_w[None, :]).then_inc(sG.h, 16),
            e.dma_start(out=b_row[0:1, :], in_=ln1_b[None, :]).then_inc(sG.h, 16)))
        wv0 = sW.take(16)
        ops["sync"].append(lambda e: e.dma_start(
            out=wcols[:], in_=conv_w.rearrange("(c p) o k -> p c (o k)", p=128))
            .then_inc(sW.h, 16))
        ops["tensor"].append(lambda e, v=gv0: kb.wait("tensor", e, sG, v))
        tn = sT.take()
        ops["tensor"].append(lambda e: (
            e.matmul(psA[:, 0, :], onesb[0:1, :], g_row[0:1, 0:512],
                     start=True, stop=True),
            e.matmul(psA[:, 1, :], onesb[0:1, :], g_row[0:1, 512:1024],
                     start=True, stop=True),
            e.matmul(psA[:, 2, :], onesb[0:1, :], b_row[0:1, 0:512],
                     start=True, stop=True),
            e.matmul(psA[:, 3, :], onesb[0:1, :], b_row[0:1, 512:1024],
                     start=True, stop=True).then_inc(sT.h, 1)))
        ops["vector"].append(lambda e, v=tn: kb.wait("vector", e, sT, v))
        vn = sV.take()
        ops["vector"].append(lambda e: (
            e.tensor_copy(out=g_til[:, 0:512], in_=psA[:, 0, :]),
            e.tensor_copy(out=g_til[:, 512:1024], in_=psA[:, 1, :]),
            e.tensor_copy(out=b_til[:, 0:512], in_=psA[:, 2, :]),
            e.tensor_copy(out=b_til[:, 512:1024], in_=psA[:, 3, :])
            .then_inc(sV.h, 1)))
        kb.block("l04_pre", ops)
        ln_pass("l04_ln", lambda t: xn_tok[:, t, :], lambda t: ln1out[:, t, :],
                gb=(g_til[:], b_til[:]))
        dbg_dump("ln1", ln1out[:], (128, NT, D), BF16)
        transpose_tok_to_feat("l04_tr", ln1out, ln1T)
        ops = {"vector": []}
        wc = sW.n
        ops["vector"].append(lambda e, v=wc: kb.wait("vector", e, sW, v))
        for c in range(NC):
            ops["vector"].append(lambda e, c=c: e.tensor_scalar(
                out=hconvT[:, c, :], in0=ln1T[:, c, :],
                scalar1=wcols[:, c, 2:3], scalar2=None, op0=ALU.mult))
        for c in range(NC):
            ops["vector"].append(lambda e, c=c: e.scalar_tensor_tensor(
                out=hconvT[:, c, 1:T], in0=ln1T[:, c, 0:T - 1],
                scalar=wcols[:, c, 1:2], in1=hconvT[:, c, 1:T],
                op0=ALU.mult, op1=ALU.add))
        for c in range(NC):
            last = (c == NC - 1)
            sV.take() if last else None
            ops["vector"].append(lambda e, c=c, last=last: (
                e.scalar_tensor_tensor(
                    out=hconvT[:, c, 2:T], in0=ln1T[:, c, 0:T - 2],
                    scalar=wcols[:, c, 0:1], in1=hconvT[:, c, 2:T],
                    op0=ALU.mult, op1=ALU.add).then_inc(sV.h, 1) if last else
                e.scalar_tensor_tensor(
                    out=hconvT[:, c, 2:T], in0=ln1T[:, c, 0:T - 2],
                    scalar=wcols[:, c, 0:1], in1=hconvT[:, c, 2:T],
                    op0=ALU.mult, op1=ALU.add)))
        kb.block("l04_conv", ops)
        dbg_dump("hconvT", hconvT[:], (128, NC, T), BF16)

    if stages >= 5:
        attention_phase("l0i_attn", hconvT, attn_w, proj_w,
                        lambda t: xn_tok[:, t, :],
                        qreg=R1, oreg=R3, causal=True, use_rope=True,
                        lora=None)
        dbg_dump("x_f1", xn_tok[:], (128, NT, D), BF16)

    h2tok = carve(R2, (128, NT, D), BF16)
    h2T = carve(R0, (128, NC, T), BF16)
    if stages >= 6:
        g_row = carve(R3 + 2 * 1024, (1, D), BF16)
        b_row = carve(R3 + 4 * 1024, (1, D), BF16)
        g_til = carve(R3 + 6 * 1024, (128, D), BF16)
        b_til = carve(R3 + 8 * 1024, (128, D), BF16)
        ops = {"gpsimd": [], "tensor": [], "vector": []}
        gv0 = sG.take(32)
        ops["gpsimd"].append(lambda e: (
            e.dma_start(out=g_row[0:1, :], in_=ln2_w[None, :]).then_inc(sG.h, 16),
            e.dma_start(out=b_row[0:1, :], in_=ln2_b[None, :]).then_inc(sG.h, 16)))
        ops["tensor"].append(lambda e, v=gv0: kb.wait("tensor", e, sG, v))
        tn = sT.take()
        ops["tensor"].append(lambda e: (
            e.matmul(psA[:, 0, :], onesb[0:1, :], g_row[0:1, 0:512],
                     start=True, stop=True),
            e.matmul(psA[:, 1, :], onesb[0:1, :], g_row[0:1, 512:1024],
                     start=True, stop=True),
            e.matmul(psA[:, 2, :], onesb[0:1, :], b_row[0:1, 0:512],
                     start=True, stop=True),
            e.matmul(psA[:, 3, :], onesb[0:1, :], b_row[0:1, 512:1024],
                     start=True, stop=True).then_inc(sT.h, 1)))
        ops["vector"].append(lambda e, v=tn: kb.wait("vector", e, sT, v))
        sV.take()
        ops["vector"].append(lambda e: (
            e.tensor_copy(out=g_til[:, 0:512], in_=psA[:, 0, :]),
            e.tensor_copy(out=g_til[:, 512:1024], in_=psA[:, 1, :]),
            e.tensor_copy(out=b_til[:, 0:512], in_=psA[:, 2, :]),
            e.tensor_copy(out=b_til[:, 512:1024], in_=psA[:, 3, :])
            .then_inc(sV.h, 1)))
        kb.block("l06_pre", ops)
        ln_pass("l06_ln", lambda t: xn_tok[:, t, :], lambda t: h2tok[:, t, :],
                gb=(g_til[:], b_til[:]))
        dbg_dump("h2", h2tok[:], (128, NT, D), BF16)
        ops = {"vector": []}
        for t in range(NT):
            last = (t == NT - 1)
            sV.take() if last else None
            ops["vector"].append(lambda e, t=t, last=last: (
                e.tensor_tensor(out=acc[:, t, :], in0=acc[:, t, :],
                                in1=xn_tok[:, t, :], op=ALU.add)
                .then_inc(sV.h, 1) if last else
                e.tensor_tensor(out=acc[:, t, :], in0=acc[:, t, :],
                                in1=xn_tok[:, t, :], op=ALU.add)))
        kb.block("l06_accadd", ops)
        transpose_tok_to_feat("l06_tr", h2tok, h2T)

        masks = carve(R3, (128, NT, 4), F32)
        lg = carve(R3 + 256, (128, NT, 4), F32)
        mgescr = carve(R3 + 512, (128, NT, 8), F32)
        rslab = carve(R3 + 1536, (128, 8, 4), BF16)
        ops = {"gpsimd": [], "tensor": [], "vector": []}
        g_, t_, v_ = ops["gpsimd"], ops["tensor"], ops["vector"]
        gv0 = sG.take(16)
        g_.append(lambda e: e.dma_start(
            out=rslab[:], in_=router_w.rearrange("(c p) n -> p c n", p=128))
            .then_inc(sG.h, 16))
        t_.append(lambda e, v=gv0: kb.wait("tensor", e, sG, v))
        rcopy = {}
        for tt in range(NT):
            bsel = tt % 2
            if bsel in rcopy:
                t_.append(lambda e, v=rcopy[bsel]: kb.wait("tensor", e, sV, v))
            tn0 = sT.take()
            t_.append(lambda e, tt=tt, bsel=bsel: [
                e.matmul(psA[0:128, bsel, 0:4], h2T[:, c, tt * 128:(tt + 1) * 128],
                         rslab[:, c, :], start=(c == 0), stop=(c == 7))
                .then_inc(sT.h, 1) if c == 7 else
                e.matmul(psA[0:128, bsel, 0:4], h2T[:, c, tt * 128:(tt + 1) * 128],
                         rslab[:, c, :], start=(c == 0), stop=False)
                for c in range(8)][-1])
            v_.append(lambda e, v=tn0: kb.wait("vector", e, sT, v))
            vn = sV.take()
            v_.append(lambda e, tt=tt, bsel=bsel: e.tensor_copy(
                out=lg[:, tt, :], in_=psA[0:128, bsel, 0:4]).then_inc(sV.h, 1))
            rcopy[bsel] = vn
        for tt in range(NT):
            v_.append(lambda e, tt=tt: e.reduce_max(
                out=mgescr[:, tt, 0:1], in_=lg[:, tt, :], axis=AX.X))
        for ei in range(4):
            for tt in range(NT):
                v_.append(lambda e, tt=tt, ei=ei: e.tensor_scalar(
                    out=mgescr[:, tt, 1 + ei:2 + ei], in0=lg[:, tt, ei:ei + 1],
                    scalar1=mgescr[:, tt, 0:1], scalar2=None, op0=ALU.is_ge))
        for tt in range(NT):
            v_.append(lambda e, tt=tt: e.tensor_copy(
                out=masks[:, tt, 0:1], in_=mgescr[:, tt, 1:2]))
        for tt in range(NT):
            v_.append(lambda e, tt=tt: e.scalar_tensor_tensor(
                out=mgescr[:, tt, 5:6], in0=mgescr[:, tt, 1:2], scalar=-1.0,
                in1=cst[:, 3:4], op0=ALU.mult, op1=ALU.add))
        for tt in range(NT):
            v_.append(lambda e, tt=tt: e.tensor_tensor(
                out=masks[:, tt, 1:2], in0=mgescr[:, tt, 2:3],
                in1=mgescr[:, tt, 5:6], op=ALU.mult))
        for tt in range(NT):
            v_.append(lambda e, tt=tt: e.scalar_tensor_tensor(
                out=mgescr[:, tt, 6:7], in0=mgescr[:, tt, 2:3], scalar=-1.0,
                in1=cst[:, 3:4], op0=ALU.mult, op1=ALU.add))
        for tt in range(NT):
            v_.append(lambda e, tt=tt: e.tensor_tensor(
                out=mgescr[:, tt, 5:6], in0=mgescr[:, tt, 5:6],
                in1=mgescr[:, tt, 6:7], op=ALU.mult))
        for tt in range(NT):
            v_.append(lambda e, tt=tt: e.tensor_tensor(
                out=masks[:, tt, 2:3], in0=mgescr[:, tt, 3:4],
                in1=mgescr[:, tt, 5:6], op=ALU.mult))
        for tt in range(NT):
            v_.append(lambda e, tt=tt: e.scalar_tensor_tensor(
                out=mgescr[:, tt, 6:7], in0=mgescr[:, tt, 3:4], scalar=-1.0,
                in1=cst[:, 3:4], op0=ALU.mult, op1=ALU.add))
        for tt in range(NT):
            v_.append(lambda e, tt=tt: e.tensor_tensor(
                out=mgescr[:, tt, 5:6], in0=mgescr[:, tt, 5:6],
                in1=mgescr[:, tt, 6:7], op=ALU.mult))
        for tt in range(NT):
            last = (tt == NT - 1)
            sV.take() if last else None
            v_.append(lambda e, tt=tt, last=last: (
                e.tensor_tensor(out=masks[:, tt, 3:4], in0=mgescr[:, tt, 4:5],
                                in1=mgescr[:, tt, 5:6], op=ALU.mult)
                .then_inc(sV.h, 1) if last else
                e.tensor_tensor(out=masks[:, tt, 3:4], in0=mgescr[:, tt, 4:5],
                                in1=mgescr[:, tt, 5:6], op=ALU.mult)))
        kb.block("l06_router", ops)
        dbg_dump("moemasks", masks, (128, NT, 4), F32)
        dbg_dump("moelg", lg, (128, NT, 4), F32)
        dbg_dump("moerslab", rslab, (128, 8, 4), BF16)

        actA = carve(R1, (128, 16, D), BF16)
        actB = carve(R2, (128, 16, D), BF16)
        sgscr = carve(R3 + 1 * 1024 + 2048, (128, 2, 512), BF16)
        wvst = [carve(R3 + 12 * 1024 + i * 4096, (128, 8, 128), F32)
                for i in range(4)]
        pjst = [carve(R3 + 28 * 1024 + i * 2048, (128, 512), F32)
                for i in range(2)]
        mask_gate = sV.n
        ops = {"sync": [], "gpsimd": [], "tensor": [], "scalar": [], "vector": []}
        sy_, g_, t_, s_, v_ = (ops["sync"], ops["gpsimd"], ops["tensor"],
                               ops["scalar"], ops["vector"])
        bankF = {}

        def bfree2(b):
            return bankF.get(b, (None, 0))

        def tw2(ename, lst, *gates):
            for sem, val in gates:
                if sem is not None:
                    lst.append(lambda e, s=sem, v=val: kb.wait(ename, e, s, v))

        wv_view = expert_wv.rearrange("e (c p) n -> e p c n", p=128)
        pj_view = expert_proj.rearrange("e (f p) n -> e p f n", p=128)
        st_free = {0: (None, 0), 1: (None, 0), 2: (None, 0), 3: (None, 0)}
        half_cast = {0: (None, 0), 1: (None, 0)}
        half_used = {0: (None, 0), 1: (None, 0)}
        for e_i in range(NE):
            for half in range(2):
                tokoff = half * 1024
                for f in range(32):
                    ring_g, ring_v = (f % 2) * 2, (f % 2) * 2 + 1
                    wv0 = sW.take(32)
                    tw2("sync", sy_, st_free[ring_g], st_free[ring_v])
                    sy_.append(lambda e, e_i=e_i, f=f, rg=ring_g, rv=ring_v: (
                        e.dma_start(out=wvst[rg][:],
                                    in_=wv_view[e_i, :, :,
                                                f * 128:(f + 1) * 128])
                        .then_inc(sW.h, 16),
                        e.dma_start(out=wvst[rv][:],
                                    in_=wv_view[e_i, :, :, DFF + f * 128:
                                                DFF + (f + 1) * 128])
                        .then_inc(sW.h, 16)))
                    tw2("vector", v_, (sW, wv0 - 16), half_used[0])
                    vng = sV.take()
                    v_.append(lambda e, rg=ring_g: e.tensor_copy(
                        out=wsl(0), in_=wvst[rg][:]).then_inc(sV.h, 1))
                    tw2("scalar", s_, (sW, wv0), half_used[1])
                    sns = sS.take()
                    s_.append(lambda e, rv=ring_v: e.copy(
                        out=wsl(1), in_=wvst[rv][:]).then_inc(sS.h, 1))
                    half_cast[0] = (sV, vng)
                    half_cast[1] = (sS, sns)
                    for tb in range(2):
                        bg, bv2 = tb, 2 + tb
                        tw2("tensor", t_, half_cast[0], bfree2(bg))
                        tn0 = sT.take()
                        for c in range(8):
                            t_.append(lambda e, c=c, tb=tb, bg=bg,
                                      tokoff=tokoff, last=(c == 7): (
                                e.matmul(psA[:, bg, :], wsl(0)[:, c, :],
                                         h2T[:, c, tokoff + tb * 512:
                                             tokoff + (tb + 1) * 512],
                                         start=(c == 0), stop=last)
                                .then_inc(sT.h, 1) if last else
                                e.matmul(psA[:, bg, :], wsl(0)[:, c, :],
                                         h2T[:, c, tokoff + tb * 512:
                                             tokoff + (tb + 1) * 512],
                                         start=(c == 0), stop=False)))
                        tw2("tensor", t_, half_cast[1], bfree2(bv2))
                        tn1 = sT.take()
                        for c in range(8):
                            t_.append(lambda e, c=c, tb=tb, bv2=bv2,
                                      tokoff=tokoff, last=(c == 7): (
                                e.matmul(psA[:, bv2, :], wsl(1)[:, c, :],
                                         h2T[:, c, tokoff + tb * 512:
                                             tokoff + (tb + 1) * 512],
                                         start=(c == 0), stop=last)
                                .then_inc(sT.h, 1) if last else
                                e.matmul(psA[:, bv2, :], wsl(1)[:, c, :],
                                         h2T[:, c, tokoff + tb * 512:
                                             tokoff + (tb + 1) * 512],
                                         start=(c == 0), stop=False)))
                        tw2("scalar", s_, (sT, tn0))
                        sn2 = sS.take()
                        s_.append(lambda e, tb=tb, bg=bg: e.activation(
                            out=sgscr[:, tb, :], in_=psA[:, bg, :],
                            func=AF.Silu).then_inc(sS.h, 1))
                        bankF[bg] = (sS, sn2)
                        tw2("vector", v_, (sS, sn2), (sT, tn1))
                        vn2 = sV.take()
                        act_sl = (actA if f < 16 else actB)
                        v_.append(lambda e, f=f, tb=tb, bv2=bv2,
                                  act_sl=act_sl: e.tensor_tensor(
                            out=act_sl[:, f % 16, tb * 512:(tb + 1) * 512],
                            in0=sgscr[:, tb, :], in1=psA[:, bv2, :],
                            op=ALU.mult).then_inc(sV.h, 1))
                        bankF[bv2] = (sV, vn2)
                    st_free[ring_g] = (sV, half_cast[0][1])
                    st_free[ring_v] = (sS, half_cast[1][1])
                    half_used[0] = (sT, sT.n)
                    half_used[1] = (sT, sT.n)
                act_done_v = sV.n
                for dh in range(2):
                    pj_free = [(sS, sS.n), (sV, sV.n)]
                    evac_done = (sV, sV.n)
                    for f in range(32):
                        ring = f % 2
                        wv0 = sW.take(16)
                        tw2("sync", sy_, pj_free[ring])
                        sy_.append(lambda e, e_i=e_i, f=f, dh=dh, ring=ring:
                            e.dma_start(out=pjst[ring][:],
                                        in_=pj_view[e_i, :, f,
                                                    dh * 512:(dh + 1) * 512])
                            .then_inc(sW.h, 16))
                        dst = wslab[:].rearrange("p a b -> p (a b)")
                        if ring == 0:
                            tw2("scalar", s_, (sW, wv0), half_used[0])
                            snc = sS.take()
                            s_.append(lambda e, ring=ring, dst=dst:
                                e.copy(out=dst[:, 0:512],
                                       in_=pjst[ring][:])
                                .then_inc(sS.h, 1))
                            pj_free[0] = (sS, snc)
                            cast_gate = (sS, snc)
                        else:
                            tw2("vector", v_, (sW, wv0), half_used[1])
                            vnc = sV.take()
                            v_.append(lambda e, ring=ring, dst=dst:
                                e.tensor_copy(out=dst[:, 512:1024],
                                              in_=pjst[ring][:])
                                .then_inc(sV.h, 1))
                            pj_free[1] = (sV, vnc)
                            cast_gate = (sV, vnc)
                        tw2("tensor", t_, cast_gate, (sV, act_done_v))
                        if f == 0:
                            tw2("tensor", t_, evac_done)
                        tn0 = sT.take()
                        for ts in range(8):
                            act_sl = (actA if f < 16 else actB)
                            t_.append(lambda e, f=f, ts=ts, ring=ring,
                                      act_sl=act_sl, dst=dst,
                                      last=(ts == 7): (
                                e.matmul(psA[:, ts, :],
                                         act_sl[:, f % 16,
                                                ts * 128:(ts + 1) * 128],
                                         dst[:, ring * 512:(ring + 1) * 512],
                                         start=(f == 0), stop=(f == 31))
                                .then_inc(sT.h, 1) if last else
                                e.matmul(psA[:, ts, :],
                                         act_sl[:, f % 16,
                                                ts * 128:(ts + 1) * 128],
                                         dst[:, ring * 512:(ring + 1) * 512],
                                         start=(f == 0), stop=(f == 31))))
                            half_used[ring] = (sT, sT.n)
                        ymm_done = tn0
                    tw2("vector", v_, (sT, sT.n), (sV, mask_gate))
                    ev = 0
                    for ts in range(8):
                        tile = half * 8 + ts
                        last = (ts == 7)
                        ev = sV.take()
                        v_.append(lambda e, ts=ts, tile=tile, dh=dh,
                                  e_i=e_i: e.scalar_tensor_tensor(
                            out=acc[:, tile, dh * 512:(dh + 1) * 512],
                            in0=psA[:, ts, :],
                            scalar=masks[:, tile, e_i:e_i + 1],
                            in1=acc[:, tile, dh * 512:(dh + 1) * 512],
                            op0=ALU.mult, op1=ALU.add).then_inc(sV.h, 1))
                    evac_done = (sV, ev)
                    for b in range(8):
                        bankF[b] = evac_done
        kb.block("l06_moe", ops)
        dbg_dump("acc_l06", acc[:], (128, NT, D), F32)

    xn1tok = carve(R0, (128, NT, D), BF16)
    xn1T = carve(R1, (128, NC, T), BF16)
    qTf = carve(R2, (128, NC, T), BF16)
    ktok = carve(R3, (128, NT, D), BF16)
    vtok = carve(R0, (128, NT, D), BF16)
    knst = nc.alloc_sbuf_tensor("knst", [128, NT, 16], F32)

    def l1_feat_pass(name, wdram, dst_feat, dst_tok, scr_off, act=None,
                     norm=False):
        scrp = carve(scr_off, (128, 512), BF16)
        ops = {"gpsimd": [], "tensor": [], "scalar": [], "vector": []}
        g_, t_, s_, v_ = (ops["gpsimd"], ops["tensor"], ops["scalar"],
                          ops["vector"])
        bankF = {}
        half_ready = {0: (None, 0), 1: (None, 0)}
        half_used = {0: (None, 0), 1: (None, 0)}

        def tw2(ename, lst, *gates):
            for sem, val in gates:
                if sem is not None:
                    lst.append(lambda e, s=sem, v=val: kb.wait(ename, e, s, v))

        for dc in range(NC):
            half = dc % 2
            tw2("gpsimd", g_, half_used[half])
            gv = sG.take(16)
            g_.append(lambda e, dc=dc, half=half: e.dma_start(
                out=wsl(half),
                in_=wdram.rearrange("(c p) n -> p c n", p=128)
                [:, :, dc * 128:(dc + 1) * 128]).then_inc(sG.h, 16))
            half_ready[half] = (sG, gv)
            for tq in range(4):
                bsel = tq % 2
                tw2("tensor", t_, half_ready[half], bankF.get(bsel, (None, 0)))
                tn0 = sT.take()
                for c in range(8):
                    t_.append(lambda e, c=c, dc=dc, tq=tq, bsel=bsel,
                              half=half, last=(c == 7): (
                        e.matmul(psA[:, bsel, :], wsl(half)[:, c, :],
                                 xn1T[:, c, tq * 512:(tq + 1) * 512],
                                 start=(c == 0), stop=last).then_inc(sT.h, 1)
                        if last else
                        e.matmul(psA[:, bsel, :], wsl(half)[:, c, :],
                                 xn1T[:, c, tq * 512:(tq + 1) * 512],
                                 start=(c == 0), stop=False)))
                half_used[half] = (sT, sT.n)
                tw2("scalar", s_, (sT, tn0))
                ssn = sS.take()
                if dst_feat is not None:
                    s_.append(lambda e, dc=dc, tq=tq, bsel=bsel: e.activation(
                        out=dst_feat[:, dc, tq * 512:(tq + 1) * 512],
                        in_=psA[:, bsel, :], func=act).then_inc(sS.h, 1))
                    bankF[bsel] = (sS, ssn)
                else:
                    s_.append(lambda e, bsel=bsel: e.activation(
                        out=scrp[:], in_=psA[:, bsel, :],
                        func=act).then_inc(sS.h, 1))
                    bankF[bsel] = (sS, ssn)
                    tb = 4 + tq % 2
                    tw2("tensor", t_, (sS, ssn), bankF.get(tb, (None, 0)))
                    tn1 = sT.take()
                    for st4 in range(4):
                        pv = psA[:, tb, st4 * 64:(st4 + 1) * 64].bitcast(BF16)
                        t_.append(lambda e, pv=pv, st4=st4, last=(st4 == 3): (
                            e.transpose(pv, scrp[:, st4 * 128:(st4 + 1) * 128],
                                        ident_b[:]).then_inc(sT.h, 1)
                            if last else
                            e.transpose(pv, scrp[:, st4 * 128:(st4 + 1) * 128],
                                        ident_b[:])))
                    tw2("vector", v_, (sT, tn1))
                    vn = sV.take()
                    v_.append(lambda e, dc=dc, tq=tq, tb=tb: e.tensor_copy(
                        out=dst_tok[:, tq * 4:(tq + 1) * 4,
                                    dc * 128:(dc + 1) * 128],
                        in_=psA[:, tb, 0:256].bitcast(BF16)
                        .rearrange("p (a b) -> p a b", a=4)).then_inc(sV.h, 1))
                    bankF[tb] = (sV, vn)
                    tw2("scalar", s_, (sT, tn1))
        kb.block(name, ops)

    if stages >= 7:
        ln_pass("l1_ln", lambda t: acc[:, t, :], lambda t: xn1tok[:, t, :])
        dbg_dump("xn1", xn1tok[:], (128, NT, D), BF16)
        transpose_tok_to_feat("l1_tr", xn1tok, xn1T)

        mslab = carve(R2, (128, 8, 112), BF16)
        ops = {"gpsimd": [], "tensor": [], "scalar": [], "vector": []}
        g_, t_, s_, v_ = (ops["gpsimd"], ops["tensor"], ops["scalar"],
                          ops["vector"])
        gv = sG.take(16 * 4)
        g_.append(lambda e: (
            e.dma_start(out=mslab[:, :, 0:16],
                        in_=Wf.rearrange("(c p) n -> p c n", p=128))
            .then_inc(sG.h, 16),
            e.dma_start(out=mslab[:, :, 16:32],
                        in_=Wgamma.rearrange("(c p) n -> p c n", p=128))
            .then_inc(sG.h, 16),
            e.dma_start(out=mslab[:, :, 32:96],
                        in_=Wg1.rearrange("(c p) n -> p c n", p=128))
            .then_inc(sG.h, 16),
            e.dma_start(out=mslab[:, :, 96:112],
                        in_=Win.rearrange("(c p) n -> p c n", p=128)
                        [:, :, 0:16]).then_inc(sG.h, 16)))
        miX = carve(R2 + 2 * 1024, (128, 8, 64), BF16)
        gv2 = sG.take(16)
        g_.append(lambda e: e.dma_start(
            out=miX[:], in_=Win.rearrange("(c p) n -> p c n", p=128))
            .then_inc(sG.h, 16))
        binrow = carve(R2 + 4 * 1024, (1, 64), BF16)
        gv3 = sG.take(16)
        g_.append(lambda e: e.dma_start(
            out=binrow[0:1, :], in_=bin_[None, :]).then_inc(sG.h, 16))
        t_.append(lambda e, v=sG.n: kb.wait("tensor", e, sG, v))
        mbank = {}
        for tt in range(NT):
            b0 = tt % 2
            if b0 in mbank:
                t_.append(lambda e, v=mbank[b0]: kb.wait("tensor", e, sS, v))
            tn0 = sT.take()
            for c in range(8):
                t_.append(lambda e, c=c, tt=tt, b0=b0, last=(c == 7): (
                    e.matmul(psA[:, b0, 0:32],
                             xn1T[:, c, tt * 128:(tt + 1) * 128],
                             mslab[:, c, 0:32], start=(c == 0), stop=last)
                    .then_inc(sT.h, 1) if last else
                    e.matmul(psA[:, b0, 0:32],
                             xn1T[:, c, tt * 128:(tt + 1) * 128],
                             mslab[:, c, 0:32], start=(c == 0), stop=False)))
            b1 = 2 + tt % 2
            if b1 in mbank:
                t_.append(lambda e, v=mbank[b1]: kb.wait("tensor", e, sS, v))
            tn1 = sT.take()
            for c in range(8):
                t_.append(lambda e, c=c, tt=tt, b1=b1: e.matmul(
                    psA[:, b1, 0:64], xn1T[:, c, tt * 128:(tt + 1) * 128],
                    miX[:, c, :], start=(c == 0), stop=False))
            t_.append(lambda e, tt=tt, b1=b1: e.matmul(
                psA[:, b1, 0:64], onesb[0:1, :], binrow[0:1, :],
                start=False, stop=True).then_inc(sT.h, 1))
            s_.append(lambda e, v=tn0: kb.wait("scalar", e, sT, v))
            sn0 = sS.take(2)
            s_.append(lambda e, tt=tt, b0=b0: (
                e.activation(out=knst[:, tt, 0:16], in_=psA[:, b0, 0:16],
                             func=AF.Sigmoid),
                e.activation(out=gammatok[:, tt, :], in_=psA[:, b0, 16:32],
                             func=AF.Sigmoid).then_inc(sS.h, 2)))
            mbank[b0] = sn0
            s_.append(lambda e, v=tn1: kb.wait("scalar", e, sT, v))
            sn1 = sS.take()
            s_.append(lambda e, tt=tt, b1=b1: e.activation(
                out=sffn[:, tt, :], in_=psA[:, b1, 0:64],
                func=AF.Identity).then_inc(sS.h, 1))
            mbank[b1] = sn1
        for tt in range(NT):
            last = (tt == NT - 1)
            sS.take() if last else None
            s_.append(lambda e, tt=tt, last=last: (
                e.activation(out=logftok[:, tt, :], in_=knst[:, tt, 0:16],
                             func=AF.Ln).then_inc(sS.h, 1) if last else
                e.activation(out=logftok[:, tt, :], in_=knst[:, tt, 0:16],
                             func=AF.Ln)))
        for tq in range(4):
            b2 = 4 + tq % 2
            if b2 in mbank:
                t_.append(lambda e, v=mbank[b2]: kb.wait("tensor", e, sS, v))
            tn2 = sT.take()
            for c in range(8):
                t_.append(lambda e, c=c, tq=tq, b2=b2, last=(c == 7): (
                    e.matmul(psA[0:64, b2, :], mslab[:, c, 32:96],
                             xn1T[:, c, tq * 512:(tq + 1) * 512],
                             start=(c == 0), stop=last).then_inc(sT.h, 1)
                    if last else
                    e.matmul(psA[0:64, b2, :], mslab[:, c, 32:96],
                             xn1T[:, c, tq * 512:(tq + 1) * 512],
                             start=(c == 0), stop=False)))
            s_.append(lambda e, v=tn2: kb.wait("scalar", e, sT, v))
            sn2b = sS.take()
            s_.append(lambda e, tq=tq, b2=b2: e.activation(
                out=t1s[:, tq * 512:(tq + 1) * 512], in_=psA[0:64, b2, :],
                func=AF.Identity).then_inc(sS.h, 1))
            mbank[b2] = sn2b
        kb.block("l1_minis", ops)
        dbg_dump("logftok", logftok[:], (128, NT, 16), F32)
        dbg_dump("sffntok", sffn[:], (128, NT, 64), BF16)

        l1_feat_pass("l1_v", Wv1, None, vtok, R2, act=AF.Silu)
        dbg_dump("v1tok", vtok[:], (128, NT, D), BF16)
        l1_feat_pass("l1_k", Wk1, None, ktok, R2, act=AF.Silu)
        ops = {"scalar": [], "vector": []}
        s_, v_ = ops["scalar"], ops["vector"]
        pdump = psA[:, 6:8, :].rearrange("p a b -> p (a b)")
        vpre = sV.n
        s_.append(lambda e, v=vpre: kb.wait("scalar", e, sV, v))
        for tt in range(NT):
            sn = sS.take()
            s_.append(lambda e, tt=tt: (
                e.activation(out=pdump, in_=ktok[:, tt, :], func=AF.Square)
                .then_inc(sS.h, 1)))
            v_.append(lambda e, v=sn: kb.wait("vector", e, sS, v))
            vn = sV.take()
            v_.append(lambda e, tt=tt: e.reduce_sum(
                out=knst[:, tt, :],
                in_=pdump.rearrange("p (h w) -> p h w", h=16), axis=AX.X)
                .then_inc(sV.h, 1))
            s_.append(lambda e, v=vn: kb.wait("scalar", e, sV, v))
        for tt in range(NT):
            last = (tt == NT - 1)
            sS.take() if last else None
            s_.append(lambda e, tt=tt, last=last: (
                e.activation(out=knst[:, tt, :], in_=knst[:, tt, :],
                             func=AF.Sqrt).then_inc(sS.h, 1) if last else
                e.activation(out=knst[:, tt, :], in_=knst[:, tt, :],
                             func=AF.Sqrt)))
        v_.append(lambda e, v=sS.n: kb.wait("vector", e, sS, v))
        for tt in range(NT):
            v_.append(lambda e, tt=tt: e.tensor_single_scalar(
                out=knst[:, tt, :], in_=knst[:, tt, :], scalar=cst[:, 1:2],
                op=ALU.max))
        for tt in range(NT):
            v_.append(lambda e, tt=tt: e.reciprocal(
                out=knst[:, tt, :], in_=knst[:, tt, :]))
        for tt in range(NT):
            last = (tt == NT - 1)
            sV.take() if last else None
            v_.append(lambda e, tt=tt, last=last: (
                e.tensor_tensor(
                    out=ktok[:, tt, :].rearrange("p (h w) -> p h w", h=16),
                    in0=ktok[:, tt, :].rearrange("p (h w) -> p h w", h=16),
                    in1=knst[:, tt, :].unsqueeze(2).broadcast_to(
                        [128, 16, 64]), op=ALU.mult).then_inc(sV.h, 1)
                if last else
                e.tensor_tensor(
                    out=ktok[:, tt, :].rearrange("p (h w) -> p h w", h=16),
                    in0=ktok[:, tt, :].rearrange("p (h w) -> p h w", h=16),
                    in1=knst[:, tt, :].unsqueeze(2).broadcast_to(
                        [128, 16, 64]), op=ALU.mult)))
        kb.block("l1_knorm", ops)
        dbg_dump("k1tok", ktok[:], (128, NT, D), BF16)
        l1_feat_pass("l1_q", Wq1, qTf, None, R3 + 31 * 1024, act=AF.Silu)
        dbg_dump("qTf", qTf[:], (128, NC, T), BF16)
        dbg_dump("gamtok", gammatok[:], (128, NT, 16), BF16)

    if stages >= 8:
        selrb = carve(R1, (16, 16 * 128), F32)
        diffb = carve(R1 + 8 * 1024, (128, 4, 128), F32)
        dmaxb = carve(R1 + 10 * 1024, (128, 4, 128), F32)
        Db = carve(R1 + 12 * 1024, (128, 4, 128), BF16)
        Dtb = carve(R1 + 13 * 1024, (128, 4, 128), BF16)
        m1b = carve(R1 + 14 * 1024, (128, 4, 128), BF16)
        AqTb = carve(R1 + 15 * 1024, (128, 4, 128), BF16)
        Tb = carve(R1 + 16 * 1024, (128, 4, 128), BF16)
        Ttb = carve(R1 + 17 * 1024, (128, 4, 128), BF16)
        m2db = carve(R1 + 18 * 1024, (128, 4, 128), BF16)
        Ub = carve(R1 + 19 * 1024, (128, 4, 64), BF16)
        kpb = carve(R1 + 19 * 1024 + 512, (128, 4, 64), BF16)
        t5b = carve(R1 + 20 * 1024, (128, 4, 64), F32)
        t6b = carve(R1 + 20 * 1024, (128, 4, 64), F32)
        Uf = carve(R1 + 21 * 1024, (128, 4, 64), F32)
        Sf = carve(R1 + 22 * 1024, (64, 16, 64), F32)
        Sb_ = carve(R1 + 26 * 1024, (128, 8, 64), BF16)
        kfeat = carve(R1 + 27 * 1024, (128, 8, 128), BF16)
        lfrow = carve(R1 + 29 * 1024, (16, 128), F32)
        Lgrow = carve(R1 + 29 * 1024 + 512, (16, 128), F32)
        brow = carve(R1 + 30 * 1024, (16, 128), F32)
        carry = carve(R1 + 30 * 1024 + 512, (16, 1), F32)
        Lgtok_ = carve(R1 + 30 * 1024 + 576, (128, 16), F32)
        gae = carve(R1 + 30 * 1024 + 640, (128, 16), F32)
        bpos = carve(R1 + 30 * 1024 + 704, (128, 16), F32)
        kfac = carve(R1 + 30 * 1024 + 768, (128, 16), F32)
        eGa = carve(R1 + 30 * 1024 + 832, (128, 16), F32)
        Lam = carve(R1 + 30 * 1024 + 896, (128, 16), F32)
        cpos = carve(R1 + 30 * 1024 + 960, (128, 16), F32)
        cscr = carve(R1 + 31 * 1024, (128, 16), F32)
        lamd = carve(R1 + 31 * 1024 + 64, (128, 16), F32)
        egad = carve(R1 + 31 * 1024 + 128, (128, 16), F32)
        prevGb = carve(R1 + 31 * 1024 + 192, (128, 16), F32)

        ops = {"sync": [], "gpsimd": [], "tensor": [], "scalar": [], "vector": []}
        sy_, g_, t_, s_, v_ = (ops["sync"], ops["gpsimd"], ops["tensor"],
                               ops["scalar"], ops["vector"])
        bankF = {}

        def bf2(b):
            return bankF.get(b, (None, 0))

        def tw2(ename, lst, *gates):
            for sem, val in gates:
                if sem is not None:
                    lst.append(lambda e, s=sem, v=val: kb.wait(ename, e, s, v))

        wv0 = sW.take(16)
        sy_.append(lambda e: e.dma_start(out=selrb[:], in_=sel_c.ap())
                   .then_inc(sW.h, 16))
        vz = sV.take()
        v_.append(lambda e: (e.memset(carry[:], 0.0),
                             e.memset(Sf[:], 0.0),
                             e.memset(prevGb[:], 0.0),
                             e.memset(Sb_[:], 0.0).then_inc(sV.h, 1)))
        t_.append(lambda e, v=wv0: kb.wait("tensor", e, sW, v))
        t_.append(lambda e, v=vz: kb.wait("tensor", e, sV, v))

        smneg = smaskT

        for ch in range(scan_chunks):
            tw2("tensor", t_, bf2(6))
            tp1 = sT.take()
            t_.append(lambda e, ch=ch: e.transpose(
                psA[0:16, 6, 0:128], logftok[:, ch, :], ident_f[:])
                .then_inc(sT.h, 1))
            tw2("vector", v_, (sT, sT.n), (sT, tp1))
            vc1 = sV.take()
            v_.append(lambda e: e.tensor_copy(out=lfrow[:], in_=psA[0:16, 6, 0:128])
                      .then_inc(sV.h, 1))
            tw2("vector", v_, (sV, vc1))
            vc2 = sV.take()
            v_.append(lambda e: e.tensor_tensor_scan(
                out=Lgrow[:], data0=lfrow[:], data1=cst[0:16, 2:3]
                .broadcast_to([16, 128]), initial=carry[:, 0:1],
                op0=ALU.add, op1=ALU.add).then_inc(sV.h, 1))
            tw2("vector", v_, (sV, vc2))
            vc3 = sV.take()
            v_.append(lambda e: e.tensor_copy(out=carry[:, 0:1],
                                              in_=Lgrow[:, 127:128])
                      .then_inc(sV.h, 1))
            tw2("tensor", t_, (sV, vc2), bf2(7))
            tp2 = sT.take()
            t_.append(lambda e: e.transpose(
                psA[0:128, 7, 0:16], Lgrow[:], ident_f[0:16, 0:16])
                .then_inc(sT.h, 1))
            tw2("vector", v_, (sT, tp2))
            vc4 = sV.take()
            v_.append(lambda e: e.tensor_copy(out=Lgtok_[:],
                                              in_=psA[0:128, 7, 0:16])
                      .then_inc(sV.h, 1))
            tw2("scalar", s_, (sV, vc4))
            sp1 = sS.take()
            s_.append(lambda e, ch=ch: e.activation(
                out=gae[:], in_=logftok[:, ch, :], func=AF.Exp)
                .then_inc(sS.h, 1))
            tw2("vector", v_, (sS, sp1))
            vc5 = sV.take()
            v_.append(lambda e, ch=ch: e.tensor_tensor(
                out=bpos[:], in0=gammatok[:, ch, :], in1=gae[:],
                op=ALU.mult).then_inc(sV.h, 1))
            tw2("tensor", t_, (sV, vc5), bf2(6))
            tp3 = sT.take()
            t_.append(lambda e: e.transpose(
                psA[0:16, 6, 0:128], bpos[:], ident_f[:]).then_inc(sT.h, 1))
            tw2("vector", v_, (sT, tp3))
            vc6 = sV.take()
            v_.append(lambda e: e.tensor_copy(out=brow[:],
                                              in_=psA[0:16, 6, 0:128])
                      .then_inc(sV.h, 1))
            bankF[6] = (sV, vc6)
            tw2("tensor", t_, (sV, vc4), bf2(7))
            tp4 = sT.take()
            t_.append(lambda e: e.matmul(
                psA[0:128, 7, 0:16], e127s[:], Lgtok_[:],
                start=True, stop=True).then_inc(sT.h, 1))
            tw2("vector", v_, (sT, tp4))
            vc7 = sV.take(5)
            v_.append(lambda e, ch=ch: (
                e.tensor_tensor(out=cscr[:], in0=psA[0:128, 7, 0:16],
                                in1=Lgtok_[:], op=ALU.subtract),
                e.tensor_tensor(out=lamd[:], in0=Lgtok_[:],
                                in1=prevGb[:], op=ALU.subtract),
                e.tensor_tensor(out=egad[:], in0=psA[0:128, 7, 0:16],
                                in1=prevGb[:], op=ALU.subtract),
                e.tensor_copy(out=prevGb[:], in_=psA[0:128, 7, 0:16]),
                e.tensor_tensor(out=cpos[:], in0=logftok[:, ch, :],
                                in1=lamd[:], op=ALU.add).then_inc(sV.h, 5)))
            tw2("scalar", s_, (sV, vc7))
            sp2 = sS.take(4)
            s_.append(lambda e: (
                e.activation(out=kfac[:], in_=cscr[:], func=AF.Exp),
                e.activation(out=eGa[:], in_=egad[:], func=AF.Exp),
                e.activation(out=Lam[:], in_=lamd[:], func=AF.Exp),
                e.activation(out=cscr[:], in_=cpos[:], func=AF.Exp)
                .then_inc(sS.h, 4)))
            bankF[7] = (sS, sp2)
            tw2("vector", v_, (sS, sp2))
            vc8 = sV.take()
            v_.append(lambda e, ch=ch: e.tensor_tensor(
                out=cpos[:], in0=gammatok[:, ch, :], in1=cscr[:],
                op=ALU.mult).then_inc(sV.h, 1))
            tw2("tensor", t_, bf2(6), bf2(7))
            tp5 = sT.take()
            for dc in range(8):
                bb = 6 + dc // 4
                pv = psA[:, bb, (dc % 4) * 64:(dc % 4) * 64 + 64].bitcast(BF16)
                t_.append(lambda e, pv=pv, dc=dc, ch=ch, last=(dc == 7): (
                    e.transpose(pv, ktok[:, ch, dc * 128:(dc + 1) * 128],
                                ident_b[:]).then_inc(sT.h, 1) if last else
                    e.transpose(pv, ktok[:, ch, dc * 128:(dc + 1) * 128],
                                ident_b[:])))
            tw2("vector", v_, (sT, tp5))
            vc9 = sV.take()
            v_.append(lambda e: (
                e.tensor_copy(out=kfeat[:, 0:4, :],
                              in_=psA[:, 6, 0:256].bitcast(BF16)
                              .rearrange("p (a b) -> p a b", a=4)),
                e.tensor_copy(out=kfeat[:, 4:8, :],
                              in_=psA[:, 7, 0:256].bitcast(BF16)
                              .rearrange("p (a b) -> p a b", a=4))
                .then_inc(sV.h, 1)))
            bankF[6] = (sV, vc9)
            bankF[7] = (sV, vc9)
            pre_done_v = sV.n
            pre_done_s = sS.n

            for sb in range(scan_sbs):
                par, grp = sb // 2, sb % 2
                heads = [2 * (grp * 4 + j) + par for j in range(4)]

                def pcols(buf, par=par, grp=grp):
                    return buf.rearrange("p (a two) -> p a two", two=2)[
                        :, grp * 4:(grp + 1) * 4, par]

                def ptok(buf, ch, par=par, grp=grp):
                    return buf[:, ch, :].rearrange(
                        "p (a two c) -> p a two c", two=2, c=64)[
                        :, grp * 4:(grp + 1) * 4, par, :]
                hsl = lambda h: (slice((h % 2) * 64, (h % 2) * 64 + 64), h // 2)
                tw2("tensor", t_, bf2(0), bf2(1), (sV, pre_done_v))
                tck = sT.take(2)
                for hi, h in enumerate(heads):
                    sub, hc = (h % 2) * 64, h // 2
                    t_.append(lambda e, hi=hi, sub=sub, hc=hc, ch=ch,
                              last=(hi == 3): (
                        e.matmul(psA[:, 0, hi * 128:(hi + 1) * 128],
                                 kfeat[sub:sub + 64, hc, :],
                                 kfeat[sub:sub + 64, hc, :],
                                 start=True, stop=True).then_inc(sT.h, 1)
                        if last else
                        e.matmul(psA[:, 0, hi * 128:(hi + 1) * 128],
                                 kfeat[sub:sub + 64, hc, :],
                                 kfeat[sub:sub + 64, hc, :],
                                 start=True, stop=True)))
                for hi, h in enumerate(heads):
                    sub, hc = (h % 2) * 64, h // 2
                    t_.append(lambda e, hi=hi, sub=sub, hc=hc, ch=ch,
                              last=(hi == 3): (
                        e.matmul(psA[:, 1, hi * 128:(hi + 1) * 128],
                                 kfeat[sub:sub + 64, hc, :],
                                 qTf[sub:sub + 64, hc,
                                     ch * 128:(ch + 1) * 128],
                                 start=True, stop=True).then_inc(sT.h, 1)
                        if last else
                        e.matmul(psA[:, 1, hi * 128:(hi + 1) * 128],
                                 kfeat[sub:sub + 64, hc, :],
                                 qTf[sub:sub + 64, hc,
                                     ch * 128:(ch + 1) * 128],
                                 start=True, stop=True)))
                if SS <= 1:
                    continue
                tw2("tensor", t_, bf2(2), bf2(3), (sV, vc6))
                tbb = sT.take(2)
                for hi, h in enumerate(heads):
                    t_.append(lambda e, hi=hi, h=h, last=(hi == 3): (
                        e.matmul(psA[:, 2, hi * 128:(hi + 1) * 128],
                                 selrb[:, h * 128:(h + 1) * 128], Lgrow[:],
                                 start=True, stop=True).then_inc(sT.h, 1)
                        if last else
                        e.matmul(psA[:, 2, hi * 128:(hi + 1) * 128],
                                 selrb[:, h * 128:(h + 1) * 128], Lgrow[:],
                                 start=True, stop=True)))
                for hi, h in enumerate(heads):
                    t_.append(lambda e, hi=hi, h=h, last=(hi == 3): (
                        e.matmul(psA[:, 3, hi * 128:(hi + 1) * 128],
                                 selrb[:, h * 128:(h + 1) * 128], brow[:],
                                 start=True, stop=True).then_inc(sT.h, 1)
                        if last else
                        e.matmul(psA[:, 3, hi * 128:(hi + 1) * 128],
                                 selrb[:, h * 128:(h + 1) * 128], brow[:],
                                 start=True, stop=True)))
                if SS <= 2:
                    continue
                tw2("vector", v_, (sT, tbb))
                vd1 = sV.take(3)
                v_.append(lambda e, lgv=pcols(Lgtok_): (
                    e.tensor_tensor(
                        out=diffb[:], in0=psA[:, 2, :].rearrange(
                            "p (a b) -> p a b", a=4),
                        in1=lgv.unsqueeze(2)
                        .broadcast_to([128, 4, 128]), op=ALU.subtract),
                    e.tensor_tensor(
                        out=m1b[:], in0=psA[:, 0, :].rearrange(
                            "p (a b) -> p a b", a=4),
                        in1=smneg[:, 0:128].unsqueeze(1)
                        .broadcast_to([128, 4, 128]), op=ALU.mult),
                    e.tensor_tensor(
                        out=m2db[:], in0=psA[:, 0, :].rearrange(
                            "p (a b) -> p a b", a=4),
                        in1=smneg[:, 128:256].unsqueeze(1)
                        .broadcast_to([128, 4, 128]), op=ALU.mult)
                    .then_inc(sV.h, 3)))
                bankF[0] = (sV, vd1)
                vd2 = sV.take(3)
                v_.append(lambda e: (
                    e.tensor_tensor(
                        out=AqTb[:], in0=psA[:, 1, :].rearrange(
                            "p (a b) -> p a b", a=4),
                        in1=smneg[:, 256:384].unsqueeze(1)
                        .broadcast_to([128, 4, 128]), op=ALU.mult),
                    e.tensor_scalar(out=dmaxb[:], in0=diffb[:], scalar1=0.0,
                                    scalar2=None, op0=ALU.max),
                    e.tensor_scalar(out=diffb[:], in0=diffb[:], scalar1=0.0,
                                    scalar2=None, op0=ALU.min)
                    .then_inc(sV.h, 3)))
                bankF[1] = (sV, vd2)
                bankF[2] = (sV, vd2)
                tw2("scalar", s_, (sV, vd2))
                se1 = sS.take(2)
                s_.append(lambda e: (
                    e.activation(out=Db[:], in_=dmaxb[:], func=AF.Exp,
                                 scale=-1.0),
                    e.activation(out=Dtb[:], in_=diffb[:], func=AF.Exp)
                    .then_inc(sS.h, 2)))
                tw2("vector", v_, (sS, se1))
                vd3 = sV.take(2)
                v_.append(lambda e: (
                    e.tensor_tensor(out=m1b[:], in0=m1b[:], in1=Db[:],
                                    op=ALU.mult),
                    e.tensor_tensor(out=m2db[:], in0=m2db[:], in1=Dtb[:],
                                    op=ALU.mult).then_inc(sV.h, 2)))
                tw2("vector", v_, (sV, vd3))
                vd4 = sV.take(3)
                v_.append(lambda e, bpv=pcols(bpos): (
                    e.tensor_tensor(
                        out=Tb[:], in0=m1b[:],
                        in1=bpv.unsqueeze(2)
                        .broadcast_to([128, 4, 128]), op=ALU.mult),
                    e.tensor_tensor(out=AqTb[:], in0=AqTb[:], in1=Dtb[:],
                                    op=ALU.mult),
                    e.tensor_tensor(
                        out=Ttb[:], in0=m2db[:],
                        in1=psA[:, 3, :].rearrange("p (a b) -> p a b", a=4),
                        op=ALU.mult).then_inc(sV.h, 3)))
                bankF[3] = (sV, vd4)
                if SS <= 3:
                    continue
                tw2("tensor", t_, bf2(4), (sS, pre_done_s), (sV, pre_done_v))
                tks = sT.take()
                for hi, h in enumerate(heads):
                    sub, hc = (h % 2) * 64, h // 2
                    t_.append(lambda e, hi=hi, h=h, sub=sub, hc=hc,
                              last=(hi == 3): (
                        e.matmul(psA[:, 4, hi * 64:(hi + 1) * 64],
                                 kfeat[sub:sub + 64, hc, :],
                                 Sb_[sub:sub + 64, hc, :],
                                 start=True, stop=True).then_inc(sT.h, 1)
                        if last else
                        e.matmul(psA[:, 4, hi * 64:(hi + 1) * 64],
                                 kfeat[sub:sub + 64, hc, :],
                                 Sb_[sub:sub + 64, hc, :],
                                 start=True, stop=True)))
                tw2("vector", v_, (sT, tks), (sV, vd4))
                vb1 = sV.take(2)
                v_.append(lambda e, cpv=pcols(cpos), vtv=ptok(vtok, ch): (
                    e.tensor_tensor(
                        out=t5b[:], in0=psA[:, 4, 0:256].rearrange(
                            "p (a b) -> p a b", a=4),
                        in1=cpv.unsqueeze(2)
                        .broadcast_to([128, 4, 64]), op=ALU.mult),
                    e.tensor_tensor(
                        out=Uf[:], in0=vtv,
                        in1=t5b[:], op=ALU.subtract).then_inc(sV.h, 2)))
                bankF[4] = (sV, vb1)
                tw2("scalar", s_, (sV, vb1))
                sb0 = sS.take()
                s_.append(lambda e: e.copy(out=Ub[:], in_=Uf[:])
                          .then_inc(sS.h, 1))
                vb1 = (sV, vb1)
                if SS <= 4:
                    continue
                last_apply = (sS, sb0)
                for j in range(7):
                    tw2("tensor", t_, last_apply, bf2(5), bf2(6))
                    ta = sT.take()
                    for hi in range(4):
                        t_.append(lambda e, hi=hi, last=(hi == 3): (
                            e.matmul(psA[:, 5, hi * 64:(hi + 1) * 64],
                                     Ttb[:, hi, :], Ub[:, hi, :],
                                     start=True, stop=True).then_inc(sT.h, 1)
                            if last else
                            e.matmul(psA[:, 5, hi * 64:(hi + 1) * 64],
                                     Ttb[:, hi, :], Ub[:, hi, :],
                                     start=True, stop=True)))
                    tw2("vector", v_, (sT, ta))
                    vu = sV.take()
                    v_.append(lambda e: e.tensor_tensor(
                        out=Uf[:], in0=Uf[:],
                        in1=psA[:, 5, 0:256].rearrange("p (a b) -> p a b", a=4),
                        op=ALU.add).then_inc(sV.h, 1))
                    bankF[5] = (sV, vu)
                    tw2("scalar", s_, (sV, vu))
                    su = sS.take()
                    s_.append(lambda e: e.copy(out=Ub[:], in_=Uf[:])
                              .then_inc(sS.h, 1))
                    last_apply = (sS, su)
                    if j < 6:
                        tw2("tensor", t_, bf2(6), bf2(7))
                        tsq = sT.take(2)
                        for hi in range(4):
                            t_.append(lambda e, hi=hi, last=(hi == 3): (
                                e.matmul(psA[:, 6, hi * 128:(hi + 1) * 128],
                                         Ttb[:, hi, :], Tb[:, hi, :],
                                         start=True, stop=True)
                                .then_inc(sT.h, 1) if last else
                                e.matmul(psA[:, 6, hi * 128:(hi + 1) * 128],
                                         Ttb[:, hi, :], Tb[:, hi, :],
                                         start=True, stop=True)))
                        for hi in range(4):
                            t_.append(lambda e, hi=hi, last=(hi == 3): (
                                e.matmul(psA[:, 7, hi * 128:(hi + 1) * 128],
                                         Tb[:, hi, :], Ttb[:, hi, :],
                                         start=True, stop=True)
                                .then_inc(sT.h, 1) if last else
                                e.matmul(psA[:, 7, hi * 128:(hi + 1) * 128],
                                         Tb[:, hi, :], Ttb[:, hi, :],
                                         start=True, stop=True)))
                        tw2("vector", v_, (sT, tsq))
                        vsq = sV.take(2)
                        v_.append(lambda e: (
                            e.tensor_copy(out=Tb[:],
                                          in_=psA[:, 6, :].rearrange(
                                              "p (a b) -> p a b", a=4)),
                            e.tensor_copy(out=Ttb[:],
                                          in_=psA[:, 7, :].rearrange(
                                              "p (a b) -> p a b", a=4))
                            .then_inc(sV.h, 2)))
                        bankF[6] = (sV, vsq)
                        bankF[7] = (sV, vsq)
                if SS <= 5:
                    continue
                tw2("tensor", t_, bf2(4))
                tq0 = sT.take()
                for hi, h in enumerate(heads):
                    sub, hc = (h % 2) * 64, h // 2
                    t_.append(lambda e, hi=hi, h=h, sub=sub, hc=hc, ch=ch,
                              last=(hi == 3): (
                        e.matmul(psA[:, 4, hi * 64:(hi + 1) * 64],
                                 qTf[sub:sub + 64, hc,
                                     ch * 128:(ch + 1) * 128],
                                 Sb_[sub:sub + 64, hc, :],
                                 start=True, stop=True).then_inc(sT.h, 1)
                        if last else
                        e.matmul(psA[:, 4, hi * 64:(hi + 1) * 64],
                                 qTf[sub:sub + 64, hc,
                                     ch * 128:(ch + 1) * 128],
                                 Sb_[sub:sub + 64, hc, :],
                                 start=True, stop=True)))
                tw2("tensor", t_, bf2(5), last_apply)
                tq1 = sT.take()
                for hi in range(4):
                    t_.append(lambda e, hi=hi, last=(hi == 3): (
                        e.matmul(psA[:, 5, hi * 64:(hi + 1) * 64],
                                 AqTb[:, hi, :], Ub[:, hi, :],
                                 start=True, stop=True).then_inc(sT.h, 1)
                        if last else
                        e.matmul(psA[:, 5, hi * 64:(hi + 1) * 64],
                                 AqTb[:, hi, :], Ub[:, hi, :],
                                 start=True, stop=True)))
                tw2("vector", v_, (sT, tq1))
                vo1 = sV.take(2)
                v_.append(lambda e, lmv=pcols(Lam), vtv=ptok(vtok, ch): (
                    e.tensor_tensor(
                        out=t6b[:], in0=psA[:, 4, 0:256].rearrange(
                            "p (a b) -> p a b", a=4),
                        in1=lmv.unsqueeze(2)
                        .broadcast_to([128, 4, 64]), op=ALU.mult),
                    e.tensor_tensor(
                        out=vtv, in0=t6b[:],
                        in1=psA[:, 5, 0:256].rearrange("p (a b) -> p a b", a=4),
                        op=ALU.add).then_inc(sV.h, 2)))
                bankF[5] = (sV, vo1)
                if SS <= 6:
                    continue
                tw2("vector", v_, (sV, vo1))
                vk1 = sV.take()
                v_.append(lambda e, ktv=ptok(ktok, ch), kfv=pcols(kfac):
                          e.tensor_tensor(
                    out=kpb[:], in0=ktv,
                    in1=kfv.unsqueeze(2)
                    .broadcast_to([128, 4, 64]), op=ALU.mult)
                    .then_inc(sV.h, 1))
                tw2("tensor", t_, (sV, vk1), bf2(4))
                tsu = sT.take()
                for hi in range(4):
                    t_.append(lambda e, hi=hi, last=(hi == 3): (
                        e.matmul(psA[0:64, 4, hi * 64:(hi + 1) * 64],
                                 kpb[:, hi, :], Ub[:, hi, :],
                                 start=True, stop=True).then_inc(sT.h, 1)
                        if last else
                        e.matmul(psA[0:64, 4, hi * 64:(hi + 1) * 64],
                                 kpb[:, hi, :], Ub[:, hi, :],
                                 start=True, stop=True)))
                tw2("vector", v_, (sT, tsu))
                vsu = sV.take(4)
                for hi, h in enumerate(heads):
                    v_.append(lambda e, hi=hi, h=h, last=(hi == 3): (
                        e.scalar_tensor_tensor(
                            out=Sf[:, h, :], in0=Sf[:, h, :],
                            scalar=eGa[0:64, h:h + 1],
                            in1=psA[0:64, 4, hi * 64:(hi + 1) * 64],
                            op0=ALU.mult, op1=ALU.add).then_inc(sV.h, 4)
                        if last else
                        e.scalar_tensor_tensor(
                            out=Sf[:, h, :], in0=Sf[:, h, :],
                            scalar=eGa[0:64, h:h + 1],
                            in1=psA[0:64, 4, hi * 64:(hi + 1) * 64],
                            op0=ALU.mult, op1=ALU.add)))
                bankF[4] = (sV, vsu)
                tw2("vector", v_, (sV, vsu))
                ssb = sV.take(4)
                for hi, h in enumerate(heads):
                    sub, hc = (h % 2) * 64, h // 2
                    v_.append(lambda e, h=h, sub=sub, hc=hc,
                              last=(hi == 3), hi=hi: (
                        e.tensor_copy(out=Sb_[sub:sub + 64, hc, :],
                                      in_=Sf[0:64, h, :]).then_inc(sV.h, 4)
                        if last else
                        e.tensor_copy(out=Sb_[sub:sub + 64, hc, :],
                                      in_=Sf[0:64, h, :])))
                bankF[("sb", sb)] = (sV, ssb)
            pre_gate = tuple(bankF[("sb", s2_)] for s2_ in range(scan_sbs)
                             if ("sb", s2_) in bankF)
            tw2("tensor", t_, *pre_gate)
        kb.block("l1_scan", ops)
        dbg_dump("otok", vtok[:], (128, NT, D), BF16)
        for nm, bufv, shp, dt_ in (
                ("sc_Lgtok", Lgtok_, (128, 16), F32),
                ("sc_Lgrow", Lgrow, (16, 128), F32),
                ("sc_bpos", bpos, (128, 16), F32),
                ("sc_cpos", cpos, (128, 16), F32),
                ("sc_kfac", kfac, (128, 16), F32),
                ("sc_eGa", eGa, (128, 16), F32),
                ("sc_Lam", Lam, (128, 16), F32),
                ("sc_Db", Db, (128, 4, 128), BF16),
                ("sc_Dtb", Dtb, (128, 4, 128), BF16),
                ("sc_Tb", Tb, (128, 4, 128), BF16),
                ("sc_Ttb", Ttb, (128, 4, 128), BF16),
                ("sc_AqTb", AqTb, (128, 4, 128), BF16),
                ("sc_Ub", Ub, (128, 4, 64), BF16),
                ("sc_kfeat", kfeat, (128, 8, 128), BF16),
                ("sc_Sf", Sf, (64, 16, 64), F32),
                ("sc_brow", brow, (16, 128), F32)):
            dbg_dump(nm, bufv[:] if not isinstance(bufv, type(Lgtok_)) else bufv, shp, dt_)

    if stages >= 9:
        wg2b = carve(R2, (64, D), BF16)
        gscr = carve(R2 + 2 * 1024, (128, 2, D), BF16)
        normb = carve(R2 + 6 * 1024, (128, D), BF16)
        nwrow = carve(R2 + 8 * 1024, (1, D), F32)
        wob = carve(R2 + 12 * 1024, (128, 2, 8, 512), BF16)
        sT_f = carve(R3, (64, T), BF16)
        gTb = carve(R3 + 4 * 1024, (64, T), BF16)
        gTq4 = carve(R3 + 8 * 1024, (64, 4, 512), F32)
        hscr = carve(R3 + 16 * 1024, (64, 2, 4, 512), BF16)
        w1c = carve(R3 + 24 * 1024, (64, 16), F32)
        b1c = carve(R3 + 24 * 1024 + 64, (64, 16), F32)
        w2c = carve(R3 + 24 * 1024 + 128, (64, 16), F32)
        b2c = carve(R3 + 24 * 1024 + 192, (64, 1), F32)
        woutb = carve(R3 + 25 * 1024, (64, D), BF16)
        boutb = carve(R3 + 27 * 1024, (1, D), BF16)

        ops = {"gpsimd": [], "tensor": [], "scalar": [], "vector": []}
        g_, t_, s_, v_ = (ops["gpsimd"], ops["tensor"], ops["scalar"],
                          ops["vector"])
        gw = sG.take(16)
        g_.append(lambda e: e.dma_start(out=wg2b[:], in_=Wg2)
                  .then_inc(sG.h, 16))
        t_.append(lambda e, v=gw: kb.wait("tensor", e, sG, v))
        bgate = {}
        for tt in range(NT):
            b0 = 2 * (tt % 2)
            if b0 in bgate:
                t_.append(lambda e, v=bgate[b0]: kb.wait("tensor", e, sS, v))
            tn = sT.take(2)
            for half in range(2):
                t_.append(lambda e, tt=tt, b0=b0, half=half: e.matmul(
                    psA[:, b0 + half, :], t1s[:, tt * 128:(tt + 1) * 128],
                    wg2b[:, half * 512:(half + 1) * 512],
                    start=True, stop=True).then_inc(sT.h, 1))
            s_.append(lambda e, v=tn: kb.wait("scalar", e, sT, v))
            sn = sS.take(2)
            for half in range(2):
                s_.append(lambda e, tt=tt, b0=b0, half=half: e.activation(
                    out=gscr[:, tt % 2, half * 512:(half + 1) * 512],
                    in_=psA[:, b0 + half, :], func=AF.Sigmoid)
                    .then_inc(sS.h, 1))
            bgate[b0] = sn
            v_.append(lambda e, v=sn: kb.wait("vector", e, sS, v))
            vn = sV.take()
            v_.append(lambda e, tt=tt: e.tensor_tensor(
                out=vtok[:, tt, :], in0=vtok[:, tt, :],
                in1=gscr[:, tt % 2, :], op=ALU.mult).then_inc(sV.h, 1))
            s_.append(lambda e, v=vn: kb.wait("scalar", e, sV, v))
        kb.block("l1_gate", ops)
        dbg_dump("ogtok", vtok[:], (128, NT, D), BF16)

        ops = {"sync": [], "tensor": [], "vector": []}
        sy_, t_, v_ = ops["sync"], ops["tensor"], ops["vector"]
        nw = sW.take(16)
        sy_.append(lambda e: e.dma_start(out=nwrow[0:1, :], in_=norm_w[None, :])
                   .then_inc(sW.h, 16))
        t_.append(lambda e, v=nw: kb.wait("tensor", e, sW, v))
        tn = sT.take(2)
        for half in range(2):
            t_.append(lambda e, half=half: e.matmul(
                psA[:, half, :], onesf[0:1, 0:128],
                nwrow[0:1, half * 512:(half + 1) * 512],
                start=True, stop=True).then_inc(sT.h, 1))
        v_.append(lambda e, v=tn: kb.wait("vector", e, sT, v))
        sV.take()
        v_.append(lambda e: e.tensor_copy(
            out=normb[:], in_=psA[:, 0:2, :].rearrange("p a b -> p (a b)"))
            .then_inc(sV.h, 1))
        kb.block("l1_nw", ops)
        ln_pass("l1_on", lambda t: vtok[:, t, :], lambda t: vtok[:, t, :],
                gb=(normb[:], None))
        dbg_dump("ogntok", vtok[:], (128, NT, D), BF16)

        ognT = carve(R1, (128, NC, T), BF16)
        transpose_tok_to_feat("l1_wo_tr", vtok, ognT)
        ops = {"gpsimd": [], "tensor": [], "vector": []}
        g_, t_, v_ = ops["gpsimd"], ops["tensor"], ops["vector"]
        half_used = {0: (None, 0), 1: (None, 0)}
        bwo = {}
        for dq in range(2):
            hw = dq % 2
            if half_used[hw][0] is not None:
                g_.append(lambda e, v=half_used[hw][1]:
                          kb.wait("gpsimd", e, sT, v))
            gv = sG.take(16)
            g_.append(lambda e, dq=dq, hw=hw: e.dma_start(
                out=wob[:, hw, :, :],
                in_=Wo1.rearrange("(c p) n -> p c n", p=128)
                [:, :, dq * 512:(dq + 1) * 512]).then_inc(sG.h, 16))
            t_.append(lambda e, v=gv: kb.wait("tensor", e, sG, v))
            for tt in range(NT):
                b0 = 4 + tt % 4
                if b0 in bwo:
                    t_.append(lambda e, v=bwo[b0]: kb.wait("tensor", e, sV, v))
                tn = sT.take()
                for c in range(NC):
                    t_.append(lambda e, c=c, tt=tt, b0=b0, hw=hw,
                              last=(c == NC - 1): (
                        e.matmul(psA[:, b0, :],
                                 ognT[:, c, tt * 128:(tt + 1) * 128],
                                 wob[:, hw, c, :], start=(c == 0), stop=last)
                        .then_inc(sT.h, 1) if last else
                        e.matmul(psA[:, b0, :],
                                 ognT[:, c, tt * 128:(tt + 1) * 128],
                                 wob[:, hw, c, :], start=(c == 0), stop=False)))
                v_.append(lambda e, v=tn: kb.wait("vector", e, sT, v))
                vn = sV.take()
                v_.append(lambda e, tt=tt, dq=dq, b0=b0: e.tensor_tensor(
                    out=acc[:, tt, dq * 512:(dq + 1) * 512],
                    in0=acc[:, tt, dq * 512:(dq + 1) * 512],
                    in1=psA[:, b0, :], op=ALU.add).then_inc(sV.h, 1))
                bwo[b0] = vn
            half_used[hw] = (sT, sT.n)
        kb.block("l1_wo", ops)

        ops = {"sync": [], "gpsimd": [], "tensor": [], "scalar": [],
               "vector": []}
        sy_, g_, t_, s_, v_ = (ops["sync"], ops["gpsimd"], ops["tensor"],
                               ops["scalar"], ops["vector"])
        fw = sW.take(16 * 4)
        sy_.append(lambda e: (
            e.dma_start(out=w1c[:], in_=W1f).then_inc(sW.h, 16),
            e.dma_start(out=b1c[:], in_=b1f).then_inc(sW.h, 16),
            e.dma_start(out=w2c[:], in_=W2f).then_inc(sW.h, 16),
            e.dma_start(out=b2c[:], in_=b2f[:, None]).then_inc(sW.h, 16)))
        gw2 = sG.take(16 * 2)
        g_.append(lambda e: (
            e.dma_start(out=woutb[:], in_=Wout).then_inc(sG.h, 16),
            e.dma_start(out=boutb[0:1, :], in_=bout[None, :])
            .then_inc(sG.h, 16)))
        for tg in range(2):
            bank = tg
            tn = sT.take()
            for k8 in range(8):
                tt = tg * 8 + k8
                pv = psA[:, bank, :].bitcast(BF16)[0:64, k8 * 128:(k8 + 1) * 128]
                t_.append(lambda e, pv=pv, tt=tt, last=(k8 == 7): (
                    e.transpose(pv, sffn[:, tt, :], ident_b[:])
                    .then_inc(sT.h, 1) if last else
                    e.transpose(pv, sffn[:, tt, :], ident_b[:])))
            v_.append(lambda e, v=tn: kb.wait("vector", e, sT, v))
            vn = sV.take()
            v_.append(lambda e, tg=tg, bank=bank: e.tensor_copy(
                out=sT_f[:, tg * 1024:(tg + 1) * 1024],
                in_=psA[:, bank, :].bitcast(BF16)[0:64, :])
                .then_inc(sV.h, 1))
            bwo[bank] = vn
        vi = sV.take()
        v_.append(lambda e, v=fw: kb.wait("vector", e, sW, v))
        v_.append(lambda e: (
            e.tensor_copy(out=gTq4[:, 0, :],
                          in_=b2c[:, 0:1].broadcast_to([64, 512])),
            e.tensor_copy(out=gTq4[:, 1, :],
                          in_=b2c[:, 0:1].broadcast_to([64, 512])),
            e.tensor_copy(out=gTq4[:, 2, :],
                          in_=b2c[:, 0:1].broadcast_to([64, 512])),
            e.tensor_copy(out=gTq4[:, 3, :],
                          in_=b2c[:, 0:1].broadcast_to([64, 512]))
            .then_inc(sV.h, 1)))
        s_.append(lambda e, v=fw: kb.wait("scalar", e, sW, v))
        s_.append(lambda e, v=sV.n: kb.wait("scalar", e, sV, v))
        v_.append(lambda e, v=vi: kb.wait("vector", e, sV, v))
        kchain = {}
        for kk in range(16):
            if (kk - 2) in kchain:
                s_.append(lambda e, v=kchain[kk - 2]:
                          kb.wait("scalar", e, sV, v))
            sk = sS.take()
            for tq in range(4):
                s_.append(lambda e, kk=kk, tq=tq, last=(tq == 3): (
                    e.activation(out=hscr[:, kk % 2, tq, :],
                                 in_=sT_f[:, tq * 512:(tq + 1) * 512],
                                 func=AF.Relu, scale=w1c[:, kk:kk + 1],
                                 bias=b1c[:, kk:kk + 1]).then_inc(sS.h, 1)
                    if last else
                    e.activation(out=hscr[:, kk % 2, tq, :],
                                 in_=sT_f[:, tq * 512:(tq + 1) * 512],
                                 func=AF.Relu, scale=w1c[:, kk:kk + 1],
                                 bias=b1c[:, kk:kk + 1])))
            v_.append(lambda e, v=sk: kb.wait("vector", e, sS, v))
            vk = sV.take()
            for tq in range(4):
                v_.append(lambda e, kk=kk, tq=tq, last=(tq == 3): (
                    e.scalar_tensor_tensor(
                        out=gTq4[:, tq, :], in0=hscr[:, kk % 2, tq, :],
                        scalar=w2c[:, kk:kk + 1], in1=gTq4[:, tq, :],
                        op0=ALU.mult, op1=ALU.add).then_inc(sV.h, 1)
                    if last else
                    e.scalar_tensor_tensor(
                        out=gTq4[:, tq, :], in0=hscr[:, kk % 2, tq, :],
                        scalar=w2c[:, kk:kk + 1], in1=gTq4[:, tq, :],
                        op0=ALU.mult, op1=ALU.add)))
            kchain[kk] = vk
        v_.append(lambda e, v=kchain[15]: kb.wait("vector", e, sV, v))
        vg = sV.take()
        for tq in range(4):
            v_.append(lambda e, tq=tq, last=(tq == 3): (
                e.tensor_copy(out=gTb[:, tq * 512:(tq + 1) * 512],
                              in_=gTq4[:, tq, :]).then_inc(sV.h, 1)
                if last else
                e.tensor_copy(out=gTb[:, tq * 512:(tq + 1) * 512],
                              in_=gTq4[:, tq, :])))
        t_.append(lambda e, v=vg: kb.wait("tensor", e, sV, v))
        t_.append(lambda e, v=gw2: kb.wait("tensor", e, sG, v))
        for tt in range(NT):
            for half in range(2):
                b0 = (2 * tt + half) % 4
                if b0 in bwo:
                    t_.append(lambda e, v=bwo[b0]: kb.wait("tensor", e, sV, v))
                tn = sT.take()
                t_.append(lambda e, tt=tt, half=half, b0=b0: (
                    e.matmul(psA[:, b0, :], gTb[:, tt * 128:(tt + 1) * 128],
                             woutb[:, half * 512:(half + 1) * 512],
                             start=True, stop=False),
                    e.matmul(psA[:, b0, :], onesb[0:1, :],
                             boutb[0:1, half * 512:(half + 1) * 512],
                             start=False, stop=True).then_inc(sT.h, 1)))
                v_.append(lambda e, v=tn: kb.wait("vector", e, sT, v))
                vn = sV.take()
                v_.append(lambda e, tt=tt, half=half, b0=b0: e.tensor_tensor(
                    out=acc[:, tt, half * 512:(half + 1) * 512],
                    in0=acc[:, tt, half * 512:(half + 1) * 512],
                    in1=psA[:, b0, :], op=ALU.add).then_inc(sV.h, 1))
                bwo[b0] = vn
        kb.block("l1_ffn", ops)
        dbg_dump("acc_final", acc[:], (128, NT, D), F32)

    fv = sW.take(16)
    fin_ops = {"sync": [
        lambda e: e.dma_start(out=out.rearrange("(t p) d -> p t d", p=128),
                              in_=acc[:]).then_inc(sW.h, 16),
        lambda e, v=fv: kb.wait("sync", e, sW, v)]}
    kb.block("fin", fin_ops)

    nc._kb = kb
    return nc, kb


def kernel(**inputs):
    nc, kb = build_nc()
    xs = np.ascontiguousarray(inputs["L0_attn_x"])
    base = {
        "W_qkv": inputs["L0_attn_W_qkv"], "b_qkv": inputs["L0_attn_b_qkv"],
        "A_qkv": inputs["L0_attn_A_qkv"], "B_qkv": inputs["L0_attn_B_qkv"],
        "W_proj": inputs["L0_attn_W_proj"], "b_proj": inputs["L0_attn_b_proj"],
        "A_proj": inputs["L0_attn_A_proj"], "B_proj": inputs["L0_attn_B_proj"],
        "ln1_w": inputs["L0_ffn_ln1_w"], "ln1_b": inputs["L0_ffn_ln1_b"],
        "ln2_w": inputs["L0_ffn_ln2_w"], "ln2_b": inputs["L0_ffn_ln2_b"],
        "conv_w": inputs["L0_ffn_conv_w"], "attn_w": inputs["L0_ffn_attn_w"],
        "proj_w": inputs["L0_ffn_proj_w"], "router_w": inputs["L0_ffn_router_w"],
        "expert_wv": inputs["L0_ffn_expert_wv"],
        "expert_proj": inputs["L0_ffn_expert_proj"],
        "Wq1": inputs["L1_attn_Wq"], "Wk1": inputs["L1_attn_Wk"],
        "Wv1": inputs["L1_attn_Wv"], "Wgamma": inputs["L1_attn_Wgamma"],
        "Wf": inputs["L1_attn_Wf"], "Wg1": inputs["L1_attn_Wg1"],
        "Wg2": inputs["L1_attn_Wg2"], "Wo1": inputs["L1_attn_Wo"],
        "norm_w": inputs["L1_attn_norm_w"],
        "Win": inputs["L1_ffn_Win"], "bin_": inputs["L1_ffn_bin_"],
        "W1f": inputs["L1_ffn_W1"], "b1f": inputs["L1_ffn_b1"],
        "W2f": inputs["L1_ffn_W2"], "b2f": inputs["L1_ffn_b2"],
        "Wout": inputs["L1_ffn_Wout"], "bout": inputs["L1_ffn_bout"],
    }
    base = {k: np.ascontiguousarray(v) for k, v in base.items()}
    in_maps = [dict(base, x_in=np.ascontiguousarray(xs[i])) for i in range(8)]
    res = run_bass_kernel_spmd(nc, in_maps, core_ids=list(range(8)))
    return np.stack([r["out"] for r in res.results], axis=0)
